# version 11
# baseline (speedup 1.0000x reference)
import contextlib
import numpy as np
import concourse.bass as bass
import concourse.mybir as mybir
from concourse.bass_utils import run_bass_kernel_spmd
from concourse.alu_op_type import AluOpType as OP

F32 = mybir.dt.float32
BF16 = mybir.dt.bfloat16
I32 = mybir.dt.int32
U32 = mybir.dt.uint32
U8 = mybir.dt.uint8
AF = mybir.ActivationFunctionType
AX = mybir.AxisListType.X

T = 2048
D = 1024
NCORES = 8
TT = T // 128
KC = D // 128
NEG = -1e30


class Sem:
    def __init__(self, es, nc, name):
        self.h = es.enter_context(nc.semaphore(name))
        self.n = 0

    def inc(self, instr, k=1):
        instr.then_inc(self.h, k)
        self.n += k
        return instr

    def wait(self, eng, n=None):
        v = self.n if n is None else n
        if v > 0:
            eng.wait_ge(self.h, v)



def build_l0_attn():
    nc = bass.Bass("TRN2", target_bir_lowering=False)
    xin = nc.dram_tensor("xin", [T, D], F32, kind="ExternalInput").ap()
    wbig = nc.dram_tensor("wbig", [D, 3200], F32, kind="ExternalInput").ap()
    bbig = nc.dram_tensor("bbig", [128, 24], F32, kind="ExternalInput").ap()
    wout = nc.dram_tensor("wout", [D, D], F32, kind="ExternalInput").ap()
    bout = nc.dram_tensor("bout", [D], F32, kind="ExternalInput").ap()
    pscale = nc.dram_tensor("pscale", [16], F32, kind="ExternalInput").ap()
    sign48 = nc.dram_tensor("sign48", [48], F32, kind="ExternalInput").ap()
    identb = nc.dram_tensor("identb", [128, 128], BF16, kind="ExternalInput").ap()
    identf = nc.dram_tensor("identf", [128, 128], F32, kind="ExternalInput").ap()
    maskd = nc.dram_tensor("maskd", [128, 128], BF16, kind="ExternalInput").ap()
    xout = nc.dram_tensor("xout", [T, D], F32, kind="ExternalOutput").ap()
    es_top = contextlib.ExitStack()
    with es_top:
        blk = es_top.enter_context(nc.Block())
        emit_l0_attn(nc, blk, es_top,
                     dict(xin=xin, wbig=wbig, bbig=bbig, wout=wout, bout=bout,
                          pscale=pscale, sign48=sign48, identb=identb,
                          maskd=maskd, xout=xout))
    return nc


def emit_l0_attn(nc, blk, es_top, aps, pfx="", bar=None, final=True):
    xin = aps["xin"]; wbig = aps["wbig"]; bbig = aps["bbig"]
    wout = aps["wout"]; bout = aps["bout"]; pscale = aps["pscale"]
    sign48 = aps["sign48"]; identb = aps["identb"]; maskd = aps["maskd"]
    xout = aps["xout"]

    es = contextlib.ExitStack()
    with es:
        sb = lambda name, shape, dt: es.enter_context(
            nc.sbuf_tensor(pfx + name, shape, dt))

        es_xnT = contextlib.ExitStack()
        xnT = es_xnT.enter_context(nc.sbuf_tensor(pfx + "xnT", [128, KC, T], BF16, side="right"))
        wq_sb = sb("wq_sb", [128, KC, 1600], BF16)
        qline = sb("qline", [128, 8, T], BF16)
        kline = sb("kline", [128, 8, T], BF16)
        vT = sb("vT", [128, TT, 8, 64], BF16)
        attn_sb = sb("attn_sb", [128, KC, T], BF16)
        probs = sb("probs", [128, T], BF16)
        probsT = sb("probsT", [128, TT, 128], BF16)
        xbuf = sb("xbuf", [128, 2, D], F32)
        xn2 = sb("xn2", [128, 2, D], BF16)
        junk = sb("junk", [128, 2, D], F32)
        p_t = sb("p_t", [128, 2, 64], F32)
        lines = sb("lines", [128, 48], F32)
        lsq = sb("lsq", [128, 48], F32)
        lsc2 = sb("lsc2", [128, 12, 8], F32)
        lsc = sb("lsc", [128, 96], BF16)
        small = sb("small", [128, 72], F32)
        identb_sb = sb("identb_sb", [128, 128], BF16)
        maskd_sb = sb("maskd_sb", [128, 128], BF16)
        braw = sb("braw", [128, 24], F32)
        bq = sb("bq", [128, 24], F32)
        scol = sb("scol", [128, 1], F32)
        psc_bc = sb("psc_bc", [128, 16], F32)
        sgn_bc = sb("sgn_bc", [128, 48], F32)
        bout_bc = sb("bout_bc", [128, D], F32)
        eps_t = sb("eps_t", [128, 2], F32)
        v_tmp = sb("v_tmp", [128, 512], BF16)
        wstage = sb("wstage", [128, 2, 400], F32)

        tp_ps = es.enter_context(nc.psum_tensor(pfx + "tp_ps", [128, 2, 4, 256], BF16))

        ld = Sem(es_top, nc, pfx + "ld")
        ldx = [Sem(es_top, nc, pfx + "ldx0"), Sem(es_top, nc, pfx + "ldx1")]
        stx = [Sem(es_top, nc, pfx + "stx0"), Sem(es_top, nc, pfx + "stx1")]
        wl = [Sem(es_top, nc, pfx + "wl0"), Sem(es_top, nc, pfx + "wl1")]
        pe = Sem(es_top, nc, pfx + "pe")
        sv = Sem(es_top, nc, pfx + "sv")
        ss = Sem(es_top, nc, pfx + "ss")
        st = Sem(es_top, nc, pfx + "st")
        if bar:
            def fbar(eng):
                for _s, _c in bar:
                    _s.wait(eng, _c)
            blk.sync(fbar)
            blk.scalar(fbar)
            blk.vector(fbar)
            blk.tensor(fbar)

        def f(sync):
            for o, i in [(identb_sb[:], identb), (maskd_sb[:], maskd), (braw[:], bbig),
                         (psc_bc[:], pscale.unsqueeze(0).to_broadcast([128, 16])),
                         (sgn_bc[:], sign48.unsqueeze(0).to_broadcast([128, 48])),
                         (bout_bc[:], bout.unsqueeze(0).to_broadcast([128, D]))]:
                ld.inc(sync.dma_start(out=o, in_=i), 16)
        blk.sync(f)
        ld_consts = ld.n

        def f(v):
            v.memset(scol[0:64, :], 0.125)
            v.memset(scol[64:128, :], 1.0)
            v.memset(eps_t[:, 0:1], 1e-5)
            v.memset(eps_t[:, 1:2], 1e-18)
            ld.wait(v, ld_consts)
            sv.inc(v.tensor_scalar(out=bq[:], in0=braw[:], scalar1=scol[:, 0:1],
                                   scalar2=None, op0=OP.mult))
        blk.vector(f)
        bq_ready = sv.n

        wconv_marks = []

        def emit_wpiece(src_ap, dst_ap, war_sv=0, war_pe=0, extra_wait_ss=0):
            i = len(wconv_marks)
            p = i % 2
            s_dma = wl[p]
            war = wconv_marks[i - 2][1] if i >= 2 else 0

            def f(sync, p=p, war=war, src_ap=src_ap):
                ss.wait(sync, war)
                s_dma.inc(sync.dma_start(out=wstage[:, p, 0:src_ap.shape[-1]],
                                         in_=src_ap), 16)
            blk.sync(f)
            dma_mark = s_dma.n

            def f(s, p=p, dma_mark=dma_mark, dst_ap=dst_ap, war_sv=war_sv,
                  war_pe=war_pe, extra_wait_ss=extra_wait_ss, s_dma=s_dma,
                  src_ap=src_ap):
                s_dma.wait(s, dma_mark)
                if war_pe:
                    pe.wait(s, war_pe)
                ss.inc(s.activation(out=dst_ap,
                                    in_=wstage[:, p, 0:src_ap.shape[-1]],
                                    func=AF.Identity))
            blk.scalar(f)
            wconv_marks.append(("ss", ss.n))
            return ("ss", ss.n)

        wsrc_all = wbig.rearrange("(kc p) f -> p kc f", p=128)
        wq_ready = ("ss", 0)
        for kc in range(KC):
            for hh in range(4):
                wq_ready = emit_wpiece(wsrc_all[:, kc, hh * 400:(hh + 1) * 400],
                                       wq_sb[:, kc, hh * 400:(hh + 1) * 400])

        v_fin = [0] * TT
        pe_fin = [0] * TT
        s_fin = [0] * TT
        for k in range(TT):
            slot = k % 2

            def f(sync, k=k, slot=slot):
                if k >= 2:
                    sv.wait(sync, v_fin[k - 2])
                ldx[k % 2].inc(sync.dma_start(out=xbuf[:, slot, :],
                                              in_=xin[k * 128:(k + 1) * 128, :]), 16)
            blk.sync(f)
            ld_k = ldx[k % 2].n

            SB = 8 * (k % 2)

            def f(v, k=k, slot=slot, ld_k=ld_k, SB=SB):
                ldx[k % 2].wait(v, ld_k)
                sv.inc(v.tensor_reduce(out=small[:, SB:SB + 1],
                                       in_=xbuf[:, slot, :], axis=AX, op=OP.add))
            blk.vector(f)
            m1 = sv.n

            def f(s, m1=m1, SB=SB):
                sv.wait(s, m1)
                ss.inc(s.activation(out=small[:, SB + 4:SB + 5],
                                    in_=small[:, SB:SB + 1],
                                    func=AF.Identity, scale=1.0 / D))
            blk.scalar(f)
            m2 = ss.n

            def f(v, k=k, slot=slot, m2=m2, SB=SB):
                ss.wait(v, m2)
                sv.inc(v.tensor_scalar(out=junk[:, slot, :], in0=xbuf[:, slot, :],
                                       scalar1=small[:, SB + 4:SB + 5],
                                       scalar2=None, op0=OP.subtract))
            blk.vector(f)
            m3 = sv.n

            def f(s, slot=slot, m3=m3, SB=SB):
                sv.wait(s, m3)
                ss.inc(s.activation(out=xn2[:, slot, :], in_=junk[:, slot, :],
                                    func=AF.Square,
                                    accum_out=small[:, SB + 5:SB + 6]))
                ss.wait(s, ss.n)
                ss.inc(s.activation(out=small[:, SB + 7:SB + 8],
                                    in_=small[:, SB + 5:SB + 6],
                                    func=AF.Ln, scale=1.0 / D,
                                    bias=eps_t[:, 0:1]))
                ss.wait(s, ss.n)
                ss.inc(s.activation(out=small[:, SB + 6:SB + 7],
                                    in_=small[:, SB + 7:SB + 8],
                                    func=AF.Exp, scale=-0.5))
            blk.scalar(f)
            m4 = ss.n

            def f(v, k=k, slot=slot, m4=m4, SB=SB):
                ss.wait(v, m4)
                if k >= 2:
                    pe.wait(v, pe_fin[k - 2])
                sv.inc(v.tensor_scalar(out=xn2[:, slot, :], in0=junk[:, slot, :],
                                       scalar1=small[:, SB + 6:SB + 7],
                                       scalar2=None, op0=OP.mult))
            blk.vector(f)
            v_fin[k] = sv.n

            def f(t, k=k, slot=slot):
                sv.wait(t, v_fin[k])
                if k >= 1:
                    ss.wait(t, s_fin[k - 1])
                for c in range(KC):
                    pe.inc(t.transpose(tp_ps[:, c // 4, c % 4, 0:128],
                                       xn2[:, slot, c * 128:(c + 1) * 128],
                                       identb_sb[:]))
            blk.tensor(f)
            pe_fin[k] = pe.n

            def f(s, k=k):
                pe.wait(s, pe_fin[k])
                for c in range(KC):
                    ss.inc(s.activation(out=xnT[:, c, k * 128:(k + 1) * 128],
                                        in_=tp_ps[:, c // 4, c % 4, 0:128],
                                        func=AF.Identity))
            blk.scalar(f)
            s_fin[k] = ss.n
        xnT_done = ss.n

        prev_exp = 0
        prev_ptr = 0
        prev_av = 0
        prev_attn_ev = 0
        pl_evac = {}
        pl_vfin = {}
        wl_out = None
        PI = [0, 0, 0, 1, 1, 2]
        PJ = [1, 2, 3, 2, 3, 3]

        for half in range(2):
            if half == 1:
                pe_h0 = pe.n
                for kc in range(KC):
                    for hh in range(4):
                        c0 = 1600 + hh * 400
                        wq_ready = emit_wpiece(
                            wsrc_all[:, kc, c0:c0 + 400],
                            wq_sb[:, kc, hh * 400:(hh + 1) * 400],
                            war_pe=pe_h0)

            es2 = contextlib.ExitStack()
            mm_ps = es2.enter_context(nc.psum_tensor(pfx + "mm_ps%d" % half,
                                                     [128, 2, 512], F32))
            pl_ps = es2.enter_context(nc.psum_tensor(pfx + "pl_ps%d" % half,
                                                     [128, 512], F32))
            lt_ps = es2.enter_context(nc.psum_tensor(pfx + "lt_ps%d" % half,
                                                     [128, 2, 1024], BF16))
            phase2_entry_ss = ss.n
            qkv_evac = []
            for mg in range(12):
                for sp in range(4):
                    bank = len(qkv_evac) % 2

                    def f(t, half=half, mg=mg, sp=sp, bank=bank, wq_ready=wq_ready,
                          war=(qkv_evac[-2] if len(qkv_evac) >= 2 else phase2_entry_ss)):
                        ss.wait(t, max(war, xnT_done, phase2_entry_ss, wq_ready[1]))
                        instr = None
                        for c in range(KC):
                            instr = t.matmul(mm_ps[:, bank, :],
                                             wq_sb[:, c, mg * 128:(mg + 1) * 128],
                                             xnT[:, c, sp * 512:(sp + 1) * 512],
                                             start=(c == 0), stop=(c == KC - 1))
                        pe.inc(instr)
                    blk.tensor(f)
                    mm_done = pe.n

                    def f(s, half=half, mg=mg, sp=sp, bank=bank, mm_done=mm_done):
                        pe.wait(s, mm_done)
                        sv.wait(s, bq_ready)
                        g = half * 12 + mg
                        cols = slice(sp * 512, (sp + 1) * 512)
                        if mg < 8:
                            ss.inc(s.activation(out=qline[0:64, mg, cols],
                                                in_=mm_ps[0:64, bank, :],
                                                func=AF.Identity,
                                                scale=scol[0:64, 0:1],
                                                bias=bq[0:64, g:g + 1]))
                            ss.inc(s.activation(out=kline[0:64, mg, cols],
                                                in_=mm_ps[64:128, bank, :],
                                                func=AF.Identity,
                                                bias=braw[64:128, g:g + 1]))
                        else:
                            ss.inc(s.activation(out=v_tmp[:], in_=mm_ps[:, bank, :],
                                                func=AF.Identity, bias=braw[:, g:g + 1]))
                    blk.scalar(f)
                    qkv_evac.append(ss.n)

                    if mg >= 8:
                        vev = ss.n

                        def f(t, vev=vev):
                            ss.wait(t, vev)
                            for b2 in range(4):
                                pe.inc(t.transpose(tp_ps[:, 0, b2, 0:128],
                                                   v_tmp[:, b2 * 128:(b2 + 1) * 128],
                                                   identb_sb[:]))
                        blk.tensor(f)
                        tdone = pe.n

                        def f(s, mg=mg, sp=sp, tdone=tdone):
                            pe.wait(s, tdone)
                            for b2 in range(4):
                                jt = sp * 4 + b2
                                hp = (mg - 8) * 2
                                ss.inc(s.activation(out=vT[:, jt, hp:hp + 2, :],
                                                    in_=tp_ps[:, 0, b2, 0:128],
                                                    func=AF.Identity))
                        blk.scalar(f)
                        qkv_evac[-1] = ss.n

            for k in range(TT):
                slot = k % 2

                def f(t, half=half, k=k, wq_ready=wq_ready,
                      war_s=pl_evac.get((half, k - 1), phase2_entry_ss),
                      war_v=pl_vfin.get((half, k - 2), 0)):
                    sv.wait(t, war_v)
                    ss.wait(t, max(war_s, xnT_done, phase2_entry_ss, wq_ready[1]))
                    instr = None
                    for c in range(KC):
                        instr = t.matmul(pl_ps[:, 0:64],
                                         xnT[:, c, k * 128:(k + 1) * 128],
                                         wq_sb[:, c, 1536:1600],
                                         start=(c == 0), stop=(c == KC - 1))
                    pe.inc(instr)
                blk.tensor(f)
                plmm = pe.n

                def f(s, k=k, slot=slot, plmm=plmm):
                    pe.wait(s, plmm)
                    ss.inc(s.activation(out=p_t[:, slot, :], in_=pl_ps[:, 0:64],
                                        func=AF.Identity))
                blk.scalar(f)
                pl_evac[(half, k)] = ss.n
                pev = ss.n

                def f(v, k=k, slot=slot, pev=pev):
                    ss.wait(v, pev)
                    pt = p_t[:, slot, :]
                    lv = lines[:].rearrange("p (h l) -> p h l", l=6)
                    for l in range(6):
                        a, b = PI[l], PJ[l]
                        v.tensor_tensor(out=lsc2[:, 2 * l, :],
                                        in0=pt[:, a * 8:a * 8 + 8],
                                        in1=pt[:, 32 + b * 8:32 + b * 8 + 8],
                                        op=OP.mult)
                        sv.inc(v.tensor_tensor(out=lsc2[:, 2 * l + 1, :],
                                               in0=pt[:, b * 8:b * 8 + 8],
                                               in1=pt[:, 32 + a * 8:32 + a * 8 + 8],
                                               op=OP.mult))
                    sv.wait(v, sv.n)
                    for l in range(6):
                        sv.inc(v.tensor_tensor(out=lv[:, :, l],
                                               in0=lsc2[:, 2 * l, :],
                                               in1=lsc2[:, 2 * l + 1, :],
                                               op=OP.subtract))
                    sv.wait(v, sv.n)
                    sv.inc(v.tensor_tensor(out=lsq[:], in0=lines[:], in1=lines[:],
                                           op=OP.mult))
                    sv.wait(v, sv.n)
                    LB = 16 + 24 * (k % 2)
                    sv.inc(v.tensor_reduce(
                        out=small[:, LB:LB + 8],
                        in_=lsq[:].rearrange("p (h l) -> p h l", l=6),
                        axis=AX, op=OP.add))
                blk.vector(f)
                lm1 = sv.n
                LB = 16 + 24 * (k % 2)

                def f(s, lm1=lm1, LB=LB):
                    sv.wait(s, lm1)
                    ss.inc(s.activation(out=small[:, LB + 8:LB + 16],
                                        in_=small[:, LB:LB + 8],
                                        func=AF.Ln, bias=eps_t[:, 1:2]))
                    ss.wait(s, ss.n)
                    ss.inc(s.activation(out=small[:, LB + 16:LB + 24],
                                        in_=small[:, LB + 8:LB + 16],
                                        func=AF.Exp, scale=-0.5))
                blk.scalar(f)
                lm2 = ss.n

                def f(v, half=half, lm2=lm2, LB=LB):
                    ss.wait(v, lm2)
                    sv.inc(v.tensor_tensor(out=small[:, LB:LB + 8],
                                    in0=small[:, LB + 16:LB + 24],
                                    in1=psc_bc[:, half * 8:half * 8 + 8], op=OP.mult))
                    sv.wait(v, sv.n)
                    rn6 = small[:, LB + 16:LB + 24].unsqueeze(2).to_broadcast([128, 8, 6])
                    rnq6 = small[:, LB:LB + 8].unsqueeze(2).to_broadcast([128, 8, 6])
                    lv = lines[:].rearrange("p (h l) -> p h l", l=6)
                    lrev = lv[:, :, ::-1]
                    qv = lsc[:, 0:48].rearrange("p (h l) -> p h l", l=6)
                    kv = lsc[:, 48:96].rearrange("p (h l) -> p h l", l=6)
                    v.tensor_tensor(out=qv, in0=lv, in1=rnq6, op=OP.mult)
                    sv.inc(v.tensor_tensor(out=kv, in0=lrev, in1=rn6, op=OP.mult))
                    sv.wait(v, sv.n)
                    sv.inc(v.tensor_tensor(out=lsc[:, 48:96], in0=lsc[:, 48:96],
                                           in1=sgn_bc[:], op=OP.mult))
                blk.vector(f)
                pl_vfin[(half, k)] = sv.n
                lfin = sv.n

                lt_marks = []
                for j in range(16):
                    reg = j % 2

                    def f(t, j=j, reg=reg, lfin=lfin,
                          war=(lt_marks[j - 2] if j >= 2 else 0)):
                        sv.wait(t, lfin)
                        ss.wait(t, war)
                        src = lsc[:, j * 6:(j + 1) * 6]
                        pe.inc(t.transpose(lt_ps[0:6, reg, 0:128],
                                           src, identb_sb[:]))
                    blk.tensor(f)
                    tdone = pe.n

                    def f(s, k=k, j=j, reg=reg, tdone=tdone):
                        pe.wait(s, tdone)
                        dst = qline if j < 8 else kline
                        h8 = j % 8
                        ss.inc(s.activation(
                            out=dst[64:70, h8, k * 128:(k + 1) * 128],
                            in_=lt_ps[0:6, reg, 0:128],
                            func=AF.Identity))
                    blk.scalar(f)
                    lt_marks.append(ss.n)

            if half == 1:
                pe_qkv_all = pe.n
                es_xnT.close()
                wo_sb = es.enter_context(nc.sbuf_tensor(pfx + "wo_sb", [128, KC, D], BF16, side="right"))
                obuf = es.enter_context(nc.sbuf_tensor(pfx + "obuf", [128, 2, D], F32, side="right"))
                xrbuf = es.enter_context(nc.sbuf_tensor(pfx + "xrbuf", [128, 2, D], F32, side="right"))
                wosrc = wout.rearrange("(kc p) f -> p kc f", p=128)
                for kc in range(KC):
                    for hh in range(4):
                        wl_out = emit_wpiece(
                            wosrc[:, kc, hh * 256:(hh + 1) * 256],
                            wo_sb[:, kc, hh * 256:(hh + 1) * 256],
                            war_pe=pe_qkv_all)
            es2.close()

            es3 = contextlib.ExitStack()
            sc_ps = es3.enter_context(nc.psum_tensor(pfx + "sc_ps%d" % half,
                                                     [128, T], F32))
            av_ps = es3.enter_context(nc.psum_tensor(pfx + "av_ps%d" % half,
                                                     [128, 512], F32))
            phase3_entry_ss = ss.n
            pend_attn = [None]

            for mg in range(8):
                h = half * 8 + mg
                for ic in range(TT):
                    j_ext = (ic + 1) * 128
                    nstrip = (j_ext + 511) // 512

                    def f(t, mg=mg, ic=ic, j_ext=j_ext, nstrip=nstrip,
                          war=max(prev_exp, phase3_entry_ss)):
                        ss.wait(t, war)
                        instr = None
                        for sp2 in range(nstrip):
                            c0 = sp2 * 512
                            c1 = min(c0 + 512, j_ext)
                            instr = t.matmul(sc_ps[:, c0:c1],
                                             qline[0:70, mg, ic * 128:(ic + 1) * 128],
                                             kline[0:70, mg, c0:c1],
                                             start=True, stop=True)
                        pe.inc(instr)
                    blk.tensor(f)
                    sc_done = pe.n

                    def f(v, ic=ic, j_ext=j_ext, sc_done=sc_done):
                        pe.wait(v, sc_done)
                        sv.inc(v.tensor_tensor(out=sc_ps[:, ic * 128:j_ext],
                                        in0=sc_ps[:, ic * 128:j_ext],
                                        in1=maskd_sb[:], op=OP.add))
                        sv.wait(v, sv.n)
                        sv.inc(v.tensor_reduce(out=small[:, 64:65], in_=sc_ps[:, 0:j_ext],
                                        axis=AX, op=OP.max))
                        sv.wait(v, sv.n)
                        sv.inc(v.tensor_scalar(out=small[:, 65:66], in0=small[:, 64:65],
                                               scalar1=-1.0, scalar2=None,
                                               op0=OP.mult))
                    blk.vector(f)
                    mx_done = sv.n

                    def f(s, j_ext=j_ext, mx_done=mx_done, war=prev_ptr,
                          pend=pend_attn[0]):
                        if pend is not None:
                            ph, pic, pav = pend
                            pe.wait(s, pav)
                            r0 = (ph % 2) * 64
                            ss.inc(s.activation(
                                out=attn_sb[r0:r0 + 64, ph // 2,
                                            pic * 128:(pic + 1) * 128],
                                in_=av_ps[0:64, 0:128], func=AF.Identity))
                        sv.wait(s, mx_done)
                        pe.wait(s, war)
                        ss.inc(s.activation(out=probs[:, 0:j_ext],
                                            in_=sc_ps[:, 0:j_ext],
                                            func=AF.Exp, bias=small[:, 65:66],
                                            scale=1.0, accum_out=small[:, 68:69]))
                    blk.scalar(f)
                    if pend_attn[0] is not None:
                        prev_attn_ev = ss.n - 1
                    pend_attn[0] = None
                    prev_exp = ss.n
                    exp_done = ss.n

                    def f(v, j_ext=j_ext, exp_done=exp_done):
                        ss.wait(v, exp_done)
                        sv.inc(v.reciprocal(out=small[:, 67:68], in_=small[:, 68:69]))
                        sv.wait(v, sv.n)
                        sv.inc(v.tensor_scalar(out=probs[:, 0:j_ext],
                                               in0=probs[:, 0:j_ext],
                                               scalar1=small[:, 67:68], scalar2=None,
                                               op0=OP.mult))
                    blk.vector(f)
                    pr_done = sv.n

                    ptr_evacs = []
                    grp_marks = []
                    ngrp = (ic + 1 + 3) // 4
                    for g2 in range(ngrp):
                        jlo = g2 * 4
                        jhi = min(jlo + 4, ic + 1)
                        bk = g2 % 2

                        def f(t, jlo=jlo, jhi=jhi, bk=bk, pr_done=pr_done,
                              war_av=prev_av,
                              war_ev=(grp_marks[g2 - 2] if g2 >= 2 else 0)):
                            sv.wait(t, pr_done)
                            pe.wait(t, war_av)
                            sv.wait(t, war_ev)
                            for jt in range(jlo, jhi):
                                pe.inc(t.transpose(tp_ps[:, bk, jt - jlo, 0:128],
                                                   probs[:, jt * 128:(jt + 1) * 128],
                                                   identb_sb[:]))
                        blk.tensor(f)
                        tdone = pe.n

                        def f(v, jlo=jlo, jhi=jhi, bk=bk, tdone=tdone):
                            pe.wait(v, tdone)
                            for jt in range(jlo, jhi):
                                sv.inc(v.tensor_copy(probsT[:, jt, :],
                                                     tp_ps[:, bk, jt - jlo,
                                                           0:128]))
                        blk.vector(f)
                        grp_marks.append(sv.n)
                    ptr_evacs.append(sv.n)
                    prev_ptr = pe.n

                    def f(t, mg=mg, ic=ic, ev=ptr_evacs[-1],
                          war=prev_attn_ev):
                        sv.wait(t, ev)
                        ss.wait(t, war)
                        instr = None
                        for jt in range(ic + 1):
                            instr = t.matmul(av_ps[0:64, 0:128], vT[:, jt, mg, :],
                                             probsT[:, jt, :],
                                             start=(jt == 0), stop=(jt == ic))
                        pe.inc(instr)
                    blk.tensor(f)
                    prev_av = pe.n
                    pend_attn[0] = (h, ic, pe.n)
            def f(s, pend=pend_attn[0]):
                ph, pic, pav = pend
                pe.wait(s, pav)
                r0 = (ph % 2) * 64
                ss.inc(s.activation(out=attn_sb[r0:r0 + 64, ph // 2,
                                                pic * 128:(pic + 1) * 128],
                                    in_=av_ps[0:64, 0:128], func=AF.Identity))
            blk.scalar(f)
            pend_attn[0] = None
            prev_attn_ev = ss.n
            es3.close()

        attn_done = ss.n

        es4 = contextlib.ExitStack()
        mo_ps = es4.enter_context(nc.psum_tensor(pfx + "mo_ps", [128, 2, 512], F32))

        v_out = [0] * TT
        o_marks = []
        state_c = {"prev_opmm": 0, "prev_ysz": 0, "prev_bcst": 0}
        for k in range(TT):
            slot = k % 2

            def f(sync, k=k, slot=slot):
                if k >= 2:
                    sv.wait(sync, v_out[k - 2])
                ldx[k % 2].inc(sync.dma_start(out=xrbuf[:, slot, :],
                                              in_=xin[k * 128:(k + 1) * 128, :]), 16)
            blk.sync(f)
            ld_k = ldx[k % 2].n

            for sp in range(2):
                bank = (k * 2 + sp) % 2

                def f(t, k=k, sp=sp, bank=bank,
                      war=(o_marks[-2] if len(o_marks) >= 2 else 0)):
                    ss.wait(t, max(attn_done, wl_out[1]))
                    sv.wait(t, war)
                    instr = None
                    for c in range(KC):
                        instr = t.matmul(mo_ps[:, bank, :],
                                         attn_sb[:, c, k * 128:(k + 1) * 128],
                                         wo_sb[:, c, sp * 512:(sp + 1) * 512],
                                         start=(c == 0), stop=(c == KC - 1))
                    pe.inc(instr)
                blk.tensor(f)
                mmd = pe.n

                def f(v, k=k, sp=sp, slot=slot, bank=bank, mmd=mmd, ld_k=ld_k):
                    pe.wait(v, mmd)
                    ldx[k % 2].wait(v, ld_k)
                    if k >= 2:
                        stx[k % 2].wait(v, 16 * ((k - 2) // 2 + 1))
                    cols = slice(sp * 512, (sp + 1) * 512)
                    sv.inc(v.scalar_tensor_tensor(out=obuf[:, slot, cols],
                                           in0=mo_ps[:, bank, :], scalar=1.0,
                                           in1=bout_bc[:, cols], op0=OP.mult,
                                           op1=OP.add))
                    sv.wait(v, sv.n)
                    sv.inc(v.tensor_tensor(out=obuf[:, slot, cols],
                                           in0=obuf[:, slot, cols],
                                           in1=xrbuf[:, slot, cols], op=OP.add))
                blk.vector(f)
                o_marks.append(sv.n)
            v_out[k] = sv.n

            def f(sync, k=k, slot=slot):
                sv.wait(sync, v_out[k])
                stx[k % 2].inc(sync.dma_start(out=xout[k * 128:(k + 1) * 128, :],
                                              in_=obuf[:, slot, :]), 16)
            blk.sync(f)
        es4.close()

        if final:
            def f(g):
                st.wait(g, st.n)
                stx[0].wait(g, stx[0].n)
                stx[1].wait(g, stx[1].n)
            blk.gpsimd(f)
    return [(s, s.n) for s in (ld, ldx[0], ldx[1], stx[0], stx[1], wl[0],
                               wl[1], pe, sv, ss, st)]


def prep_l0_attn_inputs(Wqkv, bqkv, W1p, W2p, pscale, Wout, bout_):
    import ml_dtypes
    cols = []
    bias_groups = []
    for half in range(2):
        hs = range(half * 8, half * 8 + 8)
        for h in hs:
            cols.extend(range(h * 64, (h + 1) * 64))
            cols.extend(range(D + h * 64, D + (h + 1) * 64))
        for h in hs:
            cols.extend(range(2 * D + h * 64, 2 * D + (h + 1) * 64))
        pcols_1 = [h * 4 + c for c in range(4) for h in hs]
        pcols_2 = [h * 4 + c for c in range(4) for h in hs]
        cols.append(("p", pcols_1, pcols_2))
    wparts = []
    bias_list = []
    for half in range(2):
        hs = list(range(half * 8, half * 8 + 8))
        qk = np.concatenate([np.concatenate(
            [Wqkv[:, h * 64:(h + 1) * 64], Wqkv[:, D + h * 64:D + (h + 1) * 64]],
            axis=1) for h in hs], axis=1)
        v = np.concatenate([Wqkv[:, 2 * D + h * 64:2 * D + (h + 1) * 64]
                            for h in hs], axis=1)
        p1 = W1p[:, [h * 4 + c for c in range(4) for h in hs]]
        p2 = W2p[:, [h * 4 + c for c in range(4) for h in hs]]
        wparts.append(np.concatenate([qk, v, p1, p2], axis=1))
        for h in hs:
            bias_list.append(np.concatenate(
                [bqkv[h * 64:(h + 1) * 64], bqkv[D + h * 64:D + (h + 1) * 64]]))
        for h2 in range(0, 8, 2):
            hA, hB = hs[h2], hs[h2 + 1]
            bias_list.append(np.concatenate(
                [bqkv[2 * D + hA * 64:2 * D + (hA + 1) * 64],
                 bqkv[2 * D + hB * 64:2 * D + (hB + 1) * 64]]))
    wbig = np.ascontiguousarray(np.concatenate(wparts, axis=1), dtype=np.float32)
    bbig = np.ascontiguousarray(np.stack(bias_list, axis=1), dtype=np.float32)
    assert wbig.shape == (D, 3200) and bbig.shape == (128, 24)
    sgn = np.array([1, -1, 1, 1, -1, 1], np.float32)
    sign48 = np.tile(sgn, 8).astype(np.float32)
    identb = np.eye(128, dtype=ml_dtypes.bfloat16)
    identf = np.eye(128, dtype=np.float32)
    ii, jj = np.meshgrid(np.arange(128), np.arange(128), indexing="ij")
    maskd = np.where(jj > ii, np.float32(NEG), np.float32(0.0)).astype(ml_dtypes.bfloat16)
    return dict(wbig=wbig, bbig=bbig, wout=np.ascontiguousarray(Wout, np.float32),
                bout=np.ascontiguousarray(bout_, np.float32),
                pscale=np.ascontiguousarray(pscale, np.float32),
                sign48=sign48, identb=identb, identf=identf,
                maskd=np.ascontiguousarray(maskd))


def run_l0_attn(x_full, Wqkv, bqkv, W1p, W2p, pscale, Wout, bout_):
    nc = build_l0_attn()
    shared = prep_l0_attn_inputs(Wqkv, bqkv, W1p, W2p, pscale, Wout, bout_)
    in_maps = []
    for b in range(NCORES):
        m = dict(shared)
        m["xin"] = np.ascontiguousarray(x_full[b], np.float32)
        in_maps.append(m)
    res = run_bass_kernel_spmd(nc, in_maps, core_ids=list(range(NCORES)))
    return np.stack([r["xout"] for r in res.results], axis=0)





def build_l0_ffn():
    NI = 2816
    nc = bass.Bass("TRN2", target_bir_lowering=False)
    xin = nc.dram_tensor("xin", [T, D], F32, kind="ExternalInput").ap()
    wg = nc.dram_tensor("wg", [NI, D], F32, kind="ExternalInput").ap()
    wu = nc.dram_tensor("wu", [NI, D], F32, kind="ExternalInput").ap()
    wd = nc.dram_tensor("wd", [D, NI], F32, kind="ExternalInput").ap()
    identb = nc.dram_tensor("identb", [128, 128], BF16, kind="ExternalInput").ap()
    xout = nc.dram_tensor("xout", [T, D], F32, kind="ExternalOutput").ap()
    es_top = contextlib.ExitStack()
    with es_top:
        blk = es_top.enter_context(nc.Block())
        emit_l0_ffn(nc, blk, es_top,
                    dict(xin=xin, wg=wg, wu=wu, wd=wd, identb=identb,
                         xout=xout))
    return nc


def emit_l0_ffn(nc, blk, es_top, aps, pfx="", bar=None, final=True):
    NI = 2816
    NK = NI // 128
    xin = aps["xin"]; wg = aps["wg"]; wu = aps["wu"]; wd = aps["wd"]
    identb = aps["identb"]; xout = aps["xout"]
    wdqnat = nc.dram_tensor(pfx + "wdqnat", [D, NI], BF16, kind="Internal").ap()

    es = contextlib.ExitStack()
    with es:
        sb = lambda name, shape, dt: es.enter_context(
            nc.sbuf_tensor(pfx + name, shape, dt))

        wgq = sb("wgq", [128, KC, NI], BF16)
        wuq = sb("wuq", [128, KC, NI], BF16)
        xqT = sb("xqT", [128, KC, T], BF16)
        xbuf = sb("xbuf", [128, 2, D], F32)
        junk = sb("junk", [128, 2, D], F32)
        xn2 = sb("xn2", [128, 2, D], F32)
        small = sb("small", [128, 16], F32)
        eps_t = sb("eps_t", [128, 1], F32)
        identb_sb = sb("identb_sb", [128, 128], BF16)
        icon = sb("icon", [128, 2], U32)
        qgm = sb("qgm", [128, 4, 64], F32)
        qabs = sb("qabs", [128, D], F32)
        qsgn = sb("qsgn", [128, D], BF16)
        qa = sb("qa", [128, D], F32)
        qc = sb("qc", [128, D], U8)
        qout = sb("qout", [128, D], BF16)
        sg = sb("sg", [128, 512], F32)
        hbuf = sb("hbuf", [128, 512], F32)
        hqT_sb = sb("hqT_sb", [128, 4, 128], BF16)
        wbuf = xbuf
        obuf = xn2
        xrbuf = junk
        wdb = sb("wdb", [128, 2, D], BF16)

        tp_ps = es.enter_context(nc.psum_tensor(pfx + "tp_ps", [128, 2, 4, 256], BF16))

        ldx = [Sem(es_top, nc, pfx + "ldx0"), Sem(es_top, nc, pfx + "ldx1")]
        stx = [Sem(es_top, nc, pfx + "stx0"), Sem(es_top, nc, pfx + "stx1")]
        wlp = [Sem(es_top, nc, pfx + "wl0"), Sem(es_top, nc, pfx + "wl1")]
        sdq = Sem(es_top, nc, pfx + "sdq")
        pe = Sem(es_top, nc, pfx + "pe")
        sv = Sem(es_top, nc, pfx + "sv")
        ss = Sem(es_top, nc, pfx + "ss")
        if bar:
            def fbar(eng):
                for _s, _c in bar:
                    _s.wait(eng, _c)
            blk.sync(fbar)
            blk.scalar(fbar)
            blk.vector(fbar)
            blk.tensor(fbar)

        def f(sync):
            ldx[0].inc(sync.dma_start(out=identb_sb[:], in_=identb), 16)
        blk.sync(f)

        def f(v):
            v.memset(eps_t[:], 1e-5)
            v.memset(icon[:, 0:1], 8)
            v.memset(icon[:, 1:2], 0xFFC000)
        blk.vector(f)

        state = {"lastq": 0, "wdn": 0, "wdn_mm": {}, "qn2": 0}

        def emit_quant(in_ap, F, waits_s=(), waits_v=(), out_off=0):
            nb = F // 16
            lastq = state["lastq"]

            def f(s, in_ap=in_ap, F=F, lastq=lastq, waits_s=waits_s):
                sv.wait(s, lastq)
                for sem, val in waits_s:
                    sem.wait(s, val)
                ss.inc(s.activation(out=qabs[:, 0:F], in_=in_ap, func=AF.Abs))
                ss.inc(s.activation(out=qsgn[:, 0:F], in_=in_ap, func=AF.Sign))
            blk.scalar(f)
            sm = ss.n

            def f(v, F=F, nb=nb, sm=sm, waits_v=waits_v, out_off=out_off):
                ss.wait(v, sm)
                for sem, val in waits_v:
                    sem.wait(v, val)
                gm = qgm[:, 0, 0:nb]
                gm2 = qgm[:, 1, 0:nb]
                sc6 = qgm[:, 2, 0:nb]
                rs = qgm[:, 3, 0:nb]
                sv.inc(v.tensor_reduce(out=gm, in_=qabs[:, 0:F].rearrange(
                    "p (b i) -> p b i", i=16), axis=AX, op=OP.max))
                sv.wait(v, sv.n)
                sv.inc(v.tensor_scalar(out=gm2, in0=gm, scalar1=6e-12,
                                       scalar2=None, op0=OP.max))
                sv.wait(v, sv.n)
                v.tensor_scalar(out=sc6, in0=gm2, scalar1=1.0 / 6.0,
                                scalar2=None, op0=OP.mult)
                sv.inc(v.reciprocal(out=rs, in_=gm2))
                sv.wait(v, sv.n)
                sv.inc(v.tensor_scalar(out=rs, in0=rs, scalar1=6.0,
                                       scalar2=None, op0=OP.mult))
                sv.wait(v, sv.n)
                rsb = rs.unsqueeze(2).to_broadcast([128, nb, 16])
                sv.inc(v.tensor_tensor(
                    out=qa[:, 0:F].rearrange("p (b i) -> p b i", i=16),
                    in0=qabs[:, 0:F].rearrange("p (b i) -> p b i", i=16),
                    in1=rsb, op=OP.mult))
                sv.wait(v, sv.n)
                sv.inc(v.tensor_scalar(out=qc[:, 0:F], in0=qa[:, 0:F],
                                       scalar1=1.0, scalar2=None, op0=OP.is_lt))
                sv.wait(v, sv.n)
                sv.inc(v.tensor_tensor(out=qa[:, 0:F], in0=qa[:, 0:F],
                                       in1=qc[:, 0:F], op=OP.add))
                sv.wait(v, sv.n)
                ui = qa[:, 0:F].bitcast(U32)
                sv.inc(v.tensor_scalar(out=ui, in0=ui, scalar1=icon[:, 0:1],
                                       scalar2=None, op0=OP.logical_shift_right))
                sv.wait(v, sv.n)
                sv.inc(v.tensor_scalar(out=ui, in0=ui, scalar1=8192.0,
                                       scalar2=None, op0=OP.add))
                sv.wait(v, sv.n)
                sv.inc(v.tensor_scalar(out=ui, in0=ui, scalar1=icon[:, 1:2],
                                       scalar2=icon[:, 0:1], op0=OP.bitwise_and,
                                       op1=OP.logical_shift_left))
                sv.wait(v, sv.n)
                sv.inc(v.tensor_tensor(out=qa[:, 0:F], in0=qa[:, 0:F],
                                       in1=qc[:, 0:F], op=OP.subtract))
                sv.wait(v, sv.n)
                scb = sc6.unsqueeze(2).to_broadcast([128, nb, 16])
                sv.inc(v.tensor_tensor(
                    out=qa[:, 0:F].rearrange("p (b i) -> p b i", i=16),
                    in0=qa[:, 0:F].rearrange("p (b i) -> p b i", i=16),
                    in1=scb, op=OP.mult))
                sv.wait(v, sv.n)
                sdq.wait(v, state.get("qout_store", 0))
                sv.inc(v.tensor_tensor(out=qout[:, out_off:out_off + F],
                                       in0=qa[:, 0:F],
                                       in1=qsgn[:, 0:F], op=OP.mult))
            blk.vector(f)
            state["lastq"] = sv.n
            return sv.n

        v_fin = [0] * TT
        pe_fin = [0] * TT
        s_fin = [0] * TT
        for k in range(TT):
            slot = k % 2
            SB = 8 * (k % 2)

            def f(sync, k=k, slot=slot):
                if k >= 2:
                    sv.wait(sync, v_fin[k - 2])
                ldx[k % 2].inc(sync.dma_start(out=xbuf[:, slot, :],
                                              in_=xin[k * 128:(k + 1) * 128, :]), 16)
            blk.sync(f)
            ld_k = ldx[k % 2].n

            def f(v, k=k, slot=slot, ld_k=ld_k, SB=SB):
                ldx[k % 2].wait(v, ld_k)
                sv.inc(v.tensor_reduce(out=small[:, SB:SB + 1],
                                       in_=xbuf[:, slot, :], axis=AX, op=OP.add))
            blk.vector(f)
            m1 = sv.n

            def f(s, m1=m1, SB=SB):
                sv.wait(s, m1)
                ss.inc(s.activation(out=small[:, SB + 4:SB + 5],
                                    in_=small[:, SB:SB + 1],
                                    func=AF.Identity, scale=1.0 / D))
            blk.scalar(f)
            m2 = ss.n

            def f(v, k=k, slot=slot, m2=m2, SB=SB):
                ss.wait(v, m2)
                sv.inc(v.tensor_scalar(out=junk[:, slot, :], in0=xbuf[:, slot, :],
                                       scalar1=small[:, SB + 4:SB + 5],
                                       scalar2=None, op0=OP.subtract))
            blk.vector(f)
            m3 = sv.n

            def f(s, slot=slot, m3=m3, SB=SB):
                sv.wait(s, m3)
                ss.inc(s.activation(out=xn2[:, slot, :], in_=junk[:, slot, :],
                                    func=AF.Square, accum_out=small[:, SB + 5:SB + 6]))
                ss.wait(s, ss.n)
                ss.inc(s.activation(out=small[:, SB + 7:SB + 8],
                                    in_=small[:, SB + 5:SB + 6],
                                    func=AF.Ln, scale=1.0 / D,
                                    bias=eps_t[:, 0:1]))
                ss.wait(s, ss.n)
                ss.inc(s.activation(out=small[:, SB + 6:SB + 7],
                                    in_=small[:, SB + 7:SB + 8],
                                    func=AF.Exp, scale=-0.5))
            blk.scalar(f)
            m4 = ss.n

            def f(v, k=k, slot=slot, m4=m4, SB=SB):
                ss.wait(v, m4)
                sv.inc(v.tensor_scalar(out=xn2[:, slot, :], in0=junk[:, slot, :],
                                       scalar1=small[:, SB + 6:SB + 7],
                                       scalar2=None, op0=OP.mult))
            blk.vector(f)
            xn_done = sv.n

            qm = emit_quant(xn2[:, slot, :], D)
            v_fin[k] = qm

            def f(t, k=k, qm=qm):
                sv.wait(t, qm)
                if k >= 1:
                    ss.wait(t, s_fin[k - 1])
                for c in range(KC):
                    pe.inc(t.transpose(tp_ps[:, c // 4, c % 4, 0:128],
                                       qout[:, c * 128:(c + 1) * 128],
                                       identb_sb[:]))
            blk.tensor(f)
            pe_fin[k] = pe.n

            def f(s, k=k):
                pe.wait(s, pe_fin[k])
                for c in range(KC):
                    ss.inc(s.activation(out=xqT[:, c, k * 128:(k + 1) * 128],
                                        in_=tp_ps[:, c // 4, c % 4, 0:128],
                                        func=AF.Identity))
            blk.scalar(f)
            s_fin[k] = ss.n

        wpieces = [v_fin[TT - 1], v_fin[TT - 1]]
        def emit_wpiece_q(src_ap, F, dst_fn):
            i = len(wpieces)
            p = i % 2
            war = wpieces[i - 2]

            def f(sync, p=p, war=war, src_ap=src_ap, F=F):
                sv.wait(sync, war)
                wlp[p].inc(sync.dma_start(out=wbuf[:, p, 0:F], in_=src_ap), 16)
            blk.sync(f)
            dmam = wlp[p].n

            qm = emit_quant(wbuf[:, p, 0:F], F, waits_s=((wlp[p], dmam),))
            wpieces.append(qm)
            nblk = F // 128
            for g0 in range(0, nblk, 8):
                g1 = min(g0 + 8, nblk)
                war_ev = ss.n

                def f(t, g0=g0, g1=g1, qm=qm, war_ev=war_ev):
                    sv.wait(t, qm)
                    ss.wait(t, war_ev)
                    for c in range(g0, g1):
                        cc = c - g0
                        pe.inc(t.transpose(tp_ps[:, cc // 4, cc % 4, 0:128],
                                           qout[:, c * 128:(c + 1) * 128],
                                           identb_sb[:]))
                blk.tensor(f)
                tm = pe.n

                def f(s, g0=g0, g1=g1, tm=tm, dst_fn=dst_fn):
                    pe.wait(s, tm)
                    for c in range(g0, g1):
                        cc = c - g0
                        ss.inc(s.activation(out=dst_fn(c),
                                            in_=tp_ps[:, cc // 4, cc % 4, 0:128],
                                            func=AF.Identity))
                blk.scalar(f)

        for rt in range(NK):
            emit_wpiece_q(wg[rt * 128:(rt + 1) * 128, :], D,
                          lambda c, rt=rt: wgq[:, c, rt * 128:(rt + 1) * 128])
            emit_wpiece_q(wu[rt * 128:(rt + 1) * 128, :], D,
                          lambda c, rt=rt: wuq[:, c, rt * 128:(rt + 1) * 128])
        for rt in range(KC):
            for pc, plen in ((0, 1024), (1024, 1024), (2048, 768)):
                i = len(wpieces)
                p = i % 2
                war = wpieces[i - 2]

                def f(sync, p=p, war=war, rt=rt, pc=pc, plen=plen):
                    sv.wait(sync, war)
                    wlp[p].inc(sync.dma_start(
                        out=wbuf[:, p, 0:plen],
                        in_=wd[rt * 128:(rt + 1) * 128, pc:pc + plen]), 16)
                blk.sync(f)
                dmam = wlp[p].n
                qm = emit_quant(wbuf[:, p, 0:plen], plen,
                                waits_s=((wlp[p], dmam),))
                wpieces.append(qm)

                def f(sync, qm=qm, rt=rt, pc=pc, plen=plen):
                    sv.wait(sync, qm)
                    sdq.inc(sync.dma_start(
                        out=wdqnat[rt * 128:(rt + 1) * 128, pc:pc + plen],
                        in_=qout[:, 0:plen]), 16)
                blk.sync(f)
                state["qout_store"] = sdq.n
        wq_done_s = ss.n
        wq_done_v = sv.n
        xqT_done = s_fin[TT - 1]

        es2 = contextlib.ExitStack()
        g_ps = es2.enter_context(nc.psum_tensor(pfx + "g_ps", [128, 512], F32))
        u_ps = es2.enter_context(nc.psum_tensor(pfx + "u_ps", [128, 512], F32))
        d_ps = es2.enter_context(nc.psum_tensor(pfx + "d_ps", [128, 2, 512], F32))
        t2_ps = es2.enter_context(nc.psum_tensor(pfx + "t2_ps", [128, 4, 256], BF16))

        chunk_sizes = [512, 512, 512, 512, 512, 256]
        NCH = len(chunk_sizes)
        prev_sg = 0
        o_marks = []
        state_c = {"prev_opmm": 0, "prev_ysz": 0, "prev_bcst": 0, "prev_hq": 0}
        v_out = [0] * TT
        ld_ks = {}
        pend = [None]

        def emit_tail():
            (pk, pci, pcsz, pqm, poff, pn0) = pend[0]
            pend[0] = None
            nblk = pcsz // 128

            def f(t, pqm=pqm, nblk=nblk, poff=poff,
                  war=state_c["prev_hq"]):
                sv.wait(t, pqm)
                ss.wait(t, war)
                for c in range(nblk):
                    pe.inc(t.transpose(t2_ps[:, c, 0:128],
                                       qout[:, poff + c * 128:
                                            poff + (c + 1) * 128],
                                       identb_sb[:]))
            blk.tensor(f)
            tqm = pe.n

            def f(s, tqm=tqm, nblk=nblk):
                pe.wait(s, tqm)
                for c in range(nblk):
                    ss.inc(s.activation(out=hqT_sb[:, c, 0:128],
                                        in_=t2_ps[:, c, 0:128],
                                        func=AF.Identity))
            blk.scalar(f)
            state_c["prev_hq"] = ss.n
            hq_ev = ss.n

            for c in range(nblk):
                nk2 = (pn0 // 128) + c
                wp = state["wdn"] % 2
                war_mm = state["wdn_mm"].get(state["wdn"] - 2, 0)

                def f(sync, wp=wp, nk2=nk2, war_mm=war_mm,
                      sdq_all=sdq.n):
                    sdq.wait(sync, sdq_all)
                    pe.wait(sync, war_mm)
                    wlp[wp].inc(sync.dma_start(
                        out=wdb[:, wp, :],
                        in_=wdqnat[:, nk2 * 128:(nk2 + 1) * 128],
                        transpose=True), 16)
                blk.sync(f)
                wdm = wlp[wp].n

                def f(t, pci=pci, c=c, nblk=nblk, wp=wp, hq_ev=hq_ev,
                      wdm=wdm, wq_s=wq_done_s,
                      war_o=(o_marks[-1] if o_marks else 0)):
                    ss.wait(t, max(hq_ev, wq_s))
                    wlp[wp].wait(t, wdm)
                    if pci == 0 and c == 0:
                        sv.wait(t, war_o)
                    instr = None
                    for sp in range(2):
                        instr = t.matmul(d_ps[:, sp, :],
                                         hqT_sb[:, c, 0:128],
                                         wdb[:, wp, sp * 512:(sp + 1) * 512],
                                         start=(pci == 0 and c == 0),
                                         stop=(pci == NCH - 1 and
                                               c == nblk - 1))
                    pe.inc(instr)
                blk.tensor(f)
                state["wdn_mm"][state["wdn"]] = pe.n
                state["wdn"] += 1

            if pci == NCH - 1:
                dn_mm = pe.n
                slot = pk % 2

                def f(v, pk=pk, slot=slot, dn_mm=dn_mm, ld_k=ld_ks[pk]):
                    pe.wait(v, dn_mm)
                    ldx[pk % 2].wait(v, ld_k)
                    if pk >= 2:
                        stx[pk % 2].wait(v, 16 * ((pk - 2) // 2 + 1))
                    for sp in range(2):
                        sv.inc(v.tensor_tensor(
                            out=obuf[:, slot, sp * 512:(sp + 1) * 512],
                            in0=xrbuf[:, slot, sp * 512:(sp + 1) * 512],
                            in1=d_ps[:, sp, :], op=OP.add))
                blk.vector(f)
                v_out[pk] = sv.n
                o_marks.append(sv.n)

                def f(sync, pk=pk, slot=slot):
                    sv.wait(sync, v_out[pk])
                    stx[pk % 2].inc(sync.dma_start(
                        out=xout[pk * 128:(pk + 1) * 128, :],
                        in_=obuf[:, slot, :]), 16)
                blk.sync(f)

        for k in range(TT):
            slot = k % 2

            def f(sync, k=k, slot=slot):
                if k >= 2:
                    sv.wait(sync, v_out[k - 2])
                ldx[k % 2].inc(sync.dma_start(out=xrbuf[:, slot, :],
                                              in_=xin[k * 128:(k + 1) * 128, :]), 16)
            blk.sync(f)
            ld_ks[k] = ldx[k % 2].n

            n0 = 0
            for ci, csz in enumerate(chunk_sizes):
                ncol = slice(n0, n0 + csz)

                def f(t, k=k, ci=ci, csz=csz, ncol=ncol, war=prev_sg):
                    ss.wait(t, max(xqT_done, wq_done_s))
                    sv.wait(t, max(war, wq_done_v))
                    instr = None
                    for c in range(KC):
                        instr = t.matmul(g_ps[:, 0:csz],
                                         xqT[:, c, k * 128:(k + 1) * 128],
                                         wgq[:, c, ncol],
                                         start=(c == 0), stop=(c == KC - 1))
                    pe.inc(instr)
                    for c in range(KC):
                        instr = t.matmul(u_ps[:, 0:csz],
                                         xqT[:, c, k * 128:(k + 1) * 128],
                                         wuq[:, c, ncol],
                                         start=(c == 0), stop=(c == KC - 1))
                    pe.inc(instr)
                blk.tensor(f)
                gm_mm = pe.n

                def f(s, csz=csz, gm_mm=gm_mm):
                    pe.wait(s, gm_mm)
                    ss.inc(s.activation(out=sg[:, 0:csz], in_=g_ps[:, 0:csz],
                                        func=AF.Silu))
                blk.scalar(f)
                silu_m = ss.n

                def f(v, csz=csz, silu_m=silu_m):
                    ss.wait(v, silu_m)
                    sv.inc(v.tensor_tensor(out=hbuf[:, 0:csz], in0=sg[:, 0:csz],
                                           in1=u_ps[:, 0:csz], op=OP.mult))
                blk.vector(f)
                prev_sg = sv.n

                qoff = (state["qn2"] % 2) * 512
                state["qn2"] += 1
                qm = emit_quant(hbuf[:, 0:csz], csz, out_off=qoff)
                if pend[0] is not None:
                    emit_tail()
                pend[0] = (k, ci, csz, qm, qoff, n0)
                n0 += csz
        emit_tail()
        es2.close()

        if final:
            def f(g):
                stx[0].wait(g, stx[0].n)
                stx[1].wait(g, stx[1].n)
            blk.gpsimd(f)
    return [(s, s.n) for s in (ldx[0], ldx[1], stx[0], stx[1], wlp[0],
                               wlp[1], sdq, pe, sv, ss)]


def run_l0_ffn(x_full, w_gate, w_up, w_down):
    import ml_dtypes
    nc = build_l0_ffn()
    shared = dict(
        wg=np.ascontiguousarray(w_gate, np.float32),
        wu=np.ascontiguousarray(w_up, np.float32),
        wd=np.ascontiguousarray(w_down, np.float32),
        identb=np.eye(128, dtype=ml_dtypes.bfloat16),
    )
    in_maps = []
    for b in range(NCORES):
        m = dict(shared)
        m["xin"] = np.ascontiguousarray(x_full[b], np.float32)
        in_maps.append(m)
    res = run_bass_kernel_spmd(nc, in_maps, core_ids=list(range(NCORES)))
    return np.stack([r["xout"] for r in res.results], axis=0)





def build_l1_mamba():
    DI = 2048
    nc = bass.Bass("TRN2", target_bir_lowering=False)
    xin = nc.dram_tensor("xin", [T, D], F32, kind="ExternalInput").ap()
    ipwT = nc.dram_tensor("ipwT", [D, 2 * DI], F32, kind="ExternalInput").ap()
    lng = nc.dram_tensor("lng", [128, KC], F32, kind="ExternalInput").ap()
    lnb = nc.dram_tensor("lnb", [128, KC], F32, kind="ExternalInput").ap()
    convw = nc.dram_tensor("convw", [128, DI // 128, 4], F32, kind="ExternalInput").ap()
    convb = nc.dram_tensor("convb", [128, DI // 128], F32, kind="ExternalInput").ap()
    xpwT = nc.dram_tensor("xpwT", [128, DI // 128, 96], F32, kind="ExternalInput").ap()
    dtpwT = nc.dram_tensor("dtpwT", [64, DI], F32, kind="ExternalInput").ap()
    dtpb = nc.dram_tensor("dtpb", [128, DI // 128], F32, kind="ExternalInput").ap()
    alog = nc.dram_tensor("alog", [128, DI // 128, 16], F32, kind="ExternalInput").ap()
    dpar = nc.dram_tensor("dpar", [128, DI // 128], F32, kind="ExternalInput").ap()
    opwT = nc.dram_tensor("opwT", [DI, D], F32, kind="ExternalInput").ap()
    identb = nc.dram_tensor("identb", [128, 128], BF16, kind="ExternalInput").ap()
    xout = nc.dram_tensor("xout", [T, D], F32, kind="ExternalOutput").ap()
    es_top = contextlib.ExitStack()
    with es_top:
        blk = es_top.enter_context(nc.Block())
        emit_l1_mamba(nc, blk, es_top,
                      dict(xin=xin, ipwT=ipwT, lng=lng, lnb=lnb, convw=convw,
                           convb=convb, xpwT=xpwT, dtpwT=dtpwT, dtpb=dtpb,
                           alog=alog, dpar=dpar, opwT=opwT, identb=identb,
                           xout=xout))
    return nc


def emit_l1_mamba(nc, blk, es_top, aps, pfx="", bar=None, final=True):
    DI = 2048
    DH = DI // 128
    TC = 256
    NCH = T // TC
    NTT = TC // 128
    xin = aps["xin"]; ipwT = aps["ipwT"]; lng = aps["lng"]; lnb = aps["lnb"]
    convw = aps["convw"]; convb = aps["convb"]; xpwT = aps["xpwT"]
    dtpwT = aps["dtpwT"]; dtpb = aps["dtpb"]; alog = aps["alog"]
    dpar = aps["dpar"]; opwT = aps["opwT"]; identb = aps["identb"]
    xout = aps["xout"]
    ipw_bf = nc.dram_tensor(pfx + "ipw_bf", [D, 2 * DI], BF16, kind="Internal").ap()
    bc_dram = nc.dram_tensor(pfx + "bc_dram", [2, 16, TC], F32, kind="Internal").ap()

    es = contextlib.ExitStack()
    with es:
        sb = lambda name, shape, dt: es.enter_context(
            nc.sbuf_tensor(pfx + name, shape, dt))

        xnT_ch = sb("xnT_ch", [128, KC, TC], BF16)
        xc_ch = sb("xc_ch", [128, DH, TC], F32)
        dt_ch = sb("dt_ch", [128, DH, TC], F32)
        u_ch = sb("u_ch", [128, DH, TC], F32)
        yst = sb("yst", [128, DH, TC], F32)
        sz_ch = sb("sz_ch", [128, DH, TC], BF16)
        ysz_ch = sb("ysz_ch", [128, DH, TC], BF16)
        mr_ch = sb("mr_ch", [128, NTT, 4], F32)
        opw_sb = sb("opw_sb", [128, DH, D], BF16)
        xpw_sb = sb("xpw_sb", [128, DH, 96], F32)
        dtpw_sb = sb("dtpw_sb", [64, DI], F32)
        A_sb = sb("A_sb", [128, DH, 16], F32)
        cw_sb = sb("cw_sb", [128, DH, 4], F32)
        cb_sb = sb("cb_sb", [128, DH], F32)
        db_sb = sb("db_sb", [128, DH], F32)
        dp_sb = sb("dp_sb", [128, DH], F32)
        g_sb = sb("g_sb", [128, KC], F32)
        b_sb = sb("b_sb", [128, KC], F32)
        bias_e = sb("bias_e", [128, 2 * DH], F32)
        bias_tmp = sb("bias_tmp", [128, DH], F32)
        identb_sb = sb("identb_sb", [128, 128], BF16)
        eps_t = sb("eps_t", [128, 1], F32)
        one_t = sb("one_t", [128, 1], F32)
        es_w = contextlib.ExitStack()
        wstg = es_w.enter_context(nc.sbuf_tensor(pfx + "wstg", [128, 2, DI], F32, side="right"))
        wcvt = es_w.enter_context(nc.sbuf_tensor(pfx + "wcvt", [128, DI], BF16, side="right"))
        ipb = sb("ipb", [128, 2, KC, 128], BF16)
        xbuf = sb("xbuf", [128, 2, D], F32)
        junk = sb("junk", [128, 2, D], F32)
        xn1b = sb("xn1b", [128, 2, D], F32)
        xn2b = sb("xn2b", [128, 2, D], BF16)
        small = sb("small", [128, 32], F32)
        xs_scr = sb("xs_scr", [128, 2, TC + 3], F32)
        cacc = sb("cacc", [128, 2, TC], F32)
        carry3 = sb("carry3", [128, DH, 4], F32)
        dtlow = sb("dtlow", [64, TC], F32)
        bsb = sb("bsb", [16, TC], F32)
        csb = sb("csb", [16, TC], F32)
        bcb = sb("bcb", [128, 2, 2, TC], F32)
        hcar = sb("hcar", [128, 256], F32)
        Et = sb("Et", [128, TC], F32)
        spr = sb("spr", [128, TC], F32)
        udb2 = sb("udb2", [128, 2, TC], F32)
        ht2 = sb("ht2", [128, 2, TC], F32)
        hct2 = sb("hct2", [128, 2, TC], F32)
        ldx = [Sem(es_top, nc, pfx + "ldx0"), Sem(es_top, nc, pfx + "ldx1")]
        stx = [Sem(es_top, nc, pfx + "stx0"), Sem(es_top, nc, pfx + "stx1")]
        wlp = [Sem(es_top, nc, pfx + "wl0"), Sem(es_top, nc, pfx + "wl1")]
        bcp = [Sem(es_top, nc, pfx + "bc0"), Sem(es_top, nc, pfx + "bc1")]
        sbc = Sem(es_top, nc, pfx + "sbc")
        ldc = Sem(es_top, nc, pfx + "ldc")
        pe = Sem(es_top, nc, pfx + "pe")
        sv = Sem(es_top, nc, pfx + "sv")
        ss = Sem(es_top, nc, pfx + "ss")
        if bar:
            def fbar(eng):
                for _s, _c in bar:
                    _s.wait(eng, _c)
            blk.sync(fbar)
            blk.scalar(fbar)
            blk.vector(fbar)
            blk.tensor(fbar)

        def f(sync):
            for o, i in [
                (identb_sb[:], identb),
                (A_sb[:], alog), (cw_sb[:], convw), (cb_sb[:], convb),
                (db_sb[:], dtpb), (dp_sb[:], dpar),
                (g_sb[:], lng), (b_sb[:], lnb),
                (xpw_sb[:], xpwT),
                (dtpw_sb[:], dtpwT),
            ]:
                ldc.inc(sync.dma_start(out=o, in_=i), 16)
        blk.sync(f)
        ldc_all = ldc.n

        def f(v):
            v.memset(eps_t[:], 1e-5)
            v.memset(one_t[:], 1.0)
            v.memset(hcar[:], 0.0)
            v.memset(carry3[:], 0.0)
            ldc.wait(v, ldc_all)
            sv.inc(v.tensor_scalar(out=small[:, 0:1], in0=eps_t[:, 0:1],
                                   scalar1=1.0, scalar2=None, op0=OP.mult))
        blk.vector(f)
        cst_v = sv.n

        def f(s):
            ldc.wait(s, ldc_all)
            ss.inc(s.activation(out=A_sb[:], in_=A_sb[:], func=AF.Exp))
        blk.scalar(f)
        aexp = ss.n

        def f(v, aexp=aexp):
            ss.wait(v, aexp)
            sv.inc(v.tensor_scalar(out=A_sb[:], in0=A_sb[:], scalar1=-1.0,
                                   scalar2=None, op0=OP.mult))
        blk.vector(f)
        a_done = sv.n

        es1 = contextlib.ExitStack()
        bias_ps = es1.enter_context(nc.psum_tensor(pfx + "bias_ps", [128, 512], F32))
        wcv = []
        state_b = {}
        for kc in range(KC):
            for hh in range(2):
                i = len(wcv)
                p = i % 2
                war = wcv[i - 2] if i >= 2 else 0

                def f(sync, kc=kc, hh=hh, p=p, war=war):
                    ss.wait(sync, war)
                    wlp[p].inc(sync.dma_start(
                        out=wstg[0:64, p, :],
                        in_=ipwT[kc * 128:kc * 128 + 64,
                                 hh * DI:(hh + 1) * DI]), 16)
                blk.sync(f)

                def f(s, kc=kc, hh=hh, p=p, war=war):
                    ss.wait(s, war)
                    wlp[p].inc(s.dma_start(
                        out=wstg[64:128, p, :],
                        in_=ipwT[kc * 128 + 64:(kc + 1) * 128,
                                 hh * DI:(hh + 1) * DI]), 16)
                blk.scalar(f)
                dm = wlp[p].n

                def f(v, kc=kc, p=p, dm=dm):
                    wlp[p].wait(v, dm)
                    sv.inc(v.tensor_scalar(out=wstg[:, p, :], in0=wstg[:, p, :],
                                           scalar1=g_sb[:, kc:kc + 1],
                                           scalar2=None, op0=OP.mult))
                blk.vector(f)
                sc_m = sv.n

                def f(t, kc=kc, hh=hh, p=p, sc_m=sc_m,
                      war=state_b.get("bias_ev", 0)):
                    sv.wait(t, sc_m)
                    ss.wait(t, war)
                    for et in range(DH):
                        pe.inc(t.matmul(bias_ps[:, et:et + 1],
                                        wstg[:, p, et * 128:(et + 1) * 128],
                                        b_sb[:, kc:kc + 1],
                                        start=True, stop=True))
                blk.tensor(f)
                pe_m = pe.n

                def f(s, hh=hh, pe_m=pe_m):
                    pe.wait(s, pe_m)
                    ss.inc(s.activation(out=bias_tmp[:, 0:DH],
                                        in_=bias_ps[:, 0:DH], func=AF.Identity))
                blk.scalar(f)
                bias_ev = ss.n
                state_b["bias_ev"] = bias_ev

                def f(v, kc=kc, hh=hh, bias_ev=bias_ev):
                    ss.wait(v, bias_ev)
                    if kc == 0:
                        sv.inc(v.tensor_scalar(
                            out=bias_e[:, hh * DH:(hh + 1) * DH],
                            in0=bias_tmp[:, 0:DH], scalar1=1.0, scalar2=None,
                            op0=OP.mult))
                    else:
                        sv.inc(v.tensor_tensor(
                            out=bias_e[:, hh * DH:(hh + 1) * DH],
                            in0=bias_e[:, hh * DH:(hh + 1) * DH],
                            in1=bias_tmp[:, 0:DH], op=OP.add))
                    sv.wait(v, sv.n)
                blk.vector(f)

                def f(s, p=p, sc_m=sc_m, pe_m=pe_m, kc=kc, hh=hh,
                      wst=state_b.get("wcvt_store", 0)):
                    sv.wait(s, sc_m)
                    pe.wait(s, pe_m)
                    sbc.wait(s, wst)
                    ss.inc(s.activation(out=wcvt[:, 0:DI],
                                        in_=wstg[:, p, :], func=AF.Identity))
                blk.scalar(f)
                cv_m = ss.n

                def f(sync, kc=kc, hh=hh, p=p, cv_m=cv_m):
                    ss.wait(sync, cv_m)
                    sbc.inc(sync.dma_start(
                        out=ipw_bf[kc * 128:(kc + 1) * 128, hh * DI:(hh + 1) * DI],
                        in_=wcvt[:, 0:DI]), 16)
                blk.sync(f)
                wcv.append(ss.n)
                state_b["wcvt_store"] = sbc.n
        ipw_stored = sbc.n

        bias_done_v = sv.n
        es1.close()

        ocv = [wcv[-1], wcv[-1]]
        for rt in range(DH):
            p = rt % 2
            war = ocv[rt]

            def f(sync, rt=rt, p=p, war=war):
                ss.wait(sync, war)
                wlp[p].inc(sync.dma_start(
                    out=wstg[0:64, p, 0:D],
                    in_=opwT[rt * 128:rt * 128 + 64, :]), 16)
            blk.sync(f)

            def f(s, rt=rt, p=p, war=war):
                ss.wait(s, war)
                wlp[p].inc(s.dma_start(
                    out=wstg[64:128, p, 0:D],
                    in_=opwT[rt * 128 + 64:(rt + 1) * 128, :]), 16)
            blk.scalar(f)
            dm = wlp[p].n

            def f(s, rt=rt, p=p, dm=dm):
                wlp[p].wait(s, dm)
                ss.inc(s.activation(out=opw_sb[:, rt, :], in_=wstg[:, p, 0:D],
                                    func=AF.Identity))
            blk.scalar(f)
            ocv.append(ss.n)
        opw_done = ss.n
        es_w.close()
        obuf = es.enter_context(nc.sbuf_tensor(pfx + "obuf", [128, 2, D], F32, side="right"))
        xrbuf = es.enter_context(nc.sbuf_tensor(pfx + "xrbuf", [128, 2, D], F32, side="right"))

        es2 = contextlib.ExitStack()
        tp_ps = es2.enter_context(nc.psum_tensor(pfx + "tp_ps", [128, 2, 4, 256], BF16))
        ip_ps = es2.enter_context(nc.psum_tensor(pfx + "ip_ps", [128, 512], F32))
        xd_ps = es2.enter_context(nc.psum_tensor(pfx + "xd_ps", [128, 512], F32))
        xd2_ps = es2.enter_context(nc.psum_tensor(pfx + "xd2_ps", [128, 512], F32))
        dt_ps = es2.enter_context(nc.psum_tensor(pfx + "dt_ps", [128, 512], F32))
        op_ps = es2.enter_context(nc.psum_tensor(pfx + "op_ps", [128, 2, 512], F32))

        ipb_marks = {}
        state = {"ipn": 0}
        prev_ip_ev = 0
        prev_xn_done = [0] * 2
        ln_v = [0] * (T // 128)
        ln_pe = [0] * (T // 128)
        ln_s = [0] * (T // 128)
        prev_scan_v = 0
        prev_dtev = 0
        prev_bc_store = 0
        repl_hist = [0, 0]
        state_c = {"prev_opmm": 0, "prev_ysz": 0, "prev_bcst": 0}
        v_out = {}

        for ch in range(NCH):
            t0 = ch * TC
            for kt in range(NTT):
                k = ch * NTT + kt
                slot = k % 2
                SB = 8 * slot

                def f(sync, k=k, slot=slot,
                      war=(ln_v[k - 2] if k >= 2 else 0)):
                    sv.wait(sync, war)
                    ldx[slot].inc(sync.dma_start(
                        out=xbuf[:, slot, :],
                        in_=xin[k * 128:(k + 1) * 128, :]), 16)
                blk.sync(f)
                ld_k = ldx[slot].n

                def f(v, k=k, slot=slot, ld_k=ld_k, SB=SB):
                    ldx[slot].wait(v, ld_k)
                    sv.inc(v.tensor_reduce(out=small[:, SB:SB + 1],
                                           in_=xbuf[:, slot, :], axis=AX,
                                           op=OP.add))
                blk.vector(f)
                m1 = sv.n

                def f(s, m1=m1, SB=SB):
                    sv.wait(s, m1)
                    ss.inc(s.activation(out=small[:, SB + 4:SB + 5],
                                        in_=small[:, SB:SB + 1],
                                        func=AF.Identity, scale=1.0 / D))
                blk.scalar(f)
                m2 = ss.n

                def f(v, k=k, slot=slot, m2=m2, SB=SB):
                    ss.wait(v, m2)
                    sv.inc(v.tensor_scalar(out=junk[:, slot, :],
                                           in0=xbuf[:, slot, :],
                                           scalar1=small[:, SB + 4:SB + 5],
                                           scalar2=None, op0=OP.subtract))
                blk.vector(f)
                m3 = sv.n

                def f(s, slot=slot, m3=m3, SB=SB):
                    sv.wait(s, m3)
                    ss.inc(s.activation(out=xn1b[:, slot, :],
                                        in_=junk[:, slot, :], func=AF.Square,
                                        accum_out=small[:, SB + 5:SB + 6]))
                    ss.wait(s, ss.n)
                    ss.inc(s.activation(out=small[:, SB + 7:SB + 8],
                                        in_=small[:, SB + 5:SB + 6],
                                        func=AF.Ln, scale=1.0 / D,
                                        bias=eps_t[:, 0:1]))
                    ss.wait(s, ss.n)
                    ss.inc(s.activation(out=small[:, SB + 6:SB + 7],
                                        in_=small[:, SB + 7:SB + 8],
                                        func=AF.Exp, scale=-0.5))
                blk.scalar(f)
                m4 = ss.n

                def f(v, k=k, kt=kt, slot=slot, m4=m4, SB=SB):
                    ss.wait(v, m4)
                    v.tensor_scalar(out=mr_ch[:, kt, 0:1],
                                    in0=small[:, SB + 4:SB + 5], scalar1=1.0,
                                    scalar2=None, op0=OP.mult)
                    v.tensor_scalar(out=mr_ch[:, kt, 1:2],
                                    in0=small[:, SB + 6:SB + 7], scalar1=1.0,
                                    scalar2=None, op0=OP.mult)
                    sv.inc(v.tensor_scalar(out=xn1b[:, slot, :],
                                           in0=junk[:, slot, :],
                                           scalar1=small[:, SB + 6:SB + 7],
                                           scalar2=None, op0=OP.mult))
                blk.vector(f)
                ln1_done = sv.n

                def f(v, slot=slot, ln1_done=ln1_done, SB=SB):
                    sv.wait(v, ln1_done)
                    sv.inc(v.tensor_reduce(out=small[:, SB + 2:SB + 3],
                                           in_=xn1b[:, slot, :], axis=AX,
                                           op=OP.add))
                blk.vector(f)
                n1 = sv.n

                def f(s, n1=n1, SB=SB):
                    sv.wait(s, n1)
                    ss.inc(s.activation(out=small[:, SB + 7:SB + 8],
                                        in_=small[:, SB + 2:SB + 3],
                                        func=AF.Identity, scale=1.0 / D))
                blk.scalar(f)
                n2 = ss.n

                def f(v, slot=slot, n2=n2, SB=SB):
                    ss.wait(v, n2)
                    sv.inc(v.tensor_scalar(out=junk[:, slot, :],
                                           in0=xn1b[:, slot, :],
                                           scalar1=small[:, SB + 7:SB + 8],
                                           scalar2=None, op0=OP.subtract))
                blk.vector(f)
                n3 = sv.n

                def f(s, slot=slot, n3=n3, SB=SB):
                    sv.wait(s, n3)
                    ss.inc(s.activation(out=xn2b[:, slot, :],
                                        in_=junk[:, slot, :], func=AF.Square,
                                        accum_out=small[:, SB + 5:SB + 6]))
                    ss.wait(s, ss.n)
                    ss.inc(s.activation(out=small[:, SB + 7:SB + 8],
                                        in_=small[:, SB + 5:SB + 6],
                                        func=AF.Ln, scale=1.0 / D,
                                        bias=eps_t[:, 0:1]))
                    ss.wait(s, ss.n)
                    ss.inc(s.activation(out=small[:, SB + 6:SB + 7],
                                        in_=small[:, SB + 7:SB + 8],
                                        func=AF.Exp, scale=-0.5))
                blk.scalar(f)
                n4 = ss.n

                def f(v, k=k, slot=slot, n4=n4, SB=SB,
                      war=(ln_pe[k - 2] if k >= 2 else 0)):
                    ss.wait(v, n4)
                    pe.wait(v, war)
                    sv.inc(v.tensor_scalar(out=xn2b[:, slot, :],
                                           in0=junk[:, slot, :],
                                           scalar1=small[:, SB + 6:SB + 7],
                                           scalar2=None, op0=OP.mult))
                blk.vector(f)
                ln_v[k] = sv.n

                def f(t, k=k, slot=slot, war=(ln_s[k - 1] if k >= 1 else 0)):
                    sv.wait(t, ln_v[k])
                    ss.wait(t, war)
                    for c in range(KC):
                        pe.inc(t.transpose(tp_ps[:, c // 4, c % 4, 0:128],
                                           xn2b[:, slot, c * 128:(c + 1) * 128],
                                           identb_sb[:]))
                blk.tensor(f)
                ln_pe[k] = pe.n

                def f(s, k=k, kt=kt):
                    pe.wait(s, ln_pe[k])
                    for c in range(KC):
                        ss.inc(s.activation(
                            out=xnT_ch[:, c, kt * 128:(kt + 1) * 128],
                            in_=tp_ps[:, c // 4, c % 4, 0:128],
                            func=AF.Identity))
                blk.scalar(f)
                ln_s[k] = ss.n
            xnT_ready = ss.n

            conv_evs = []
            conv_vm = []
            conv_sil = []
            evac_v = []
            sil_pend = []
            for et in range(2 * DH):
                ipn = state["ipn"]
                p = ipn % 2
                war_mm = ipb_marks.get(ipn - 2, 0)

                def f(sync, et=et, p=p, war_mm=war_mm, ipst=ipw_stored):
                    sbc.wait(sync, ipst)
                    pe.wait(sync, war_mm)
                    wlp[p].inc(sync.dma_start(
                        out=ipb[:, p, :, :],
                        in_=ipw_bf.rearrange("(kc p) e -> p kc e", p=128)[
                            :, :, et * 128:(et + 1) * 128]), 16)
                blk.sync(f)
                dm = wlp[p].n

                def f(t, et=et, p=p, dm=dm, xr=xnT_ready,
                      war=(evac_v[-2] if len(evac_v) >= 2 else 0),
                      bd=bias_done_v, wv=state_c["prev_ysz"]):
                    wlp[p].wait(t, dm)
                    ss.wait(t, xr)
                    sv.wait(t, max(bd, wv, war))
                    ips = ip_ps if et % 2 == 0 else xd_ps
                    instr = None
                    for c in range(KC):
                        instr = t.matmul(ips[:, 0:TC], ipb[:, p, c, :],
                                         xnT_ch[:, c, :],
                                         start=(c == 0), stop=(c == KC - 1))
                    pe.inc(instr)
                blk.tensor(f)
                ipb_marks[ipn] = pe.n
                state["ipn"] += 1
                mm_m = pe.n

                if et < DH:
                    cs = et % 2

                    def f(v, ch=ch, et=et, cs=cs, mm_m=mm_m,
                          war=(conv_sil[-2] if len(conv_sil) >= 2 else 0)):
                        pe.wait(v, mm_m)
                        ss.wait(v, war)
                        ips = ip_ps if et % 2 == 0 else xd_ps
                        sv.inc(v.tensor_tensor(
                            out=xs_scr[:, cs, 3:3 + TC], in0=ips[:, 0:TC],
                            in1=bias_e[:, et:et + 1].to_broadcast([128, TC]),
                            op=OP.add))
                        evac_v.append(sv.n)
                        sv.wait(v, sv.n)
                        sv.inc(v.tensor_scalar(out=xs_scr[:, cs, 0:3],
                                               in0=carry3[:, et, 0:3],
                                               scalar1=1.0, scalar2=None,
                                               op0=OP.mult))
                        sv.wait(v, sv.n)
                        sv.inc(v.tensor_scalar(out=carry3[:, et, 0:3],
                                               in0=xs_scr[:, cs, TC:TC + 3],
                                               scalar1=1.0, scalar2=None,
                                               op0=OP.mult))
                        sv.inc(v.scalar_tensor_tensor(
                            out=cacc[:, cs, :], in0=xs_scr[:, cs, 0:TC],
                            scalar=cw_sb[:, et, 0:1],
                            in1=cb_sb[:, et:et + 1].to_broadcast([128, TC]),
                            op0=OP.mult, op1=OP.add))
                        sv.wait(v, sv.n)
                        for j in range(1, 4):
                            sv.inc(v.scalar_tensor_tensor(
                                out=cacc[:, cs, :], in0=xs_scr[:, cs, j:j + TC],
                                scalar=cw_sb[:, et, j:j + 1], in1=cacc[:, cs, :],
                                op0=OP.mult, op1=OP.add))
                            sv.wait(v, sv.n)
                    blk.vector(f)
                    cv_m = sv.n
                    conv_vm.append(cv_m)
                    sil_pend.append((et, cs, cv_m, True))
                    if len(sil_pend) == 2:
                        def f(s, pend=tuple(sil_pend)):
                            for pet, pcs, pcv, isx in pend:
                                sv.wait(s, pcv)
                                if isx:
                                    ss.inc(s.activation(out=xc_ch[:, pet, :],
                                                        in_=cacc[:, pcs, :],
                                                        func=AF.Silu))
                                else:
                                    ss.inc(s.activation(
                                        out=sz_ch[:, pet - DH, :],
                                        in_=cacc[:, pcs, :], func=AF.Silu))
                                conv_sil.append(ss.n)
                                conv_evs.append(ss.n)
                        blk.scalar(f)
                        sil_pend.clear()
                else:
                    cs = et % 2

                    def f(v, et=et, cs=cs, mm_m=mm_m,
                          war=(conv_sil[-2] if len(conv_sil) >= 2 else 0)):
                        pe.wait(v, mm_m)
                        ss.wait(v, war)
                        ips = ip_ps if et % 2 == 0 else xd_ps
                        sv.inc(v.tensor_tensor(
                            out=cacc[:, cs, :], in0=ips[:, 0:TC],
                            in1=bias_e[:, et:et + 1].to_broadcast([128, TC]),
                            op=OP.add))
                    blk.vector(f)
                    zv_m = sv.n
                    evac_v.append(zv_m)
                    sil_pend.append((et, cs, zv_m, False))
                    if len(sil_pend) == 2:
                        def f(s, pend=tuple(sil_pend)):
                            for pet, pcs, pcv, isx in pend:
                                sv.wait(s, pcv)
                                if isx:
                                    ss.inc(s.activation(out=xc_ch[:, pet, :],
                                                        in_=cacc[:, pcs, :],
                                                        func=AF.Silu))
                                else:
                                    ss.inc(s.activation(
                                        out=sz_ch[:, pet - DH, :],
                                        in_=cacc[:, pcs, :], func=AF.Silu))
                                conv_sil.append(ss.n)
                                conv_evs.append(ss.n)
                        blk.scalar(f)
                        sil_pend.clear()
            xc_done = ss.n

            def f(t, xc_done=xc_done, war=prev_dtev):
                ss.wait(t, max(xc_done, war))
                instr = None
                for c in range(DH):
                    instr = t.matmul(xd_ps[0:80, 0:TC], xpw_sb[:, c, 0:80],
                                     xc_ch[:, c, :],
                                     start=(c == 0), stop=(c == DH - 1))
                pe.inc(instr)
                instr = None
                for c in range(DH):
                    instr = t.matmul(xd2_ps[0:16, 0:TC], xpw_sb[:, c, 80:96],
                                     xc_ch[:, c, :],
                                     start=(c == 0), stop=(c == DH - 1))
                pe.inc(instr)
            blk.tensor(f)
            xd_mm = pe.n

            def f(s, xd_mm=xd_mm, wb=state_c["prev_bcst"]):
                pe.wait(s, xd_mm)
                sbc.wait(s, wb)
                ss.inc(s.activation(out=dtlow[:, :], in_=xd_ps[0:64, 0:TC],
                                    func=AF.Identity))
                ss.inc(s.activation(out=bsb[0:16, :], in_=xd_ps[64:80, 0:TC],
                                    func=AF.Identity))
                ss.inc(s.activation(out=csb[0:16, :], in_=xd2_ps[0:16, 0:TC],
                                    func=AF.Identity))
            blk.scalar(f)
            xd_ev = ss.n

            def f(sync, ch=ch, xd_ev=xd_ev):
                ss.wait(sync, xd_ev)
                sbc.inc(sync.dma_start(out=bc_dram[0], in_=bsb[0:16, :]), 16)
                sbc.inc(sync.dma_start(out=bc_dram[1], in_=csb[0:16, :]), 16)
            blk.sync(f)
            bc_store = sbc.n
            state_c["prev_bcst"] = bc_store

            dt_evs = []
            dtv_marks = [0]
            for dhi in range(DH):
                def f(t, dhi=dhi, xd_ev=xd_ev,
                      war=(dt_evs[-1] if dt_evs else prev_dtev)):
                    ss.wait(t, max(xd_ev, war))
                    pe.inc(t.matmul(dt_ps[:, 0:TC],
                                    dtpw_sb[:, dhi * 128:(dhi + 1) * 128],
                                    dtlow[:, :], start=True, stop=True))
                blk.tensor(f)
                dtm = pe.n

                def f(s, dhi=dhi, dtm=dtm, wv=dtv_marks[-1]):
                    pe.wait(s, dtm)
                    sv.wait(s, wv)
                    ss.inc(s.activation(out=Et[:, :], in_=dt_ps[:, 0:TC],
                                        func=AF.Abs, bias=db_sb[:, dhi:dhi + 1]))
                    ss.wait(s, ss.n)
                    ss.inc(s.activation(out=spr[:, :], in_=Et[:, :],
                                        func=AF.Exp, scale=-1.0))
                    ss.wait(s, ss.n)
                    ss.inc(s.activation(out=Et[:, :], in_=spr[:, :],
                                        func=AF.Ln, bias=one_t[:, 0:1]))
                    ss.wait(s, ss.n)
                    ss.inc(s.activation(out=spr[:, :], in_=dt_ps[:, 0:TC],
                                        func=AF.Relu, bias=db_sb[:, dhi:dhi + 1]))
                blk.scalar(f)
                sp_m = ss.n
                dt_evs.append(ss.n)

                def f(v, dhi=dhi, sp_m=sp_m):
                    ss.wait(v, sp_m)
                    sv.inc(v.tensor_tensor(out=dt_ch[:, dhi, :], in0=Et[:, :],
                                           in1=spr[:, :], op=OP.add))
                    sv.wait(v, sv.n)
                    sv.inc(v.tensor_tensor(out=u_ch[:, dhi, :],
                                           in0=dt_ch[:, dhi, :],
                                           in1=xc_ch[:, dhi, :], op=OP.mult))
                blk.vector(f)
                dtv_marks.append(sv.n)
            dt_done = sv.n
            prev_dtev = ss.n

            eta_t = [xn1b, junk]
            eta_sm = {}
            pq_scan = [0, 0]
            eta_war = [dt_done, dt_done]

            def emit_eta(n):
                def f(s, n=n, war=(eta_war[0], eta_war[1])):
                    for hh in range(2):
                        sv.wait(s, war[hh])
                        for di in range(8):
                            dhi = hh * 8 + di
                            ev = eta_t[hh][:, di // 4,
                                           (di % 4) * TC:(di % 4 + 1) * TC]
                            ss.inc(s.activation(out=ev, in_=dt_ch[:, dhi, :],
                                                func=AF.Exp,
                                                scale=A_sb[:, dhi, n:n + 1]))
                        eta_sm[(n, hh)] = ss.n
                blk.scalar(f)

            emit_eta(0)
            for n in range(16):
                p = n % 2
                war_r = repl_hist[-2]

                def f(sync, ch=ch, n=n, p=p, bc_store=bc_store, war_r=war_r):
                    sbc.wait(sync, bc_store)
                    sv.wait(sync, war_r)
                    bcp[p].inc(sync.dma_start(
                        out=bcb[:, p, 0, :],
                        in_=bc_dram[0, n, :].unsqueeze(0).to_broadcast([128, TC])),
                        16)
                    bcp[p].inc(sync.dma_start(
                        out=bcb[:, p, 1, :],
                        in_=bc_dram[1, n, :].unsqueeze(0).to_broadcast([128, TC])),
                        16)
                blk.sync(f)
                repl_m = bcp[p].n

                for hh in range(2):
                    for d0 in range(0, 8, 2):
                        def f(v, n=n, hh=hh, d0=d0, p=p, repl_m=repl_m):
                            if d0 == 0:
                                ss.wait(v, eta_sm[(n, hh)])
                                bcp[p].wait(v, repl_m)
                            pair = []
                            for di in (d0, d0 + 1):
                                dhi = hh * 8 + di
                                q = dhi % 2
                                idx = dhi * 16 + n
                                ev = eta_t[hh][:, di // 4,
                                               (di % 4) * TC:(di % 4 + 1) * TC]
                                pair.append((dhi, q, idx, ev))
                            mu = []
                            for dhi, q, idx, ev in pair:
                                sv.wait(v, pq_scan[q])
                                sv.inc(v.tensor_tensor(out=udb2[:, q, :],
                                                       in0=u_ch[:, dhi, :],
                                                       in1=bcb[:, p, 0, :],
                                                       op=OP.mult))
                                mu.append(sv.n)
                            msc = []
                            for (dhi, q, idx, ev), m in zip(pair, mu):
                                sv.wait(v, m)
                                sv.inc(v.tensor_tensor_scan(
                                    out=ht2[:, q, :], data0=ev,
                                    data1=udb2[:, q, :],
                                    initial=hcar[:, idx:idx + 1],
                                    op0=OP.mult, op1=OP.add))
                                msc.append(sv.n)
                                pq_scan[q] = sv.n
                                eta_war[hh] = sv.n
                            mh = []
                            for (dhi, q, idx, ev), m in zip(pair, msc):
                                sv.wait(v, m)
                                sv.inc(v.tensor_scalar(out=hcar[:, idx:idx + 1],
                                                       in0=ht2[:, q, TC - 1:TC],
                                                       scalar1=1.0, scalar2=None,
                                                       op0=OP.mult))
                                sv.inc(v.tensor_tensor(out=hct2[:, q, :],
                                                       in0=ht2[:, q, :],
                                                       in1=bcb[:, p, 1, :],
                                                       op=OP.mult))
                                mh.append(sv.n)
                            for (dhi, q, idx, ev), m in zip(pair, mh):
                                sv.wait(v, m)
                                if n == 0:
                                    sv.inc(v.tensor_scalar(out=yst[:, dhi, :],
                                                           in0=hct2[:, q, :],
                                                           scalar1=1.0,
                                                           scalar2=None,
                                                           op0=OP.mult))
                                else:
                                    sv.inc(v.tensor_tensor(out=yst[:, dhi, :],
                                                           in0=yst[:, dhi, :],
                                                           in1=hct2[:, q, :],
                                                           op=OP.add))
                        blk.vector(f)
                if n + 1 < 16:
                    emit_eta(n + 1)
                repl_hist.append(sv.n)
            prev_scan_v = sv.n

            fin_v = []
            for dhi in range(DH):
                def f(v, dhi=dhi, war=prev_scan_v, wpe=state_c["prev_opmm"]):
                    sv.wait(v, war)
                    pe.wait(v, wpe)
                    sv.inc(v.scalar_tensor_tensor(out=yst[:, dhi, :],
                                                  in0=xc_ch[:, dhi, :],
                                                  scalar=dp_sb[:, dhi:dhi + 1],
                                                  in1=yst[:, dhi, :],
                                                  op0=OP.mult, op1=OP.add))
                    sv.wait(v, sv.n)
                    sv.inc(v.tensor_tensor(out=ysz_ch[:, dhi, :],
                                           in0=yst[:, dhi, :],
                                           in1=sz_ch[:, dhi, :], op=OP.mult))
                blk.vector(f)
                fin_v.append(sv.n)
            ysz_done = sv.n

            for kt in range(NTT):
                k = ch * NTT + kt
                slot = k % 2

                def f(sync, k=k, slot=slot,
                      war=v_out.get(k - 2, 0)):
                    sv.wait(sync, war)
                    ldx[slot].inc(sync.dma_start(
                        out=xrbuf[:, slot, :],
                        in_=xin[k * 128:(k + 1) * 128, :]), 16)
                blk.sync(f)
                ld_k = ldx[slot].n

                for sp in range(2):
                    def f(t, k=k, sp=sp, kt=kt, ysz_done=ysz_done, opd=opw_done,
                          war=v_out.get(k - 1, 0)):
                        sv.wait(t, max(ysz_done, war))
                        ss.wait(t, opd)
                        instr = None
                        for c in range(DH):
                            instr = t.matmul(op_ps[:, sp, :],
                                             ysz_ch[:, c, kt * 128:(kt + 1) * 128],
                                             opw_sb[:, c, sp * 512:(sp + 1) * 512],
                                             start=(c == 0), stop=(c == DH - 1))
                        pe.inc(instr)
                    blk.tensor(f)
                op_mm = pe.n

                def f(v, k=k, kt=kt, slot=slot, op_mm=op_mm, ld_k=ld_k):
                    pe.wait(v, op_mm)
                    ldx[slot].wait(v, ld_k)
                    if k >= 2:
                        stx[slot].wait(v, 16 * ((k - 2) // 2 + 1))
                    sv.inc(v.tensor_scalar(out=junk[:, slot, :],
                                           in0=xrbuf[:, slot, :],
                                           scalar1=mr_ch[:, kt, 0:1],
                                           scalar2=mr_ch[:, kt, 1:2],
                                           op0=OP.subtract, op1=OP.mult))
                    sv.wait(v, sv.n)
                    sv.inc(v.tensor_tensor(out=junk[:, slot, :],
                                           in0=junk[:, slot, :],
                                           in1=xrbuf[:, slot, :], op=OP.add))
                    sv.wait(v, sv.n)
                    for sp in range(2):
                        sv.inc(v.tensor_tensor(
                            out=obuf[:, slot, sp * 512:(sp + 1) * 512],
                            in0=junk[:, slot, sp * 512:(sp + 1) * 512],
                            in1=op_ps[:, sp, :], op=OP.add))
                blk.vector(f)
                v_out[k] = sv.n

                def f(sync, k=k, slot=slot):
                    sv.wait(sync, v_out[k])
                    stx[slot].inc(sync.dma_start(
                        out=xout[k * 128:(k + 1) * 128, :],
                        in_=obuf[:, slot, :]), 16)
                blk.sync(f)
            state_c["prev_opmm"] = pe.n
            state_c["prev_ysz"] = ysz_done
        es2.close()

        if final:
            def f(g):
                stx[0].wait(g, stx[0].n)
                stx[1].wait(g, stx[1].n)
            blk.gpsimd(f)
    return [(s, s.n) for s in (ldx[0], ldx[1], stx[0], stx[1], wlp[0],
                               wlp[1], bcp[0], bcp[1], sbc, ldc, pe, sv, ss)]


def run_l1_mamba(x_full, ln_gamma, ln_beta, in_proj_w, conv_w, conv_b, x_proj_w,
                 dt_proj_w, dt_proj_b, A_log, D_param, out_proj_w):
    import ml_dtypes
    nc = build_l1_mamba()
    r1 = lambda a: np.ascontiguousarray(
        np.asarray(a, np.float32).reshape(-1, 128).T)
    r2 = lambda a, w: np.ascontiguousarray(
        np.asarray(a, np.float32).reshape(-1, 128, w).transpose(1, 0, 2))
    shared = dict(
        ipwT=np.ascontiguousarray(in_proj_w.T, np.float32),
        lng=r1(ln_gamma), lnb=r1(ln_beta),
        convw=r2(conv_w, 4), convb=r1(conv_b),
        xpwT=r2(x_proj_w.T, 96),
        dtpwT=np.ascontiguousarray(dt_proj_w.T, np.float32),
        dtpb=r1(dt_proj_b),
        alog=r2(A_log, 16), dpar=r1(D_param),
        opwT=np.ascontiguousarray(out_proj_w.T, np.float32),
        identb=np.eye(128, dtype=ml_dtypes.bfloat16),
    )
    in_maps = []
    for b in range(NCORES):
        m = dict(shared)
        m["xin"] = np.ascontiguousarray(x_full[b], np.float32)
        in_maps.append(m)
    res = run_bass_kernel_spmd(nc, in_maps, core_ids=list(range(NCORES)))
    return np.stack([r["xout"] for r in res.results], axis=0)





def build_l1_stats():
    FQ = 2048
    nc = bass.Bass("TRN2", target_bir_lowering=False)
    xin = nc.dram_tensor("xin", [T, D], F32, kind="ExternalInput").ap()
    wq = nc.dram_tensor("wq", [D, FQ], F32, kind="ExternalInput").ap()
    bq = nc.dram_tensor("bq", [128, 16], F32, kind="ExternalInput").ap()
    identb = nc.dram_tensor("identb", [128, 128], BF16, kind="ExternalInput").ap()
    statout = nc.dram_tensor("statout", [128, 4], F32, kind="ExternalOutput").ap()
    es_top = contextlib.ExitStack()
    with es_top:
        blk = es_top.enter_context(nc.Block())
        emit_l1_stats(nc, blk, es_top,
                      dict(xin=xin, wq=wq, bq=bq, identb=identb,
                           statout=statout))
    return nc


def emit_l1_stats(nc, blk, es_top, aps, pfx="", bar=None, final=True):
    FQ = 2048
    xin = aps["xin"]; wq = aps["wq"]; bq = aps["bq"]
    identb = aps["identb"]; statout = aps["statout"]

    es = contextlib.ExitStack()
    with es:
        sb = lambda name, shape, dt: es.enter_context(
            nc.sbuf_tensor(pfx + name, shape, dt))
        wq_sb = sb("wq_sb", [128, KC, FQ], BF16)
        xbuf = sb("xbuf", [128, 2, D], F32)
        junk = sb("junk", [128, 2, D], F32)
        hsb = sb("hsb", [128, 2, D], BF16)
        hsT = sb("hsT", [128, KC, 128], BF16)
        small = sb("small", [128, 32], F32)
        eps_t = sb("eps_t", [128, 1], F32)
        identb_sb = sb("identb_sb", [128, 128], BF16)
        bq_sb = sb("bq_sb", [128, 16], F32)
        qjunk = sb("qjunk", [128, 128], F32)
        acc = sb("acc", [128, 8], F32)
        wstg = sb("wstg", [128, 2, FQ], F32)

        tp_ps = es.enter_context(nc.psum_tensor(pfx + "tp_ps", [128, 2, 4, 256], BF16))
        q_ps = es.enter_context(nc.psum_tensor(pfx + "q_ps", [128, 2, 512], F32))

        ldx = [Sem(es_top, nc, pfx + "ldx0"), Sem(es_top, nc, pfx + "ldx1")]
        wlp = [Sem(es_top, nc, pfx + "wl0"), Sem(es_top, nc, pfx + "wl1")]
        ldc = Sem(es_top, nc, pfx + "ldc")
        stx = Sem(es_top, nc, pfx + "stx")
        pe = Sem(es_top, nc, pfx + "pe")
        sv = Sem(es_top, nc, pfx + "sv")
        ss = Sem(es_top, nc, pfx + "ss")
        if bar:
            def fbar(eng):
                for _s, _c in bar:
                    _s.wait(eng, _c)
            blk.sync(fbar)
            blk.scalar(fbar)
            blk.vector(fbar)
            blk.tensor(fbar)

        def f(sync):
            ldc.inc(sync.dma_start(out=identb_sb[:], in_=identb), 16)
            ldc.inc(sync.dma_start(out=bq_sb[:], in_=bq), 16)
        blk.sync(f)
        ldc_n = ldc.n

        def f(v):
            v.memset(eps_t[:], 1e-5)
            v.memset(acc[:], 0.0)
        blk.vector(f)

        wmarks = [0, 0]
        for kc in range(KC):
            p = kc % 2
            war = wmarks[kc]

            def f(sync, kc=kc, p=p, war=war):
                ss.wait(sync, war)
                wlp[p].inc(sync.dma_start(out=wstg[:, p, :],
                                          in_=wq[kc * 128:(kc + 1) * 128, :]), 16)
            blk.sync(f)
            dm = wlp[p].n

            def f(s, kc=kc, p=p, dm=dm):
                wlp[p].wait(s, dm)
                ss.inc(s.activation(out=wq_sb[:, kc, :], in_=wstg[:, p, :],
                                    func=AF.Identity))
            blk.scalar(f)
            wmarks.append(ss.n)
        wq_done = ss.n

        stat_marks = [0, 0]
        pe_fin = [0] * TT
        s_fin = [0] * TT
        v_fin = [0] * TT
        for k in range(TT):
            slot = k % 2
            SB = 8 * slot

            def f(sync, k=k, slot=slot):
                if k >= 2:
                    sv.wait(sync, v_fin[k - 2])
                ldx[slot].inc(sync.dma_start(out=xbuf[:, slot, :],
                                             in_=xin[k * 128:(k + 1) * 128, :]), 16)
            blk.sync(f)
            ld_k = ldx[slot].n

            def f(v, k=k, slot=slot, ld_k=ld_k, SB=SB):
                ldx[slot].wait(v, ld_k)
                sv.inc(v.tensor_reduce(out=small[:, SB:SB + 1],
                                       in_=xbuf[:, slot, :], axis=AX, op=OP.add))
            blk.vector(f)
            m1 = sv.n

            def f(s, m1=m1, SB=SB):
                sv.wait(s, m1)
                ss.inc(s.activation(out=small[:, SB + 4:SB + 5],
                                    in_=small[:, SB:SB + 1],
                                    func=AF.Identity, scale=1.0 / D))
            blk.scalar(f)
            m2 = ss.n

            def f(v, k=k, slot=slot, m2=m2, SB=SB):
                ss.wait(v, m2)
                sv.inc(v.tensor_scalar(out=junk[:, slot, :], in0=xbuf[:, slot, :],
                                       scalar1=small[:, SB + 4:SB + 5],
                                       scalar2=None, op0=OP.subtract))
            blk.vector(f)
            m3 = sv.n

            def f(s, slot=slot, m3=m3, SB=SB):
                sv.wait(s, m3)
                ss.inc(s.activation(out=hsb[:, slot, :], in_=junk[:, slot, :],
                                    func=AF.Square, accum_out=small[:, SB + 5:SB + 6]))
                ss.wait(s, ss.n)
                ss.inc(s.activation(out=small[:, SB + 7:SB + 8],
                                    in_=small[:, SB + 5:SB + 6],
                                    func=AF.Ln, scale=1.0 / D,
                                    bias=eps_t[:, 0:1]))
                ss.wait(s, ss.n)
                ss.inc(s.activation(out=small[:, SB + 6:SB + 7],
                                    in_=small[:, SB + 7:SB + 8],
                                    func=AF.Exp, scale=-0.5))
            blk.scalar(f)
            m4 = ss.n

            def f(v, k=k, slot=slot, m4=m4, SB=SB,
                  war=(pe_fin[k - 2] if k >= 2 else 0)):
                ss.wait(v, m4)
                pe.wait(v, war)
                sv.inc(v.tensor_scalar(out=hsb[:, slot, :], in0=junk[:, slot, :],
                                       scalar1=small[:, SB + 6:SB + 7],
                                       scalar2=None, op0=OP.mult))
            blk.vector(f)
            v_fin[k] = sv.n

            def f(t, k=k, slot=slot, war=(s_fin[k - 1] if k >= 1 else 0)):
                sv.wait(t, v_fin[k])
                ss.wait(t, war)
                for c in range(KC):
                    pe.inc(t.transpose(tp_ps[:, c // 4, c % 4, 0:128],
                                       hsb[:, slot, c * 128:(c + 1) * 128],
                                       identb_sb[:]))
            blk.tensor(f)
            pe_tr = pe.n

            def f(s, k=k, pe_tr=pe_tr):
                pe.wait(s, pe_tr)
                for c in range(KC):
                    ss.inc(s.activation(out=hsT[:, c, :],
                                        in_=tp_ps[:, c // 4, c % 4, 0:128],
                                        func=AF.Identity))
            blk.scalar(f)
            hsT_m = ss.n
            s_fin[k] = ss.n

            for ft in range(16):
                bank = ft % 2

                def f(t, ft=ft, bank=bank, hsT_m=hsT_m, wq_done=wq_done,
                      war=stat_marks[-1]):
                    ss.wait(t, max(hsT_m, wq_done))
                    sv.wait(t, war)
                    instr = None
                    for c in range(KC):
                        instr = t.matmul(q_ps[:, bank, 0:128], wq_sb[:, c,
                                         ft * 128:(ft + 1) * 128],
                                         hsT[:, c, :],
                                         start=(c == 0), stop=(c == KC - 1))
                    pe.inc(instr)
                blk.tensor(f)
                qm = pe.n

                def f(s, ft=ft, bank=bank, qm=qm):
                    pe.wait(s, qm)
                    ss.inc(s.activation(out=qjunk[:], in_=q_ps[:, bank, 0:128],
                                        func=AF.Identity, bias=bq_sb[:, ft:ft + 1],
                                        accum_out=acc[:, 4:5]))
                    ss.wait(s, ss.n)
                    ss.inc(s.activation(out=qjunk[:], in_=q_ps[:, bank, 0:128],
                                        func=AF.Square, bias=bq_sb[:, ft:ft + 1],
                                        accum_out=acc[:, 5:6]))
                blk.scalar(f)
                sacc = ss.n

                def f(v, ft=ft, sacc=sacc):
                    ss.wait(v, sacc)
                    half = ft % 2
                    sv.inc(v.tensor_tensor(out=acc[:, half:half + 1],
                                           in0=acc[:, half:half + 1],
                                           in1=acc[:, 4:5], op=OP.add))
                    sv.inc(v.tensor_tensor(out=acc[:, 2 + half:3 + half],
                                           in0=acc[:, 2 + half:3 + half],
                                           in1=acc[:, 5:6], op=OP.add))
                    sv.wait(v, sv.n)
                blk.vector(f)
                stat_marks.append(sv.n)

        def f(sync):
            sv.wait(sync, sv.n)
            stx.inc(sync.dma_start(out=statout, in_=acc[:, 0:4]), 16)
        blk.sync(f)

        if final:
            def f(g):
                stx.wait(g, 16)
            blk.gpsimd(f)
    return [(s, s.n) for s in (ldx[0], ldx[1], wlp[0], wlp[1], ldc, stx,
                               pe, sv, ss)]


def run_l1_stats(x_full, Wq, bq_):
    import ml_dtypes
    nc = build_l1_stats()
    bqr = np.ascontiguousarray(np.asarray(bq_, np.float32).reshape(16, 128).T)
    shared = dict(wq=np.ascontiguousarray(Wq, np.float32), bq=bqr,
                  identb=np.eye(128, dtype=ml_dtypes.bfloat16))
    in_maps = []
    for b in range(NCORES):
        m = dict(shared)
        m["xin"] = np.ascontiguousarray(x_full[b], np.float32)
        in_maps.append(m)
    res = run_bass_kernel_spmd(nc, in_maps, core_ids=list(range(NCORES)))
    return np.stack([r["statout"] for r in res.results], axis=0)





def build_l1_pkm():
    FQ = 2048
    E = 1024
    nc = bass.Bass("TRN2", target_bir_lowering=False)
    xin = nc.dram_tensor("xin", [T, D], F32, kind="ExternalInput").ap()
    stats = nc.dram_tensor("stats", [128, NCORES, 4], F32, kind="ExternalInput").ap()
    wq = nc.dram_tensor("wq", [D, FQ], F32, kind="ExternalInput").ap()
    bq = nc.dram_tensor("bq", [128, 16], F32, kind="ExternalInput").ap()
    bng = nc.dram_tensor("bng", [128, 2], F32, kind="ExternalInput").ap()
    bnb = nc.dram_tensor("bnb", [128, 2], F32, kind="ExternalInput").ap()
    sk0 = nc.dram_tensor("sk0", [32, 128], F32, kind="ExternalInput").ap()
    sk1 = nc.dram_tensor("sk1", [32, 128], F32, kind="ExternalInput").ap()
    edT = nc.dram_tensor("edT", [D, E], F32, kind="ExternalInput").ap()
    eu = nc.dram_tensor("eu", [E, D], F32, kind="ExternalInput").ap()
    iota32 = nc.dram_tensor("iota32", [128, 32], F32, kind="ExternalInput").ap()
    identb = nc.dram_tensor("identb", [128, 128], BF16, kind="ExternalInput").ap()
    xout = nc.dram_tensor("xout", [T, D], F32, kind="ExternalOutput").ap()

    es = contextlib.ExitStack()
    with es:
        sb = lambda name, shape, dt: es.enter_context(nc.sbuf_tensor(name, shape, dt))
        wq_sb = sb("wq_sb", [128, KC, FQ], BF16)
        ed_sb = sb("ed_sb", [128, KC, E], BF16)
        eu_sb = sb("eu_sb", [128, KC, D], BF16)
        kb = sb("kb", [128, 2, 33], BF16)
        xbuf = sb("xbuf", [128, 2, D], F32)
        junk = sb("junk", [128, 2, D], F32)
        hsb = sb("hsb", [128, 2, D], BF16)
        hsT = sb("hsT", [128, KC, 128], BF16)
        qhat = sb("qhat", [128, 16, 128], BF16)
        sqh = sb("sqh", [128, 16, 128], BF16)
        s0n = sb("s0n", [128, 16, 32], F32)
        tsv = sb("tsv", [128, 16, 8], F32)
        tif = sb("tif", [128, 16, 8], F32)
        tiu = sb("tiu", [128, 16, 8], U32)
        eg = sb("eg", [128, 2, 32], F32)
        egs = sb("egs", [128, 8, 2, 32], F32)
        nmx = sb("nmx", [128, 16], F32)
        m0t = sb("m0t", [128, 32], F32)
        m0t2 = sb("m0t2", [128, E], F32)
        gsum = sb("gsum", [128, E], F32)
        gf = sb("gf", [128, E], F32)
        ggf = sb("ggf", [128, E], BF16)
        ggfT = sb("ggfT", [128, KC, 128], BF16)
        small = sb("small", [128, 32], F32)
        bstat = sb("bstat", [128, 16], F32)
        coefs = sb("coefs", [128, 2, 16], F32)
        eps_t = sb("eps_t", [128, 1], F32)
        e18_t = sb("e18_t", [128, 1], F32)
        identb_sb = sb("identb_sb", [128, 128], BF16)
        bq_sb = sb("bq_sb", [128, 16], F32)
        bng_sb = sb("bng_sb", [128, 2], F32)
        bnb_sb = sb("bnb_sb", [128, 2], F32)
        iota_sb = sb("iota_sb", [128, 32], F32)
        stats_sb = sb("stats_sb", [128, NCORES, 4], F32)
        keyst = sb("keyst", [32, 128], F32)
        keyn = sb("keyn", [32, 4], F32)
        wstg = sb("wstg", [128, 2, FQ], F32)
        obuf = sb("obuf", [128, 2, D], F32)
        xrbuf = sb("xrbuf", [128, 2, D], F32)

        tp_ps = es.enter_context(nc.psum_tensor("tp_ps", [128, 2, 4, 256], BF16))
        q_ps = es.enter_context(nc.psum_tensor("q_ps", [128, 2, 512], F32))
        sc_ps = es.enter_context(nc.psum_tensor("sc_ps", [128, 2, 512], F32))
        fd_ps = es.enter_context(nc.psum_tensor("fd_ps", [128, 2, 512], F32))
        ou_ps = fd_ps

        ldx = [Sem(es, nc, "ldx0"), Sem(es, nc, "ldx1")]
        stx = [Sem(es, nc, "stx0"), Sem(es, nc, "stx1")]
        wlp = [Sem(es, nc, "wl0"), Sem(es, nc, "wl1")]
        ldc = Sem(es, nc, "ldc")
        pe = Sem(es, nc, "pe")
        sv = Sem(es, nc, "sv")
        ss = Sem(es, nc, "ss")
        blk = es.enter_context(nc.Block())

        def f(sync):
            for o, i in [(identb_sb[:], identb), (bq_sb[:], bq),
                         (bng_sb[:], bng), (bnb_sb[:], bnb),
                         (iota_sb[:], iota32), (stats_sb[:], stats),
                         (keyst[:], sk0)]:
                ldc.inc(sync.dma_start(out=o, in_=i), 16)
        blk.sync(f)
        ldc_n = ldc.n

        def f(v):
            v.memset(eps_t[:], 1e-5)
            v.memset(e18_t[:], 1e-18)
        blk.vector(f)

        def f(v):
            ldc.wait(v, ldc_n)
            sv.inc(v.tensor_reduce(out=bstat[:, 0:4],
                                   in_=stats_sb[:].rearrange("p c s -> p s c"),
                                   axis=AX, op=OP.add))
            sv.wait(v, sv.n)
            CNT = 1.0 / (NCORES * T * 8)
            sv.inc(v.tensor_scalar(out=bstat[:, 4:8], in0=bstat[:, 0:4],
                                   scalar1=CNT, scalar2=None, op0=OP.mult))
            sv.wait(v, sv.n)
            sv.inc(v.tensor_tensor(out=bstat[:, 8:10], in0=bstat[:, 4:6],
                                   in1=bstat[:, 4:6], op=OP.mult))
            sv.wait(v, sv.n)
            sv.inc(v.tensor_tensor(out=bstat[:, 10:12], in0=bstat[:, 6:8],
                                   in1=bstat[:, 8:10], op=OP.subtract))
            sv.wait(v, sv.n)
        blk.vector(f)
        bn1 = sv.n

        def f(s, bn1=bn1):
            sv.wait(s, bn1)
            ss.inc(s.activation(out=bstat[:, 12:14], in_=bstat[:, 10:12],
                                func=AF.Ln, bias=eps_t[:, 0:1]))
            ss.wait(s, ss.n)
            ss.inc(s.activation(out=bstat[:, 0:2], in_=bstat[:, 12:14],
                                func=AF.Exp, scale=-0.5))
        blk.scalar(f)
        bn2 = ss.n

        def f(v, bn2=bn2):
            ss.wait(v, bn2)
            sv.inc(v.tensor_tensor(out=coefs[:, 0, 0:2], in0=bstat[:, 0:2],
                                   in1=bng_sb[:], op=OP.mult))
            sv.wait(v, sv.n)
            sv.inc(v.tensor_tensor(out=bstat[:, 2:4], in0=bstat[:, 4:6],
                                   in1=coefs[:, 0, 0:2], op=OP.mult))
            sv.wait(v, sv.n)
            sv.inc(v.tensor_tensor(out=bstat[:, 2:4], in0=bnb_sb[:],
                                   in1=bstat[:, 2:4], op=OP.subtract))
            sv.wait(v, sv.n)
            for ft in range(16):
                hf = ft % 2
                sv.inc(v.scalar_tensor_tensor(
                    out=coefs[:, 1, ft:ft + 1], in0=bq_sb[:, ft:ft + 1],
                    scalar=1.0, in1=coefs[:, 0, hf:hf + 1],
                    op0=OP.mult, op1=OP.mult))
            sv.wait(v, sv.n)
            for ft in range(16):
                hf = ft % 2
                sv.inc(v.tensor_tensor(out=coefs[:, 1, ft:ft + 1],
                                       in0=coefs[:, 1, ft:ft + 1],
                                       in1=bstat[:, 2 + hf:3 + hf], op=OP.add))
            sv.wait(v, sv.n)
        blk.vector(f)
        bn_done = sv.n

        key_done = []
        for hf in range(2):
            if hf == 1:
                def f(sync, m=key_done[-1]):
                    ss.wait(sync, m)
                    ldc.inc(sync.dma_start(out=keyst[:], in_=sk1), 16)
                blk.sync(f)
            ldck = ldc.n

            def f(s, hf=hf, ldck=ldck):
                ldc.wait(s, ldck)
                ss.inc(s.activation(out=junk[0:32, 0, 0:128],
                                    in_=keyst[0:32, :], func=AF.Square,
                                    accum_out=keyn[0:32, 1:2]))
            blk.scalar(f)
            k1 = ss.n

            def f(s2, k1=k1):
                ss.wait(s2, k1)
                ss.inc(s2.activation(out=keyn[0:32, 2:3], in_=keyn[0:32, 1:2],
                                     func=AF.Ln, bias=e18_t[0:32, 0:1]))
                ss.wait(s2, ss.n)
                ss.inc(s2.activation(out=keyn[0:32, 3:4], in_=keyn[0:32, 2:3],
                                     func=AF.Exp, scale=-0.5))
            blk.scalar(f)
            k2 = ss.n

            def f(v, hf=hf, k2=k2):
                ss.wait(v, k2)
                sv.inc(v.tensor_scalar(out=keyst[0:32, :], in0=keyst[0:32, :],
                                       scalar1=keyn[0:32, 3:4], scalar2=None,
                                       op0=OP.mult))
            blk.vector(f)
            k3 = sv.n

            def f(v, k3=k3):
                sv.wait(v, k3)
                sv.inc(v.tensor_copy(hsb[0:32, 0, 0:128], keyst[0:32, :]))
            blk.vector(f)
            k4 = sv.n

            def f(t, k4=k4):
                sv.wait(t, k4)
                pe.inc(t.transpose(tp_ps[:, 0, 0, 0:32], hsb[0:32, 0, 0:128],
                                   identb_sb[0:32, 0:32]))
            blk.tensor(f)
            k5 = pe.n

            def f(s, hf=hf, k5=k5):
                pe.wait(s, k5)
                ss.inc(s.activation(out=kb[:, hf, 0:32], in_=tp_ps[:, 0, 0, 0:32],
                                    func=AF.Identity))
            blk.scalar(f)
            key_done.append(ss.n)

        def f(v, m=key_done[-1]):
            ss.wait(v, m)
            v.memset(kb[:, 0, 32:33], 1.0)
            sv.inc(v.memset(kb[:, 1, 32:33], 1.0))
        blk.vector(f)
        keys_done_v = sv.n
        keys_done_s = key_done[-1]

        wmarks = [0, 0]
        wsrcs = ([(wq[kc * 128:(kc + 1) * 128, :], wq_sb[:, kc, 0:FQ], FQ)
                  for kc in range(KC)] +
                 [(edT[kc * 128:(kc + 1) * 128, :], ed_sb[:, kc, 0:E], E)
                  for kc in range(KC)] +
                 [(eu[kc * 128:(kc + 1) * 128, :], eu_sb[:, kc, 0:D], D)
                  for kc in range(KC)])
        for i, (src_ap, dst_ap, F) in enumerate(wsrcs):
            p = i % 2
            war = wmarks[i]

            def f(sync, src_ap=src_ap, p=p, war=war, F=F):
                ss.wait(sync, war)
                wlp[p].inc(sync.dma_start(out=wstg[:, p, 0:F], in_=src_ap), 16)
            blk.sync(f)
            dm = wlp[p].n

            def f(s, dst_ap=dst_ap, p=p, dm=dm, F=F):
                wlp[p].wait(s, dm)
                ss.inc(s.activation(out=dst_ap, in_=wstg[:, p, 0:F],
                                    func=AF.Identity))
            blk.scalar(f)
            wmarks.append(ss.n)
        w_done = ss.n

        v_fin = [0] * TT
        pe_fin = [0] * TT
        s_fin = [0] * TT
        v_out = [0] * TT
        prev_q_ev = 0
        prev_sc_v = 0
        prev_fd_ev = 0
        prev_eg_war = 0
        prev_ou_v = 0
        for k in range(TT):
            slot = k % 2
            SB = 8 * slot

            def f(sync, k=k, slot=slot):
                if k >= 2:
                    sv.wait(sync, v_fin[k - 2])
                ldx[slot].inc(sync.dma_start(out=xbuf[:, slot, :],
                                             in_=xin[k * 128:(k + 1) * 128, :]), 16)
            blk.sync(f)
            ld_k = ldx[slot].n

            def f(v, k=k, slot=slot, ld_k=ld_k, SB=SB):
                ldx[slot].wait(v, ld_k)
                sv.inc(v.tensor_reduce(out=small[:, SB:SB + 1],
                                       in_=xbuf[:, slot, :], axis=AX, op=OP.add))
            blk.vector(f)
            m1 = sv.n

            def f(s, m1=m1, SB=SB):
                sv.wait(s, m1)
                ss.inc(s.activation(out=small[:, SB + 4:SB + 5],
                                    in_=small[:, SB:SB + 1],
                                    func=AF.Identity, scale=1.0 / D))
            blk.scalar(f)
            m2 = ss.n

            def f(v, k=k, slot=slot, m2=m2, SB=SB):
                ss.wait(v, m2)
                sv.inc(v.tensor_scalar(out=junk[:, slot, :], in0=xbuf[:, slot, :],
                                       scalar1=small[:, SB + 4:SB + 5],
                                       scalar2=None, op0=OP.subtract))
            blk.vector(f)
            m3 = sv.n

            def f(s, slot=slot, m3=m3, SB=SB):
                sv.wait(s, m3)
                ss.inc(s.activation(out=hsb[:, slot, :], in_=junk[:, slot, :],
                                    func=AF.Square, accum_out=small[:, SB + 5:SB + 6]))
                ss.wait(s, ss.n)
                ss.inc(s.activation(out=small[:, SB + 7:SB + 8],
                                    in_=small[:, SB + 5:SB + 6],
                                    func=AF.Ln, scale=1.0 / D,
                                    bias=eps_t[:, 0:1]))
                ss.wait(s, ss.n)
                ss.inc(s.activation(out=small[:, SB + 6:SB + 7],
                                    in_=small[:, SB + 7:SB + 8],
                                    func=AF.Exp, scale=-0.5))
            blk.scalar(f)
            m4 = ss.n

            def f(v, k=k, slot=slot, m4=m4, SB=SB,
                  war=(pe_fin[k - 2] if k >= 2 else 0), kdv=keys_done_v):
                ss.wait(v, m4)
                sv.wait(v, kdv)
                pe.wait(v, war)
                sv.inc(v.tensor_scalar(out=hsb[:, slot, :], in0=junk[:, slot, :],
                                       scalar1=small[:, SB + 6:SB + 7],
                                       scalar2=None, op0=OP.mult))
            blk.vector(f)
            v_fin[k] = sv.n

            def f(t, k=k, slot=slot, war=(s_fin[k - 1] if k >= 1 else 0)):
                sv.wait(t, v_fin[k])
                ss.wait(t, war)
                for c in range(KC):
                    pe.inc(t.transpose(tp_ps[:, c // 4, c % 4, 0:128],
                                       hsb[:, slot, c * 128:(c + 1) * 128],
                                       identb_sb[:]))
            blk.tensor(f)
            tr_m = pe.n
            pe_fin[k] = pe.n

            def f(s, k=k, tr_m=tr_m):
                pe.wait(s, tr_m)
                for c in range(KC):
                    ss.inc(s.activation(out=hsT[:, c, :],
                                        in_=tp_ps[:, c // 4, c % 4, 0:128],
                                        func=AF.Identity))
            blk.scalar(f)
            hsT_m = ss.n
            s_fin[k] = ss.n

            q_evs = []
            for f0 in range(0, 16, 2):
                qms = []
                for ft in (f0, f0 + 1):
                    bank = ft % 2

                    def f(t, ft=ft, bank=bank, hsT_m=hsT_m, wd=w_done,
                          war=(q_evs[ft - 2] if ft >= 2 else prev_q_ev)):
                        ss.wait(t, max(hsT_m, wd, war))
                        instr = None
                        for c in range(KC):
                            instr = t.matmul(q_ps[:, bank, 0:128],
                                             wq_sb[:, c, ft * 128:(ft + 1) * 128],
                                             hsT[:, c, :],
                                             start=(c == 0), stop=(c == KC - 1))
                        pe.inc(instr)
                    blk.tensor(f)
                    qms.append(pe.n)

                def f(s, f0=f0, qms=tuple(qms), bnd=bn_done):
                    sv.wait(s, bnd)
                    for ft, qm in zip((f0, f0 + 1), qms):
                        bank = ft % 2
                        hf = ft % 2
                        pe.wait(s, qm)
                        ss.inc(s.activation(out=qhat[:, ft, :],
                                            in_=q_ps[:, bank, 0:128],
                                            func=AF.Identity,
                                            scale=coefs[:, 0, hf:hf + 1],
                                            bias=coefs[:, 1, ft:ft + 1]))
                        ss.wait(s, ss.n)
                        ss.inc(s.activation(out=sqh[:, ft, :],
                                            in_=qhat[:, ft, :],
                                            func=AF.Square))
                        q_evs.append(ss.n)
                blk.scalar(f)
            prev_q_ev = ss.n
            qhat_m = ss.n

            sc_vs = []
            for f0 in range(0, 16, 2):
                scms = []
                for ft in (f0, f0 + 1):
                    def f(t, ft=ft, qhat_m=qhat_m, ksd=keys_done_s,
                          war=(sc_vs[ft - 2] if ft >= 2 else prev_sc_v)):
                        ss.wait(t, max(qhat_m, ksd))
                        sv.wait(t, war)
                        hf = ft % 2
                        bk = ft % 2
                        r2 = ((ft // 2) % 2) * 256
                        pe.inc(t.matmul(sc_ps[:, bk, r2:r2 + 33],
                                        qhat[:, ft, :],
                                        kb[:, hf, 0:33], start=True, stop=True))
                        pe.inc(t.matmul(sc_ps[:, bk, r2 + 64:r2 + 65],
                                        sqh[:, ft, :],
                                        kb[:, hf, 32:33], start=True, stop=True))
                    blk.tensor(f)
                    scms.append(pe.n)

                def f(s, f0=f0, scms=tuple(scms)):
                    for ft, scm in zip((f0, f0 + 1), scms):
                        pe.wait(s, scm)
                        bk = ft % 2
                        r2 = ((ft // 2) % 2) * 256
                        c0 = 16 + (ft % 2) * 2
                        ss.inc(s.activation(out=small[:, c0:c0 + 1],
                                            in_=sc_ps[:, bk, r2 + 64:r2 + 65],
                                            func=AF.Ln, bias=e18_t[:, 0:1]))
                        ss.wait(s, ss.n)
                        ss.inc(s.activation(out=small[:, c0 + 1:c0 + 2],
                                            in_=small[:, c0:c0 + 1],
                                            func=AF.Exp, scale=-0.5))
                blk.scalar(f)
                nsq = ss.n

                def f(v, f0=f0, nsq=nsq):
                    ss.wait(v, nsq)
                    for ft in (f0, f0 + 1):
                        bk = ft % 2
                        r2 = ((ft // 2) % 2) * 256
                        c0 = 16 + (ft % 2) * 2
                        sv.inc(v.tensor_scalar(out=s0n[:, ft, :],
                                               in0=sc_ps[:, bk, r2:r2 + 32],
                                               scalar1=small[:, c0 + 1:c0 + 2],
                                               scalar2=None, op0=OP.mult))
                        sv.wait(v, sv.n)
                        sv.inc(v.max(out=tsv[:, ft, :], in_=s0n[:, ft, :]))
                        sv.wait(v, sv.n)
                        sv.inc(v.max_index(out=tiu[:, ft, :],
                                           in_max=tsv[:, ft, :],
                                           in_values=s0n[:, ft, :]))
                        sv.wait(v, sv.n)
                        sv.inc(v.tensor_scalar(out=tif[:, ft, 0:4],
                                               in0=tiu[:, ft, 0:4], scalar1=1.0,
                                               scalar2=None, op0=OP.mult))
                        sv.wait(v, sv.n)
                        sc_vs.append(sv.n)
                blk.vector(f)
            prev_sc_v = sv.n
            top_m = sv.n

            def f(t, k=k, hsT_m=hsT_m, wd=w_done, war=prev_fd_ev,
                  war_v=prev_ou_v):
                ss.wait(t, max(hsT_m, wd, war))
                sv.wait(t, war_v)
                for sp in range(2):
                    instr = None
                    for c in range(KC):
                        instr = t.matmul(fd_ps[:, sp, :], hsT[:, c, :],
                                         ed_sb[:, c, sp * 512:(sp + 1) * 512],
                                         start=(c == 0), stop=(c == KC - 1))
                    pe.inc(instr)
            blk.tensor(f)
            fdm = pe.n

            def f(s, fdm=fdm):
                pe.wait(s, fdm)
                for sp in range(2):
                    ss.inc(s.activation(out=gf[:, sp * 512:(sp + 1) * 512],
                                        in_=fd_ps[:, sp, :], func=AF.Gelu))
            blk.scalar(f)
            prev_fd_ev = ss.n
            gf_m = ss.n

            def f(v, top_m=top_m):
                sv.wait(v, top_m)
                sv.inc(v.tensor_scalar(out=nmx[:].unsqueeze(2),
                                       in0=tsv[:, :, 0:1], scalar1=-1.0,
                                       scalar2=None, op0=OP.mult))
                sv.wait(v, sv.n)
            blk.vector(f)
            nm_all = sv.n

            e_ms = []
            for h0 in range(0, 8, 4):
                def f(s, h0=h0, nm_all=nm_all, war=prev_eg_war):
                    sv.wait(s, nm_all)
                    sv.wait(s, war)
                    for h in range(h0, h0 + 4):
                        for hf in range(2):
                            ft = 2 * h + hf
                            ss.inc(s.activation(out=egs[:, h, hf, :],
                                                in_=s0n[:, ft, :],
                                                func=AF.Exp,
                                                bias=nmx[:, ft:ft + 1]))
                blk.scalar(f)
                e_ms.append(ss.n)

            gacc = 0
            for h in range(8):
                ft0, ft1 = 2 * h, 2 * h + 1
                e_m = e_ms[h // 4]

                def f(v, h=h, ft0=ft0, ft1=ft1, e_m=e_m, gacc=gacc):
                    ss.wait(v, e_m)
                    for hf, ft in ((0, ft0), (1, ft1)):
                        sv.inc(v.tensor_scalar(out=m0t[:, 0:32],
                                               in0=iota_sb[:, :],
                                               scalar1=tif[:, ft, 0:1],
                                               scalar2=None, op0=OP.is_equal))
                        sv.wait(v, sv.n)
                        for kk in range(1, 4):
                            sv.inc(v.scalar_tensor_tensor(
                                out=m0t[:, 0:32], in0=iota_sb[:, :],
                                scalar=tif[:, ft, kk:kk + 1], in1=m0t[:, 0:32],
                                op0=OP.is_equal, op1=OP.add))
                            sv.wait(v, sv.n)
                        sv.inc(v.tensor_tensor(out=eg[:, hf, :],
                                               in0=egs[:, h, hf, :],
                                               in1=m0t[:, 0:32], op=OP.mult))
                        sv.wait(v, sv.n)
                        sv.inc(v.tensor_reduce(out=small[:, 22 + hf:23 + hf],
                                               in_=eg[:, hf, :], axis=AX,
                                               op=OP.add))
                        sv.wait(v, sv.n)
                    sv.inc(v.tensor_tensor(out=small[:, 24:25],
                                           in0=small[:, 22:23],
                                           in1=small[:, 23:24], op=OP.mult))
                    sv.wait(v, sv.n)
                    sv.inc(v.reciprocal(out=small[:, 25:26], in_=small[:, 24:25]))
                    sv.wait(v, sv.n)
                    sv.inc(v.tensor_scalar(out=eg[:, 0, :], in0=eg[:, 0, :],
                                           scalar1=small[:, 25:26], scalar2=None,
                                           op0=OP.mult))
                    sv.wait(v, sv.n)
                    g0v = eg[:, 0, :].unsqueeze(2).to_broadcast([128, 32, 32])
                    g1v = eg[:, 1, :].unsqueeze(1).to_broadcast([128, 32, 32])
                    gv = gsum[:].rearrange("p (a b) -> p a b", b=32)
                    if h == 0:
                        sv.inc(v.tensor_tensor(out=gv, in0=g0v, in1=g1v,
                                               op=OP.mult))
                    else:
                        sv.inc(v.tensor_tensor(out=m0t2[:].rearrange(
                            "p (a b) -> p a b", b=32), in0=g0v, in1=g1v,
                            op=OP.mult))
                        sv.wait(v, sv.n)
                        sv.inc(v.tensor_tensor(out=gsum[:], in0=gsum[:],
                                               in1=m0t2[:], op=OP.add))
                    sv.wait(v, sv.n)
                blk.vector(f)
                gacc = sv.n
            gsum_m = sv.n
            prev_eg_war = sv.n

            def f(v, gsum_m=gsum_m, gf_m=gf_m):
                sv.wait(v, gsum_m)
                ss.wait(v, gf_m)
                sv.inc(v.tensor_tensor(out=ggf[:], in0=gsum[:], in1=gf[:],
                                       op=OP.mult))
            blk.vector(f)
            ggf_m = sv.n

            def f(t, k=k, ggf_m=ggf_m):
                sv.wait(t, ggf_m)
                ss.wait(t, hsT_m)
                for c in range(KC):
                    pe.inc(t.transpose(tp_ps[:, c // 4, c % 4, 0:128],
                                       ggf[:, c * 128:(c + 1) * 128],
                                       identb_sb[:]))
            blk.tensor(f)
            gt_m = pe.n

            def f(s, gt_m=gt_m):
                pe.wait(s, gt_m)
                for c in range(KC):
                    ss.inc(s.activation(out=ggfT[:, c, :],
                                        in_=tp_ps[:, c // 4, c % 4, 0:128],
                                        func=AF.Identity))
            blk.scalar(f)
            ggfT_m = ss.n

            def f(sync, k=k, slot=slot, war=(v_out[k - 2] if k >= 2 else 0)):
                sv.wait(sync, war)
                ldx[slot].inc(sync.dma_start(out=xrbuf[:, slot, :],
                                             in_=xin[k * 128:(k + 1) * 128, :]), 16)
            blk.sync(f)
            ldr = ldx[slot].n

            def f(t, k=k, ggfT_m=ggfT_m, wd=w_done, war=prev_ou_v):
                ss.wait(t, max(ggfT_m, wd))
                sv.wait(t, war)
                for sp in range(2):
                    instr = None
                    for c in range(KC):
                        instr = t.matmul(ou_ps[:, sp, :], ggfT[:, c, :],
                                         eu_sb[:, c, sp * 512:(sp + 1) * 512],
                                         start=(c == 0), stop=(c == KC - 1))
                    pe.inc(instr)
            blk.tensor(f)
            oum = pe.n

            def f(v, k=k, slot=slot, oum=oum, ldr=ldr):
                pe.wait(v, oum)
                ldx[slot].wait(v, ldr)
                if k >= 2:
                    stx[slot].wait(v, 16 * ((k - 2) // 2 + 1))
                for sp in range(2):
                    sv.inc(v.tensor_tensor(
                        out=obuf[:, slot, sp * 512:(sp + 1) * 512],
                        in0=xrbuf[:, slot, sp * 512:(sp + 1) * 512],
                        in1=ou_ps[:, sp, :], op=OP.add))
            blk.vector(f)
            v_out[k] = sv.n
            prev_ou_v = sv.n

            def f(sync, k=k, slot=slot):
                sv.wait(sync, v_out[k])
                stx[slot].inc(sync.dma_start(out=xout[k * 128:(k + 1) * 128, :],
                                             in_=obuf[:, slot, :]), 16)
            blk.sync(f)

        def f(g):
            stx[0].wait(g, stx[0].n)
            stx[1].wait(g, stx[1].n)
        blk.gpsimd(f)
    return nc


def run_l1_pkm(x_full, stats_all, Wq, bq_, bng_, bnb_, sk0_, sk1_, ed_, eu_):
    import ml_dtypes
    nc = build_l1_pkm()
    shared = dict(
        wq=np.ascontiguousarray(Wq, np.float32),
        bq=np.ascontiguousarray(np.asarray(bq_, np.float32).reshape(16, 128).T),
        bng=np.ascontiguousarray(np.asarray(bng_, np.float32).reshape(2, 128).T),
        bnb=np.ascontiguousarray(np.asarray(bnb_, np.float32).reshape(2, 128).T),
        sk0=np.ascontiguousarray(sk0_, np.float32),
        sk1=np.ascontiguousarray(sk1_, np.float32),
        edT=np.ascontiguousarray(np.asarray(ed_, np.float32).T),
        eu=np.ascontiguousarray(eu_, np.float32),
        iota32=np.ascontiguousarray(
            np.broadcast_to(np.arange(32, dtype=np.float32), (128, 32))),
        identb=np.eye(128, dtype=ml_dtypes.bfloat16),
        stats=np.ascontiguousarray(stats_all.transpose(1, 0, 2), np.float32),
    )
    in_maps = []
    for b in range(NCORES):
        m = dict(shared)
        m["xin"] = np.ascontiguousarray(x_full[b], np.float32)
        in_maps.append(m)
    res = run_bass_kernel_spmd(nc, in_maps, core_ids=list(range(NCORES)))
    return np.stack([r["xout"] for r in res.results], axis=0)





def build_mega():
    NI = 2816
    DI = 2048
    FQ = 2048
    nc = bass.Bass("TRN2", target_bir_lowering=False)
    xin = nc.dram_tensor("xin", [T, D], F32, kind="ExternalInput").ap()
    identb = nc.dram_tensor("identb", [128, 128], BF16, kind="ExternalInput").ap()
    a_wbig = nc.dram_tensor("a_wbig", [D, 3200], F32, kind="ExternalInput").ap()
    a_bbig = nc.dram_tensor("a_bbig", [128, 24], F32, kind="ExternalInput").ap()
    a_wout = nc.dram_tensor("a_wout", [D, D], F32, kind="ExternalInput").ap()
    a_bout = nc.dram_tensor("a_bout", [D], F32, kind="ExternalInput").ap()
    a_pscale = nc.dram_tensor("a_pscale", [16], F32, kind="ExternalInput").ap()
    a_sign48 = nc.dram_tensor("a_sign48", [48], F32, kind="ExternalInput").ap()
    a_maskd = nc.dram_tensor("a_maskd", [128, 128], BF16, kind="ExternalInput").ap()
    b_wg = nc.dram_tensor("b_wg", [NI, D], F32, kind="ExternalInput").ap()
    b_wu = nc.dram_tensor("b_wu", [NI, D], F32, kind="ExternalInput").ap()
    b_wd = nc.dram_tensor("b_wd", [D, NI], F32, kind="ExternalInput").ap()
    c_ipwT = nc.dram_tensor("c_ipwT", [D, 2 * DI], F32, kind="ExternalInput").ap()
    c_lng = nc.dram_tensor("c_lng", [128, KC], F32, kind="ExternalInput").ap()
    c_lnb = nc.dram_tensor("c_lnb", [128, KC], F32, kind="ExternalInput").ap()
    c_convw = nc.dram_tensor("c_convw", [128, 16, 4], F32, kind="ExternalInput").ap()
    c_convb = nc.dram_tensor("c_convb", [128, 16], F32, kind="ExternalInput").ap()
    c_xpwT = nc.dram_tensor("c_xpwT", [128, 16, 96], F32, kind="ExternalInput").ap()
    c_dtpwT = nc.dram_tensor("c_dtpwT", [64, DI], F32, kind="ExternalInput").ap()
    c_dtpb = nc.dram_tensor("c_dtpb", [128, 16], F32, kind="ExternalInput").ap()
    c_alog = nc.dram_tensor("c_alog", [128, 16, 16], F32, kind="ExternalInput").ap()
    c_dpar = nc.dram_tensor("c_dpar", [128, 16], F32, kind="ExternalInput").ap()
    c_opwT = nc.dram_tensor("c_opwT", [DI, D], F32, kind="ExternalInput").ap()
    d_wq = nc.dram_tensor("d_wq", [D, FQ], F32, kind="ExternalInput").ap()
    d_bq = nc.dram_tensor("d_bq", [128, 16], F32, kind="ExternalInput").ap()
    xout = nc.dram_tensor("xout", [T, D], F32, kind="ExternalOutput").ap()
    statout = nc.dram_tensor("statout", [128, 4], F32, kind="ExternalOutput").ap()
    x1d = nc.dram_tensor("x1d", [T, D], F32, kind="Internal").ap()
    x2d = nc.dram_tensor("x2d", [T, D], F32, kind="Internal").ap()

    es_top = contextlib.ExitStack()
    with es_top:
        blk = es_top.enter_context(nc.Block())
        b1 = emit_l0_attn(nc, blk, es_top,
                          dict(xin=xin, wbig=a_wbig, bbig=a_bbig, wout=a_wout,
                               bout=a_bout, pscale=a_pscale, sign48=a_sign48,
                               identb=identb, maskd=a_maskd, xout=x1d),
                          pfx="a_", bar=None, final=False)
        b2 = emit_l0_ffn(nc, blk, es_top,
                         dict(xin=x1d, wg=b_wg, wu=b_wu, wd=b_wd,
                              identb=identb, xout=x2d),
                         pfx="b_", bar=b1, final=False)
        b3 = emit_l1_mamba(nc, blk, es_top,
                           dict(xin=x2d, ipwT=c_ipwT, lng=c_lng, lnb=c_lnb,
                                convw=c_convw, convb=c_convb, xpwT=c_xpwT,
                                dtpwT=c_dtpwT, dtpb=c_dtpb, alog=c_alog,
                                dpar=c_dpar, opwT=c_opwT, identb=identb,
                                xout=xout),
                           pfx="c_", bar=b2, final=False)
        emit_l1_stats(nc, blk, es_top,
                      dict(xin=xout, wq=d_wq, bq=d_bq, identb=identb,
                           statout=statout),
                      pfx="d_", bar=b3, final=True)
    return nc


def run_mega(x_full, inputs):
    import ml_dtypes
    nc = build_mega()
    pa = prep_l0_attn_inputs(
        inputs["L0_attn_Wqkv"], inputs["L0_attn_bqkv"], inputs["L0_attn_W1p"],
        inputs["L0_attn_W2p"], inputs["L0_attn_plucker_scale"],
        inputs["L0_attn_Wout"], inputs["L0_attn_bout"])
    r1 = lambda a: np.ascontiguousarray(
        np.asarray(a, np.float32).reshape(-1, 128).T)
    r2 = lambda a, w: np.ascontiguousarray(
        np.asarray(a, np.float32).reshape(-1, 128, w).transpose(1, 0, 2))
    shared = dict(
        identb=pa["identb"],
        a_wbig=pa["wbig"], a_bbig=pa["bbig"], a_wout=pa["wout"],
        a_bout=pa["bout"], a_pscale=pa["pscale"], a_sign48=pa["sign48"],
        a_maskd=pa["maskd"],
        b_wg=np.ascontiguousarray(inputs["L0_ffn_w_gate"], np.float32),
        b_wu=np.ascontiguousarray(inputs["L0_ffn_w_up"], np.float32),
        b_wd=np.ascontiguousarray(inputs["L0_ffn_w_down"], np.float32),
        c_ipwT=np.ascontiguousarray(inputs["L1_attn_in_proj_w"].T, np.float32),
        c_lng=r1(inputs["L1_attn_ln_gamma"]),
        c_lnb=r1(inputs["L1_attn_ln_beta"]),
        c_convw=r2(inputs["L1_attn_conv_w"], 4),
        c_convb=r1(inputs["L1_attn_conv_b"]),
        c_xpwT=r2(inputs["L1_attn_x_proj_w"].T, 96),
        c_dtpwT=np.ascontiguousarray(inputs["L1_attn_dt_proj_w"].T, np.float32),
        c_dtpb=r1(inputs["L1_attn_dt_proj_b"]),
        c_alog=r2(inputs["L1_attn_A_log"], 16),
        c_dpar=r1(inputs["L1_attn_D_param"]),
        c_opwT=np.ascontiguousarray(inputs["L1_attn_out_proj_w"].T, np.float32),
        d_wq=np.ascontiguousarray(inputs["L1_ffn_Wq"], np.float32),
        d_bq=np.ascontiguousarray(
            np.asarray(inputs["L1_ffn_bq"], np.float32).reshape(16, 128).T),
    )
    in_maps = []
    for b in range(NCORES):
        m = dict(shared)
        m["xin"] = np.ascontiguousarray(x_full[b], np.float32)
        in_maps.append(m)
    res = run_bass_kernel_spmd(nc, in_maps, core_ids=list(range(NCORES)))
    x3 = np.stack([r["xout"] for r in res.results], axis=0)
    stats = np.stack([r["statout"] for r in res.results], axis=0)
    return x3, stats


def kernel(**inputs):
    x = np.ascontiguousarray(inputs["L0_attn_x"], np.float32)
    x3, stats = run_mega(x, inputs)
    x4 = run_l1_pkm(x3, stats, inputs["L1_ffn_Wq"], inputs["L1_ffn_bq"],
                    inputs["L1_ffn_bn_gamma"], inputs["L1_ffn_bn_beta"],
                    inputs["L1_ffn_sub_key0"], inputs["L1_ffn_sub_key1"],
                    inputs["L1_ffn_expert_down"], inputs["L1_ffn_expert_up"])
    return np.ascontiguousarray(x4, np.float32)


# version 12
# speedup vs baseline: 1.2347x; 1.2347x over previous
import contextlib
import numpy as np
import concourse.bass as bass
import concourse.mybir as mybir
from concourse.bass_utils import run_bass_kernel_spmd
from concourse.alu_op_type import AluOpType as OP

F32 = mybir.dt.float32
BF16 = mybir.dt.bfloat16
I32 = mybir.dt.int32
U32 = mybir.dt.uint32
U8 = mybir.dt.uint8
AF = mybir.ActivationFunctionType
AX = mybir.AxisListType.X

T = 2048
D = 1024
NCORES = 8
TT = T // 128
KC = D // 128
NEG = -1e30


class Sem:
    def __init__(self, es, nc, name):
        self.h = es.enter_context(nc.semaphore(name))
        self.n = 0

    def inc(self, instr, k=1):
        instr.then_inc(self.h, k)
        self.n += k
        return instr

    def wait(self, eng, n=None):
        v = self.n if n is None else n
        if v > 0:
            eng.wait_ge(self.h, v)



def build_l0_attn():
    nc = bass.Bass("TRN2", target_bir_lowering=False)
    xin = nc.dram_tensor("xin", [T, D], F32, kind="ExternalInput").ap()
    wbig = nc.dram_tensor("wbig", [D, 3200], F32, kind="ExternalInput").ap()
    bbig = nc.dram_tensor("bbig", [128, 24], F32, kind="ExternalInput").ap()
    wout = nc.dram_tensor("wout", [D, D], F32, kind="ExternalInput").ap()
    bout = nc.dram_tensor("bout", [D], F32, kind="ExternalInput").ap()
    pscale = nc.dram_tensor("pscale", [16], F32, kind="ExternalInput").ap()
    sign48 = nc.dram_tensor("sign48", [48], F32, kind="ExternalInput").ap()
    identb = nc.dram_tensor("identb", [128, 128], BF16, kind="ExternalInput").ap()
    identf = nc.dram_tensor("identf", [128, 128], F32, kind="ExternalInput").ap()
    maskd = nc.dram_tensor("maskd", [128, 128], BF16, kind="ExternalInput").ap()
    xout = nc.dram_tensor("xout", [T, D], F32, kind="ExternalOutput").ap()
    es_top = contextlib.ExitStack()
    with es_top:
        blk = es_top.enter_context(nc.Block())
        emit_l0_attn(nc, blk, es_top,
                     dict(xin=xin, wbig=wbig, bbig=bbig, wout=wout, bout=bout,
                          pscale=pscale, sign48=sign48, identb=identb,
                          maskd=maskd, xout=xout))
    return nc


def emit_l0_attn(nc, blk, es_top, aps, pfx="", bar=None, final=True):
    xin = aps["xin"]; wbig = aps["wbig"]; bbig = aps["bbig"]
    wout = aps["wout"]; bout = aps["bout"]; pscale = aps["pscale"]
    sign48 = aps["sign48"]; identb = aps["identb"]; maskd = aps["maskd"]
    xout = aps["xout"]

    es = contextlib.ExitStack()
    with es:
        sb = lambda name, shape, dt: es.enter_context(
            nc.sbuf_tensor(pfx + name, shape, dt))

        es_xnT = contextlib.ExitStack()
        xnT = es_xnT.enter_context(nc.sbuf_tensor(pfx + "xnT", [128, KC, T], BF16, side="right"))
        wq_sb = sb("wq_sb", [128, KC, 1600], BF16)
        qline = sb("qline", [128, 8, T], BF16)
        kline = sb("kline", [128, 8, T], BF16)
        vT = sb("vT", [128, TT, 8, 64], BF16)
        attn_sb = sb("attn_sb", [128, KC, T], BF16)
        probs = sb("probs", [128, T], BF16)
        probsT = sb("probsT", [128, TT, 128], BF16)
        xbuf = sb("xbuf", [128, 2, D], F32)
        xn2 = sb("xn2", [128, 2, D], BF16)
        junk = sb("junk", [128, 2, D], F32)
        p_t = sb("p_t", [128, 2, 64], F32)
        lines = sb("lines", [128, 48], F32)
        lsq = sb("lsq", [128, 48], F32)
        lsc2 = sb("lsc2", [128, 12, 8], F32)
        lsc = sb("lsc", [128, 96], BF16)
        small = sb("small", [128, 72], F32)
        identb_sb = sb("identb_sb", [128, 128], BF16)
        maskd_sb = sb("maskd_sb", [128, 128], BF16)
        braw = sb("braw", [128, 24], F32)
        bq = sb("bq", [128, 24], F32)
        scol = sb("scol", [128, 1], F32)
        psc_bc = sb("psc_bc", [128, 16], F32)
        sgn_bc = sb("sgn_bc", [128, 48], F32)
        bout_bc = sb("bout_bc", [128, D], F32)
        eps_t = sb("eps_t", [128, 2], F32)
        v_tmp = sb("v_tmp", [128, 512], BF16)
        wstage = sb("wstage", [128, 2, 400], F32)

        tp_ps = es.enter_context(nc.psum_tensor(pfx + "tp_ps", [128, 2, 4, 256], BF16))

        ld = Sem(es_top, nc, pfx + "ld")
        ldx = [Sem(es_top, nc, pfx + "ldx0"), Sem(es_top, nc, pfx + "ldx1")]
        stx = [Sem(es_top, nc, pfx + "stx0"), Sem(es_top, nc, pfx + "stx1")]
        wl = [Sem(es_top, nc, pfx + "wl0"), Sem(es_top, nc, pfx + "wl1")]
        pe = Sem(es_top, nc, pfx + "pe")
        sv = Sem(es_top, nc, pfx + "sv")
        ss = Sem(es_top, nc, pfx + "ss")
        st = Sem(es_top, nc, pfx + "st")
        if bar:
            def fbar(eng):
                for _s, _c in bar:
                    _s.wait(eng, _c)
            blk.sync(fbar)
            blk.scalar(fbar)
            blk.vector(fbar)
            blk.tensor(fbar)

        def f(sync):
            for o, i in [(identb_sb[:], identb), (maskd_sb[:], maskd), (braw[:], bbig),
                         (psc_bc[:], pscale.unsqueeze(0).to_broadcast([128, 16])),
                         (sgn_bc[:], sign48.unsqueeze(0).to_broadcast([128, 48])),
                         (bout_bc[:], bout.unsqueeze(0).to_broadcast([128, D]))]:
                ld.inc(sync.dma_start(out=o, in_=i), 16)
        blk.sync(f)
        ld_consts = ld.n

        def f(v):
            v.memset(scol[0:64, :], 0.125)
            v.memset(scol[64:128, :], 1.0)
            v.memset(eps_t[:, 0:1], 1e-5)
            v.memset(eps_t[:, 1:2], 1e-18)
            ld.wait(v, ld_consts)
            sv.inc(v.tensor_scalar(out=bq[:], in0=braw[:], scalar1=scol[:, 0:1],
                                   scalar2=None, op0=OP.mult))
        blk.vector(f)
        bq_ready = sv.n

        wconv_marks = []

        def emit_wpiece(src_ap, dst_ap, war_sv=0, war_pe=0, extra_wait_ss=0):
            i = len(wconv_marks)
            p = i % 2
            s_dma = wl[p]
            war = wconv_marks[i - 2][1] if i >= 2 else 0

            def f(sync, p=p, war=war, src_ap=src_ap):
                ss.wait(sync, war)
                s_dma.inc(sync.dma_start(out=wstage[:, p, 0:src_ap.shape[-1]],
                                         in_=src_ap), 16)
            blk.sync(f)
            dma_mark = s_dma.n

            def f(s, p=p, dma_mark=dma_mark, dst_ap=dst_ap, war_sv=war_sv,
                  war_pe=war_pe, extra_wait_ss=extra_wait_ss, s_dma=s_dma,
                  src_ap=src_ap):
                s_dma.wait(s, dma_mark)
                if war_pe:
                    pe.wait(s, war_pe)
                ss.inc(s.activation(out=dst_ap,
                                    in_=wstage[:, p, 0:src_ap.shape[-1]],
                                    func=AF.Identity))
            blk.scalar(f)
            wconv_marks.append(("ss", ss.n))
            return ("ss", ss.n)

        wsrc_all = wbig.rearrange("(kc p) f -> p kc f", p=128)
        wq_ready = ("ss", 0)
        for kc in range(KC):
            for hh in range(4):
                wq_ready = emit_wpiece(wsrc_all[:, kc, hh * 400:(hh + 1) * 400],
                                       wq_sb[:, kc, hh * 400:(hh + 1) * 400])

        v_fin = [0] * TT
        pe_fin = [0] * TT
        s_fin = [0] * TT
        for k in range(TT):
            slot = k % 2

            def f(sync, k=k, slot=slot):
                if k >= 2:
                    sv.wait(sync, v_fin[k - 2])
                ldx[k % 2].inc(sync.dma_start(out=xbuf[:, slot, :],
                                              in_=xin[k * 128:(k + 1) * 128, :]), 16)
            blk.sync(f)
            ld_k = ldx[k % 2].n

            SB = 8 * (k % 2)

            def f(v, k=k, slot=slot, ld_k=ld_k, SB=SB):
                ldx[k % 2].wait(v, ld_k)
                sv.inc(v.tensor_reduce(out=small[:, SB:SB + 1],
                                       in_=xbuf[:, slot, :], axis=AX, op=OP.add))
            blk.vector(f)
            m1 = sv.n

            def f(s, m1=m1, SB=SB):
                sv.wait(s, m1)
                ss.inc(s.activation(out=small[:, SB + 4:SB + 5],
                                    in_=small[:, SB:SB + 1],
                                    func=AF.Identity, scale=1.0 / D))
            blk.scalar(f)
            m2 = ss.n

            def f(v, k=k, slot=slot, m2=m2, SB=SB):
                ss.wait(v, m2)
                sv.inc(v.tensor_scalar(out=junk[:, slot, :], in0=xbuf[:, slot, :],
                                       scalar1=small[:, SB + 4:SB + 5],
                                       scalar2=None, op0=OP.subtract))
            blk.vector(f)
            m3 = sv.n

            def f(s, slot=slot, m3=m3, SB=SB):
                sv.wait(s, m3)
                ss.inc(s.activation(out=xn2[:, slot, :], in_=junk[:, slot, :],
                                    func=AF.Square,
                                    accum_out=small[:, SB + 5:SB + 6]))
                ss.wait(s, ss.n)
                ss.inc(s.activation(out=small[:, SB + 7:SB + 8],
                                    in_=small[:, SB + 5:SB + 6],
                                    func=AF.Ln, scale=1.0 / D,
                                    bias=eps_t[:, 0:1]))
                ss.wait(s, ss.n)
                ss.inc(s.activation(out=small[:, SB + 6:SB + 7],
                                    in_=small[:, SB + 7:SB + 8],
                                    func=AF.Exp, scale=-0.5))
            blk.scalar(f)
            m4 = ss.n

            def f(v, k=k, slot=slot, m4=m4, SB=SB):
                ss.wait(v, m4)
                if k >= 2:
                    pe.wait(v, pe_fin[k - 2])
                sv.inc(v.tensor_scalar(out=xn2[:, slot, :], in0=junk[:, slot, :],
                                       scalar1=small[:, SB + 6:SB + 7],
                                       scalar2=None, op0=OP.mult))
            blk.vector(f)
            v_fin[k] = sv.n

            def f(t, k=k, slot=slot):
                sv.wait(t, v_fin[k])
                if k >= 1:
                    ss.wait(t, s_fin[k - 1])
                for c in range(KC):
                    pe.inc(t.transpose(tp_ps[:, c // 4, c % 4, 0:128],
                                       xn2[:, slot, c * 128:(c + 1) * 128],
                                       identb_sb[:]))
            blk.tensor(f)
            pe_fin[k] = pe.n

            def f(s, k=k):
                pe.wait(s, pe_fin[k])
                for c in range(KC):
                    ss.inc(s.activation(out=xnT[:, c, k * 128:(k + 1) * 128],
                                        in_=tp_ps[:, c // 4, c % 4, 0:128],
                                        func=AF.Identity))
            blk.scalar(f)
            s_fin[k] = ss.n
        xnT_done = ss.n

        prev_exp = 0
        prev_ptr = 0
        prev_av = 0
        prev_attn_ev = 0
        pl_evac = {}
        pl_vfin = {}
        wl_out = None
        PI = [0, 0, 0, 1, 1, 2]
        PJ = [1, 2, 3, 2, 3, 3]

        for half in range(2):
            if half == 1:
                pe_h0 = pe.n
                for kc in range(KC):
                    for hh in range(4):
                        c0 = 1600 + hh * 400
                        wq_ready = emit_wpiece(
                            wsrc_all[:, kc, c0:c0 + 400],
                            wq_sb[:, kc, hh * 400:(hh + 1) * 400],
                            war_pe=pe_h0)

            es2 = contextlib.ExitStack()
            mm_ps = es2.enter_context(nc.psum_tensor(pfx + "mm_ps%d" % half,
                                                     [128, 2, 512], F32))
            pl_ps = es2.enter_context(nc.psum_tensor(pfx + "pl_ps%d" % half,
                                                     [128, 512], F32))
            lt_ps = es2.enter_context(nc.psum_tensor(pfx + "lt_ps%d" % half,
                                                     [128, 2, 1024], BF16))
            phase2_entry_ss = ss.n
            qkv_evac = []
            for mg in range(12):
                for sp in range(4):
                    bank = len(qkv_evac) % 2

                    def f(t, half=half, mg=mg, sp=sp, bank=bank, wq_ready=wq_ready,
                          war=(qkv_evac[-2] if len(qkv_evac) >= 2 else phase2_entry_ss)):
                        ss.wait(t, max(war, xnT_done, phase2_entry_ss, wq_ready[1]))
                        instr = None
                        for c in range(KC):
                            instr = t.matmul(mm_ps[:, bank, :],
                                             wq_sb[:, c, mg * 128:(mg + 1) * 128],
                                             xnT[:, c, sp * 512:(sp + 1) * 512],
                                             start=(c == 0), stop=(c == KC - 1))
                        pe.inc(instr)
                    blk.tensor(f)
                    mm_done = pe.n

                    def f(s, half=half, mg=mg, sp=sp, bank=bank, mm_done=mm_done):
                        pe.wait(s, mm_done)
                        sv.wait(s, bq_ready)
                        g = half * 12 + mg
                        cols = slice(sp * 512, (sp + 1) * 512)
                        if mg < 8:
                            ss.inc(s.activation(out=qline[0:64, mg, cols],
                                                in_=mm_ps[0:64, bank, :],
                                                func=AF.Identity,
                                                scale=scol[0:64, 0:1],
                                                bias=bq[0:64, g:g + 1]))
                            ss.inc(s.activation(out=kline[0:64, mg, cols],
                                                in_=mm_ps[64:128, bank, :],
                                                func=AF.Identity,
                                                bias=braw[64:128, g:g + 1]))
                        else:
                            ss.inc(s.activation(out=v_tmp[:], in_=mm_ps[:, bank, :],
                                                func=AF.Identity, bias=braw[:, g:g + 1]))
                    blk.scalar(f)
                    qkv_evac.append(ss.n)

                    if mg >= 8:
                        vev = ss.n

                        def f(t, vev=vev):
                            ss.wait(t, vev)
                            for b2 in range(4):
                                pe.inc(t.transpose(tp_ps[:, 0, b2, 0:128],
                                                   v_tmp[:, b2 * 128:(b2 + 1) * 128],
                                                   identb_sb[:]))
                        blk.tensor(f)
                        tdone = pe.n

                        def f(s, mg=mg, sp=sp, tdone=tdone):
                            pe.wait(s, tdone)
                            for b2 in range(4):
                                jt = sp * 4 + b2
                                hp = (mg - 8) * 2
                                ss.inc(s.activation(out=vT[:, jt, hp:hp + 2, :],
                                                    in_=tp_ps[:, 0, b2, 0:128],
                                                    func=AF.Identity))
                        blk.scalar(f)
                        qkv_evac[-1] = ss.n

            for k in range(TT):
                slot = k % 2

                def f(t, half=half, k=k, wq_ready=wq_ready,
                      war_s=pl_evac.get((half, k - 1), phase2_entry_ss),
                      war_v=pl_vfin.get((half, k - 2), 0)):
                    sv.wait(t, war_v)
                    ss.wait(t, max(war_s, xnT_done, phase2_entry_ss, wq_ready[1]))
                    instr = None
                    for c in range(KC):
                        instr = t.matmul(pl_ps[:, 0:64],
                                         xnT[:, c, k * 128:(k + 1) * 128],
                                         wq_sb[:, c, 1536:1600],
                                         start=(c == 0), stop=(c == KC - 1))
                    pe.inc(instr)
                blk.tensor(f)
                plmm = pe.n

                def f(s, k=k, slot=slot, plmm=plmm):
                    pe.wait(s, plmm)
                    ss.inc(s.activation(out=p_t[:, slot, :], in_=pl_ps[:, 0:64],
                                        func=AF.Identity))
                blk.scalar(f)
                pl_evac[(half, k)] = ss.n
                pev = ss.n

                def f(v, k=k, slot=slot, pev=pev):
                    ss.wait(v, pev)
                    pt = p_t[:, slot, :]
                    lv = lines[:].rearrange("p (h l) -> p h l", l=6)
                    for l in range(6):
                        a, b = PI[l], PJ[l]
                        v.tensor_tensor(out=lsc2[:, 2 * l, :],
                                        in0=pt[:, a * 8:a * 8 + 8],
                                        in1=pt[:, 32 + b * 8:32 + b * 8 + 8],
                                        op=OP.mult)
                        sv.inc(v.tensor_tensor(out=lsc2[:, 2 * l + 1, :],
                                               in0=pt[:, b * 8:b * 8 + 8],
                                               in1=pt[:, 32 + a * 8:32 + a * 8 + 8],
                                               op=OP.mult))
                    sv.wait(v, sv.n)
                    for l in range(6):
                        sv.inc(v.tensor_tensor(out=lv[:, :, l],
                                               in0=lsc2[:, 2 * l, :],
                                               in1=lsc2[:, 2 * l + 1, :],
                                               op=OP.subtract))
                    sv.wait(v, sv.n)
                    sv.inc(v.tensor_tensor(out=lsq[:], in0=lines[:], in1=lines[:],
                                           op=OP.mult))
                    sv.wait(v, sv.n)
                    LB = 16 + 24 * (k % 2)
                    sv.inc(v.tensor_reduce(
                        out=small[:, LB:LB + 8],
                        in_=lsq[:].rearrange("p (h l) -> p h l", l=6),
                        axis=AX, op=OP.add))
                blk.vector(f)
                lm1 = sv.n
                LB = 16 + 24 * (k % 2)

                def f(s, lm1=lm1, LB=LB):
                    sv.wait(s, lm1)
                    ss.inc(s.activation(out=small[:, LB + 8:LB + 16],
                                        in_=small[:, LB:LB + 8],
                                        func=AF.Ln, bias=eps_t[:, 1:2]))
                    ss.wait(s, ss.n)
                    ss.inc(s.activation(out=small[:, LB + 16:LB + 24],
                                        in_=small[:, LB + 8:LB + 16],
                                        func=AF.Exp, scale=-0.5))
                blk.scalar(f)
                lm2 = ss.n

                def f(v, half=half, lm2=lm2, LB=LB):
                    ss.wait(v, lm2)
                    sv.inc(v.tensor_tensor(out=small[:, LB:LB + 8],
                                    in0=small[:, LB + 16:LB + 24],
                                    in1=psc_bc[:, half * 8:half * 8 + 8], op=OP.mult))
                    sv.wait(v, sv.n)
                    rn6 = small[:, LB + 16:LB + 24].unsqueeze(2).to_broadcast([128, 8, 6])
                    rnq6 = small[:, LB:LB + 8].unsqueeze(2).to_broadcast([128, 8, 6])
                    lv = lines[:].rearrange("p (h l) -> p h l", l=6)
                    lrev = lv[:, :, ::-1]
                    qv = lsc[:, 0:48].rearrange("p (h l) -> p h l", l=6)
                    kv = lsc[:, 48:96].rearrange("p (h l) -> p h l", l=6)
                    v.tensor_tensor(out=qv, in0=lv, in1=rnq6, op=OP.mult)
                    sv.inc(v.tensor_tensor(out=kv, in0=lrev, in1=rn6, op=OP.mult))
                    sv.wait(v, sv.n)
                    sv.inc(v.tensor_tensor(out=lsc[:, 48:96], in0=lsc[:, 48:96],
                                           in1=sgn_bc[:], op=OP.mult))
                blk.vector(f)
                pl_vfin[(half, k)] = sv.n
                lfin = sv.n

                lt_marks = []
                for j in range(16):
                    reg = j % 2

                    def f(t, j=j, reg=reg, lfin=lfin,
                          war=(lt_marks[j - 2] if j >= 2 else 0)):
                        sv.wait(t, lfin)
                        ss.wait(t, war)
                        src = lsc[:, j * 6:(j + 1) * 6]
                        pe.inc(t.transpose(lt_ps[0:6, reg, 0:128],
                                           src, identb_sb[:]))
                    blk.tensor(f)
                    tdone = pe.n

                    def f(s, k=k, j=j, reg=reg, tdone=tdone):
                        pe.wait(s, tdone)
                        dst = qline if j < 8 else kline
                        h8 = j % 8
                        ss.inc(s.activation(
                            out=dst[64:70, h8, k * 128:(k + 1) * 128],
                            in_=lt_ps[0:6, reg, 0:128],
                            func=AF.Identity))
                    blk.scalar(f)
                    lt_marks.append(ss.n)

            if half == 1:
                pe_qkv_all = pe.n
                es_xnT.close()
                wo_sb = es.enter_context(nc.sbuf_tensor(pfx + "wo_sb", [128, KC, D], BF16, side="right"))
                obuf = es.enter_context(nc.sbuf_tensor(pfx + "obuf", [128, 2, D], F32, side="right"))
                xrbuf = es.enter_context(nc.sbuf_tensor(pfx + "xrbuf", [128, 2, D], F32, side="right"))
                wosrc = wout.rearrange("(kc p) f -> p kc f", p=128)
                for kc in range(KC):
                    for hh in range(4):
                        wl_out = emit_wpiece(
                            wosrc[:, kc, hh * 256:(hh + 1) * 256],
                            wo_sb[:, kc, hh * 256:(hh + 1) * 256],
                            war_pe=pe_qkv_all)
            es2.close()

            es3 = contextlib.ExitStack()
            sc_ps = es3.enter_context(nc.psum_tensor(pfx + "sc_ps%d" % half,
                                                     [128, T], F32))
            av_ps = es3.enter_context(nc.psum_tensor(pfx + "av_ps%d" % half,
                                                     [128, 512], F32))
            phase3_entry_ss = ss.n
            pend_attn = [None]

            for mg in range(8):
                h = half * 8 + mg
                for ic in range(TT):
                    j_ext = (ic + 1) * 128
                    nstrip = (j_ext + 511) // 512

                    def f(t, mg=mg, ic=ic, j_ext=j_ext, nstrip=nstrip,
                          war=max(prev_exp, phase3_entry_ss)):
                        ss.wait(t, war)
                        instr = None
                        for sp2 in range(nstrip):
                            c0 = sp2 * 512
                            c1 = min(c0 + 512, j_ext)
                            instr = t.matmul(sc_ps[:, c0:c1],
                                             qline[0:70, mg, ic * 128:(ic + 1) * 128],
                                             kline[0:70, mg, c0:c1],
                                             start=True, stop=True)
                        pe.inc(instr)
                    blk.tensor(f)
                    sc_done = pe.n

                    def f(v, ic=ic, j_ext=j_ext, sc_done=sc_done):
                        pe.wait(v, sc_done)
                        sv.inc(v.tensor_tensor(out=sc_ps[:, ic * 128:j_ext],
                                        in0=sc_ps[:, ic * 128:j_ext],
                                        in1=maskd_sb[:], op=OP.add))
                        sv.wait(v, sv.n)
                        sv.inc(v.tensor_reduce(out=small[:, 64:65], in_=sc_ps[:, 0:j_ext],
                                        axis=AX, op=OP.max))
                        sv.wait(v, sv.n)
                        sv.inc(v.tensor_scalar(out=small[:, 65:66], in0=small[:, 64:65],
                                               scalar1=-1.0, scalar2=None,
                                               op0=OP.mult))
                    blk.vector(f)
                    mx_done = sv.n

                    def f(s, j_ext=j_ext, mx_done=mx_done, war=prev_ptr,
                          pend=pend_attn[0]):
                        if pend is not None:
                            ph, pic, pav = pend
                            pe.wait(s, pav)
                            r0 = (ph % 2) * 64
                            ss.inc(s.activation(
                                out=attn_sb[r0:r0 + 64, ph // 2,
                                            pic * 128:(pic + 1) * 128],
                                in_=av_ps[0:64, 0:128], func=AF.Identity))
                        sv.wait(s, mx_done)
                        pe.wait(s, war)
                        ss.inc(s.activation(out=probs[:, 0:j_ext],
                                            in_=sc_ps[:, 0:j_ext],
                                            func=AF.Exp, bias=small[:, 65:66],
                                            scale=1.0, accum_out=small[:, 68:69]))
                    blk.scalar(f)
                    if pend_attn[0] is not None:
                        prev_attn_ev = ss.n - 1
                    pend_attn[0] = None
                    prev_exp = ss.n
                    exp_done = ss.n

                    def f(v, j_ext=j_ext, exp_done=exp_done):
                        ss.wait(v, exp_done)
                        sv.inc(v.reciprocal(out=small[:, 67:68], in_=small[:, 68:69]))
                        sv.wait(v, sv.n)
                        sv.inc(v.tensor_scalar(out=probs[:, 0:j_ext],
                                               in0=probs[:, 0:j_ext],
                                               scalar1=small[:, 67:68], scalar2=None,
                                               op0=OP.mult))
                    blk.vector(f)
                    pr_done = sv.n

                    ptr_evacs = []
                    grp_marks = []
                    ngrp = (ic + 1 + 3) // 4
                    for g2 in range(ngrp):
                        jlo = g2 * 4
                        jhi = min(jlo + 4, ic + 1)
                        bk = g2 % 2

                        def f(t, jlo=jlo, jhi=jhi, bk=bk, pr_done=pr_done,
                              war_av=prev_av,
                              war_ev=(grp_marks[g2 - 2] if g2 >= 2 else 0)):
                            sv.wait(t, pr_done)
                            pe.wait(t, war_av)
                            sv.wait(t, war_ev)
                            for jt in range(jlo, jhi):
                                pe.inc(t.transpose(tp_ps[:, bk, jt - jlo, 0:128],
                                                   probs[:, jt * 128:(jt + 1) * 128],
                                                   identb_sb[:]))
                        blk.tensor(f)
                        tdone = pe.n

                        def f(v, jlo=jlo, jhi=jhi, bk=bk, tdone=tdone):
                            pe.wait(v, tdone)
                            for jt in range(jlo, jhi):
                                sv.inc(v.tensor_copy(probsT[:, jt, :],
                                                     tp_ps[:, bk, jt - jlo,
                                                           0:128]))
                        blk.vector(f)
                        grp_marks.append(sv.n)
                    ptr_evacs.append(sv.n)
                    prev_ptr = pe.n

                    def f(t, mg=mg, ic=ic, ev=ptr_evacs[-1],
                          war=prev_attn_ev):
                        sv.wait(t, ev)
                        ss.wait(t, war)
                        instr = None
                        for jt in range(ic + 1):
                            instr = t.matmul(av_ps[0:64, 0:128], vT[:, jt, mg, :],
                                             probsT[:, jt, :],
                                             start=(jt == 0), stop=(jt == ic))
                        pe.inc(instr)
                    blk.tensor(f)
                    prev_av = pe.n
                    pend_attn[0] = (h, ic, pe.n)
            def f(s, pend=pend_attn[0]):
                ph, pic, pav = pend
                pe.wait(s, pav)
                r0 = (ph % 2) * 64
                ss.inc(s.activation(out=attn_sb[r0:r0 + 64, ph // 2,
                                                pic * 128:(pic + 1) * 128],
                                    in_=av_ps[0:64, 0:128], func=AF.Identity))
            blk.scalar(f)
            pend_attn[0] = None
            prev_attn_ev = ss.n
            es3.close()

        attn_done = ss.n

        es4 = contextlib.ExitStack()
        mo_ps = es4.enter_context(nc.psum_tensor(pfx + "mo_ps", [128, 2, 512], F32))

        v_out = [0] * TT
        o_marks = []
        state_c = {"prev_opmm": 0, "prev_ysz": 0, "prev_bcst": 0}
        for k in range(TT):
            slot = k % 2

            def f(sync, k=k, slot=slot):
                if k >= 2:
                    sv.wait(sync, v_out[k - 2])
                ldx[k % 2].inc(sync.dma_start(out=xrbuf[:, slot, :],
                                              in_=xin[k * 128:(k + 1) * 128, :]), 16)
            blk.sync(f)
            ld_k = ldx[k % 2].n

            for sp in range(2):
                bank = (k * 2 + sp) % 2

                def f(t, k=k, sp=sp, bank=bank,
                      war=(o_marks[-2] if len(o_marks) >= 2 else 0)):
                    ss.wait(t, max(attn_done, wl_out[1]))
                    sv.wait(t, war)
                    instr = None
                    for c in range(KC):
                        instr = t.matmul(mo_ps[:, bank, :],
                                         attn_sb[:, c, k * 128:(k + 1) * 128],
                                         wo_sb[:, c, sp * 512:(sp + 1) * 512],
                                         start=(c == 0), stop=(c == KC - 1))
                    pe.inc(instr)
                blk.tensor(f)
                mmd = pe.n

                def f(v, k=k, sp=sp, slot=slot, bank=bank, mmd=mmd, ld_k=ld_k):
                    pe.wait(v, mmd)
                    ldx[k % 2].wait(v, ld_k)
                    if k >= 2:
                        stx[k % 2].wait(v, 16 * ((k - 2) // 2 + 1))
                    cols = slice(sp * 512, (sp + 1) * 512)
                    sv.inc(v.scalar_tensor_tensor(out=obuf[:, slot, cols],
                                           in0=mo_ps[:, bank, :], scalar=1.0,
                                           in1=bout_bc[:, cols], op0=OP.mult,
                                           op1=OP.add))
                    sv.wait(v, sv.n)
                    sv.inc(v.tensor_tensor(out=obuf[:, slot, cols],
                                           in0=obuf[:, slot, cols],
                                           in1=xrbuf[:, slot, cols], op=OP.add))
                blk.vector(f)
                o_marks.append(sv.n)
            v_out[k] = sv.n

            def f(sync, k=k, slot=slot):
                sv.wait(sync, v_out[k])
                stx[k % 2].inc(sync.dma_start(out=xout[k * 128:(k + 1) * 128, :],
                                              in_=obuf[:, slot, :]), 16)
            blk.sync(f)
        es4.close()

        if final:
            def f(g):
                st.wait(g, st.n)
                stx[0].wait(g, stx[0].n)
                stx[1].wait(g, stx[1].n)
            blk.gpsimd(f)
    return [(s, s.n) for s in (ld, ldx[0], ldx[1], stx[0], stx[1], wl[0],
                               wl[1], pe, sv, ss, st)]


def prep_l0_attn_inputs(Wqkv, bqkv, W1p, W2p, pscale, Wout, bout_):
    import ml_dtypes
    cols = []
    bias_groups = []
    for half in range(2):
        hs = range(half * 8, half * 8 + 8)
        for h in hs:
            cols.extend(range(h * 64, (h + 1) * 64))
            cols.extend(range(D + h * 64, D + (h + 1) * 64))
        for h in hs:
            cols.extend(range(2 * D + h * 64, 2 * D + (h + 1) * 64))
        pcols_1 = [h * 4 + c for c in range(4) for h in hs]
        pcols_2 = [h * 4 + c for c in range(4) for h in hs]
        cols.append(("p", pcols_1, pcols_2))
    wparts = []
    bias_list = []
    for half in range(2):
        hs = list(range(half * 8, half * 8 + 8))
        qk = np.concatenate([np.concatenate(
            [Wqkv[:, h * 64:(h + 1) * 64], Wqkv[:, D + h * 64:D + (h + 1) * 64]],
            axis=1) for h in hs], axis=1)
        v = np.concatenate([Wqkv[:, 2 * D + h * 64:2 * D + (h + 1) * 64]
                            for h in hs], axis=1)
        p1 = W1p[:, [h * 4 + c for c in range(4) for h in hs]]
        p2 = W2p[:, [h * 4 + c for c in range(4) for h in hs]]
        wparts.append(np.concatenate([qk, v, p1, p2], axis=1))
        for h in hs:
            bias_list.append(np.concatenate(
                [bqkv[h * 64:(h + 1) * 64], bqkv[D + h * 64:D + (h + 1) * 64]]))
        for h2 in range(0, 8, 2):
            hA, hB = hs[h2], hs[h2 + 1]
            bias_list.append(np.concatenate(
                [bqkv[2 * D + hA * 64:2 * D + (hA + 1) * 64],
                 bqkv[2 * D + hB * 64:2 * D + (hB + 1) * 64]]))
    wbig = np.ascontiguousarray(np.concatenate(wparts, axis=1), dtype=np.float32)
    bbig = np.ascontiguousarray(np.stack(bias_list, axis=1), dtype=np.float32)
    assert wbig.shape == (D, 3200) and bbig.shape == (128, 24)
    sgn = np.array([1, -1, 1, 1, -1, 1], np.float32)
    sign48 = np.tile(sgn, 8).astype(np.float32)
    identb = np.eye(128, dtype=ml_dtypes.bfloat16)
    identf = np.eye(128, dtype=np.float32)
    ii, jj = np.meshgrid(np.arange(128), np.arange(128), indexing="ij")
    maskd = np.where(jj > ii, np.float32(NEG), np.float32(0.0)).astype(ml_dtypes.bfloat16)
    return dict(wbig=wbig, bbig=bbig, wout=np.ascontiguousarray(Wout, np.float32),
                bout=np.ascontiguousarray(bout_, np.float32),
                pscale=np.ascontiguousarray(pscale, np.float32),
                sign48=sign48, identb=identb, identf=identf,
                maskd=np.ascontiguousarray(maskd))


def run_l0_attn(x_full, Wqkv, bqkv, W1p, W2p, pscale, Wout, bout_):
    nc = build_l0_attn()
    shared = prep_l0_attn_inputs(Wqkv, bqkv, W1p, W2p, pscale, Wout, bout_)
    in_maps = []
    for b in range(NCORES):
        m = dict(shared)
        m["xin"] = np.ascontiguousarray(x_full[b], np.float32)
        in_maps.append(m)
    res = run_bass_kernel_spmd(nc, in_maps, core_ids=list(range(NCORES)))
    return np.stack([r["xout"] for r in res.results], axis=0)





def build_l0_ffn():
    NI = 2816
    nc = bass.Bass("TRN2", target_bir_lowering=False)
    xin = nc.dram_tensor("xin", [T, D], F32, kind="ExternalInput").ap()
    wg = nc.dram_tensor("wg", [NI, D], F32, kind="ExternalInput").ap()
    wu = nc.dram_tensor("wu", [NI, D], F32, kind="ExternalInput").ap()
    wd = nc.dram_tensor("wd", [D, NI], F32, kind="ExternalInput").ap()
    identb = nc.dram_tensor("identb", [128, 128], BF16, kind="ExternalInput").ap()
    xout = nc.dram_tensor("xout", [T, D], F32, kind="ExternalOutput").ap()
    es_top = contextlib.ExitStack()
    with es_top:
        blk = es_top.enter_context(nc.Block())
        emit_l0_ffn(nc, blk, es_top,
                    dict(xin=xin, wg=wg, wu=wu, wd=wd, identb=identb,
                         xout=xout))
    return nc


def emit_l0_ffn(nc, blk, es_top, aps, pfx="", bar=None, final=True):
    NI = 2816
    NK = NI // 128
    xin = aps["xin"]; wg = aps["wg"]; wu = aps["wu"]; wd = aps["wd"]
    identb = aps["identb"]; xout = aps["xout"]
    wdqnat = nc.dram_tensor(pfx + "wdqnat", [D, NI], BF16, kind="Internal").ap()

    es = contextlib.ExitStack()
    with es:
        sb = lambda name, shape, dt: es.enter_context(
            nc.sbuf_tensor(pfx + name, shape, dt))

        wgq = sb("wgq", [128, KC, NI], BF16)
        wuq = sb("wuq", [128, KC, NI], BF16)
        xqT = sb("xqT", [128, KC, T], BF16)
        xbuf = sb("xbuf", [128, 2, D], F32)
        junk = sb("junk", [128, 2, D], F32)
        xn2 = sb("xn2", [128, 2, D], F32)
        small = sb("small", [128, 16], F32)
        eps_t = sb("eps_t", [128, 1], F32)
        identb_sb = sb("identb_sb", [128, 128], BF16)
        icon = sb("icon", [128, 2], U32)
        qgm = sb("qgm", [128, 4, 64], F32)
        qabs = sb("qabs", [128, D], F32)
        qsgn = sb("qsgn", [128, D], BF16)
        qa = sb("qa", [128, D], F32)
        qc = sb("qc", [128, D], U8)
        qout = sb("qout", [128, D], BF16)
        qgmB = sb("qgmB", [128, 4, 64], F32)
        qabsB = sb("qabsB", [128, D], F32)
        qsgnB = sb("qsgnB", [128, D], BF16)
        qaB = sb("qaB", [128, D], F32)
        qcB = sb("qcB", [128, D], U8)
        qoutB = sb("qoutB", [128, D], BF16)
        sg = sb("sg", [128, 512], F32)
        hbuf = sb("hbuf", [128, 512], F32)
        hqT_sb = sb("hqT_sb", [128, 4, 128], BF16)
        wbuf = xbuf
        obuf = xn2
        xrbuf = junk
        wdb = sb("wdb", [128, 2, D], BF16)

        tp_ps = es.enter_context(nc.psum_tensor(pfx + "tp_ps", [128, 2, 4, 256], BF16))

        ldx = [Sem(es_top, nc, pfx + "ldx0"), Sem(es_top, nc, pfx + "ldx1")]
        stx = [Sem(es_top, nc, pfx + "stx0"), Sem(es_top, nc, pfx + "stx1")]
        wlp = [Sem(es_top, nc, pfx + "wl0"), Sem(es_top, nc, pfx + "wl1")]
        sdq = Sem(es_top, nc, pfx + "sdq")
        pe = Sem(es_top, nc, pfx + "pe")
        sv = Sem(es_top, nc, pfx + "sv")
        ss = Sem(es_top, nc, pfx + "ss")
        if bar:
            def fbar(eng):
                for _s, _c in bar:
                    _s.wait(eng, _c)
            blk.sync(fbar)
            blk.scalar(fbar)
            blk.vector(fbar)
            blk.tensor(fbar)

        def f(sync):
            ldx[0].inc(sync.dma_start(out=identb_sb[:], in_=identb), 16)
        blk.sync(f)

        def f(v):
            v.memset(eps_t[:], 1e-5)
            v.memset(icon[:, 0:1], 8)
            v.memset(icon[:, 1:2], 0xFFC000)
        blk.vector(f)

        state = {"lastq": 0, "wdn": 0, "wdn_mm": {}, "qn2": 0}

        def emit_quant(in_ap, F, waits_s=(), waits_v=(), out_off=0):
            nb = F // 16
            lastq = state["lastq"]

            def f(s, in_ap=in_ap, F=F, lastq=lastq, waits_s=waits_s):
                sv.wait(s, lastq)
                for sem, val in waits_s:
                    sem.wait(s, val)
                ss.inc(s.activation(out=qabs[:, 0:F], in_=in_ap, func=AF.Abs))
                ss.inc(s.activation(out=qsgn[:, 0:F], in_=in_ap, func=AF.Sign))
            blk.scalar(f)
            sm = ss.n

            def f(v, F=F, nb=nb, sm=sm, waits_v=waits_v, out_off=out_off):
                ss.wait(v, sm)
                for sem, val in waits_v:
                    sem.wait(v, val)
                gm = qgm[:, 0, 0:nb]
                gm2 = qgm[:, 1, 0:nb]
                sc6 = qgm[:, 2, 0:nb]
                rs = qgm[:, 3, 0:nb]
                sv.inc(v.tensor_reduce(out=gm, in_=qabs[:, 0:F].rearrange(
                    "p (b i) -> p b i", i=16), axis=AX, op=OP.max))
                sv.wait(v, sv.n)
                sv.inc(v.tensor_scalar(out=gm2, in0=gm, scalar1=6e-12,
                                       scalar2=None, op0=OP.max))
                sv.wait(v, sv.n)
                v.tensor_scalar(out=sc6, in0=gm2, scalar1=1.0 / 6.0,
                                scalar2=None, op0=OP.mult)
                sv.inc(v.reciprocal(out=rs, in_=gm2))
                sv.wait(v, sv.n)
                sv.inc(v.tensor_scalar(out=rs, in0=rs, scalar1=6.0,
                                       scalar2=None, op0=OP.mult))
                sv.wait(v, sv.n)
                rsb = rs.unsqueeze(2).to_broadcast([128, nb, 16])
                sv.inc(v.tensor_tensor(
                    out=qa[:, 0:F].rearrange("p (b i) -> p b i", i=16),
                    in0=qabs[:, 0:F].rearrange("p (b i) -> p b i", i=16),
                    in1=rsb, op=OP.mult))
                sv.wait(v, sv.n)
                sv.inc(v.tensor_scalar(out=qc[:, 0:F], in0=qa[:, 0:F],
                                       scalar1=1.0, scalar2=None, op0=OP.is_lt))
                sv.wait(v, sv.n)
                sv.inc(v.tensor_tensor(out=qa[:, 0:F], in0=qa[:, 0:F],
                                       in1=qc[:, 0:F], op=OP.add))
                sv.wait(v, sv.n)
                ui = qa[:, 0:F].bitcast(U32)
                sv.inc(v.tensor_scalar(out=ui, in0=ui, scalar1=icon[:, 0:1],
                                       scalar2=None, op0=OP.logical_shift_right))
                sv.wait(v, sv.n)
                sv.inc(v.tensor_scalar(out=ui, in0=ui, scalar1=8192.0,
                                       scalar2=None, op0=OP.add))
                sv.wait(v, sv.n)
                sv.inc(v.tensor_scalar(out=ui, in0=ui, scalar1=icon[:, 1:2],
                                       scalar2=icon[:, 0:1], op0=OP.bitwise_and,
                                       op1=OP.logical_shift_left))
                sv.wait(v, sv.n)
                sv.inc(v.tensor_tensor(out=qa[:, 0:F], in0=qa[:, 0:F],
                                       in1=qc[:, 0:F], op=OP.subtract))
                sv.wait(v, sv.n)
                scb = sc6.unsqueeze(2).to_broadcast([128, nb, 16])
                sv.inc(v.tensor_tensor(
                    out=qa[:, 0:F].rearrange("p (b i) -> p b i", i=16),
                    in0=qa[:, 0:F].rearrange("p (b i) -> p b i", i=16),
                    in1=scb, op=OP.mult))
                sv.wait(v, sv.n)
                sdq.wait(v, state.get("qout_store", 0))
                sv.inc(v.tensor_tensor(out=qout[:, out_off:out_off + F],
                                       in0=qa[:, 0:F],
                                       in1=qsgn[:, 0:F], op=OP.mult))
            blk.vector(f)
            state["lastq"] = sv.n
            return sv.n

        def emit_quant_pair(inA, FA, inB, FB, waits_sA=(), waits_sB=()):
            lastq = state["lastq"]
            sl = [dict(F=FA, nb=FA // 16, in_ap=inA, ab=qabs, sg=qsgn, a=qa,
                       c=qc, gm=qgm, out=qout, w=waits_sA),
                  dict(F=FB, nb=FB // 16, in_ap=inB, ab=qabsB, sg=qsgnB, a=qaB,
                       c=qcB, gm=qgmB, out=qoutB, w=waits_sB)]

            def f(s, lastq=lastq):
                sv.wait(s, lastq)
                for S in sl:
                    for sem, val in S["w"]:
                        sem.wait(s, val)
                    F = S["F"]
                    ss.inc(s.activation(out=S["ab"][:, 0:F], in_=S["in_ap"],
                                        func=AF.Abs))
                    ss.inc(s.activation(out=S["sg"][:, 0:F], in_=S["in_ap"],
                                        func=AF.Sign))
                    S["sm"] = ss.n
            blk.scalar(f)

            def steps(v, S, i):
                F, nb = S["F"], S["nb"]
                gm = S["gm"][:, 0, 0:nb]
                gm2 = S["gm"][:, 1, 0:nb]
                sc6 = S["gm"][:, 2, 0:nb]
                rs = S["gm"][:, 3, 0:nb]
                ab3 = S["ab"][:, 0:F].rearrange("p (b i) -> p b i", i=16)
                a3 = S["a"][:, 0:F].rearrange("p (b i) -> p b i", i=16)
                if i == 0:
                    sv.inc(v.tensor_reduce(out=gm, in_=ab3, axis=AX,
                                           op=OP.max))
                elif i == 1:
                    sv.inc(v.tensor_scalar(out=gm2, in0=gm, scalar1=6e-12,
                                           scalar2=None, op0=OP.max))
                elif i == 2:
                    v.tensor_scalar(out=sc6, in0=gm2, scalar1=1.0 / 6.0,
                                    scalar2=None, op0=OP.mult)
                    sv.inc(v.reciprocal(out=rs, in_=gm2))
                elif i == 3:
                    sv.inc(v.tensor_scalar(out=rs, in0=rs, scalar1=6.0,
                                           scalar2=None, op0=OP.mult))
                elif i == 4:
                    rsb = rs.unsqueeze(2).to_broadcast([128, nb, 16])
                    sv.inc(v.tensor_tensor(out=a3, in0=ab3, in1=rsb,
                                           op=OP.mult))
                elif i == 5:
                    sv.inc(v.tensor_scalar(out=S["c"][:, 0:F],
                                           in0=S["a"][:, 0:F], scalar1=1.0,
                                           scalar2=None, op0=OP.is_lt))
                elif i == 6:
                    sv.inc(v.tensor_tensor(out=S["a"][:, 0:F],
                                           in0=S["a"][:, 0:F],
                                           in1=S["c"][:, 0:F], op=OP.add))
                elif i == 7:
                    ui = S["a"][:, 0:F].bitcast(U32)
                    sv.inc(v.tensor_scalar(out=ui, in0=ui,
                                           scalar1=icon[:, 0:1], scalar2=None,
                                           op0=OP.logical_shift_right))
                elif i == 8:
                    ui = S["a"][:, 0:F].bitcast(U32)
                    sv.inc(v.tensor_scalar(out=ui, in0=ui, scalar1=8192.0,
                                           scalar2=None, op0=OP.add))
                elif i == 9:
                    ui = S["a"][:, 0:F].bitcast(U32)
                    sv.inc(v.tensor_scalar(out=ui, in0=ui,
                                           scalar1=icon[:, 1:2],
                                           scalar2=icon[:, 0:1],
                                           op0=OP.bitwise_and,
                                           op1=OP.logical_shift_left))
                elif i == 10:
                    sv.inc(v.tensor_tensor(out=S["a"][:, 0:F],
                                           in0=S["a"][:, 0:F],
                                           in1=S["c"][:, 0:F],
                                           op=OP.subtract))
                elif i == 11:
                    scb = sc6.unsqueeze(2).to_broadcast([128, nb, 16])
                    sv.inc(v.tensor_tensor(out=a3, in0=a3, in1=scb,
                                           op=OP.mult))
                elif i == 12:
                    sv.inc(v.tensor_tensor(out=S["out"][:, 0:S["F"]],
                                           in0=S["a"][:, 0:S["F"]],
                                           in1=S["sg"][:, 0:S["F"]],
                                           op=OP.mult))
                S["mk"].append(sv.n)

            def f(v, sl=sl, cons=state.get("q_cons_pe", 0),
                  qst=state.get("qout_store", 0)):
                pe.wait(v, cons)
                sdq.wait(v, qst)
                for S in sl:
                    S["mk"] = []
                for i in range(13):
                    for S in sl:
                        if i == 0:
                            ss.wait(v, S["sm"])
                        else:
                            sv.wait(v, S["mk"][i - 1])
                        steps(v, S, i)
            blk.vector(f)
            state["lastq"] = sv.n
            return sl[0]["mk"][12], sl[1]["mk"][12]

        v_fin = [0] * TT
        pe_fin = [0] * TT
        s_fin = [0] * TT
        for k in range(TT):
            slot = k % 2
            SB = 8 * (k % 2)

            def f(sync, k=k, slot=slot):
                if k >= 2:
                    sv.wait(sync, v_fin[k - 2])
                ldx[k % 2].inc(sync.dma_start(out=xbuf[:, slot, :],
                                              in_=xin[k * 128:(k + 1) * 128, :]), 16)
            blk.sync(f)
            ld_k = ldx[k % 2].n

            def f(v, k=k, slot=slot, ld_k=ld_k, SB=SB):
                ldx[k % 2].wait(v, ld_k)
                sv.inc(v.tensor_reduce(out=small[:, SB:SB + 1],
                                       in_=xbuf[:, slot, :], axis=AX, op=OP.add))
            blk.vector(f)
            m1 = sv.n

            def f(s, m1=m1, SB=SB):
                sv.wait(s, m1)
                ss.inc(s.activation(out=small[:, SB + 4:SB + 5],
                                    in_=small[:, SB:SB + 1],
                                    func=AF.Identity, scale=1.0 / D))
            blk.scalar(f)
            m2 = ss.n

            def f(v, k=k, slot=slot, m2=m2, SB=SB):
                ss.wait(v, m2)
                sv.inc(v.tensor_scalar(out=junk[:, slot, :], in0=xbuf[:, slot, :],
                                       scalar1=small[:, SB + 4:SB + 5],
                                       scalar2=None, op0=OP.subtract))
            blk.vector(f)
            m3 = sv.n

            def f(s, slot=slot, m3=m3, SB=SB):
                sv.wait(s, m3)
                ss.inc(s.activation(out=xn2[:, slot, :], in_=junk[:, slot, :],
                                    func=AF.Square, accum_out=small[:, SB + 5:SB + 6]))
                ss.wait(s, ss.n)
                ss.inc(s.activation(out=small[:, SB + 7:SB + 8],
                                    in_=small[:, SB + 5:SB + 6],
                                    func=AF.Ln, scale=1.0 / D,
                                    bias=eps_t[:, 0:1]))
                ss.wait(s, ss.n)
                ss.inc(s.activation(out=small[:, SB + 6:SB + 7],
                                    in_=small[:, SB + 7:SB + 8],
                                    func=AF.Exp, scale=-0.5))
            blk.scalar(f)
            m4 = ss.n

            def f(v, k=k, slot=slot, m4=m4, SB=SB):
                ss.wait(v, m4)
                sv.inc(v.tensor_scalar(out=xn2[:, slot, :], in0=junk[:, slot, :],
                                       scalar1=small[:, SB + 6:SB + 7],
                                       scalar2=None, op0=OP.mult))
            blk.vector(f)
            xn_done = sv.n

            qm = emit_quant(xn2[:, slot, :], D)
            v_fin[k] = qm

            def f(t, k=k, qm=qm):
                sv.wait(t, qm)
                if k >= 1:
                    ss.wait(t, s_fin[k - 1])
                for c in range(KC):
                    pe.inc(t.transpose(tp_ps[:, c // 4, c % 4, 0:128],
                                       qout[:, c * 128:(c + 1) * 128],
                                       identb_sb[:]))
            blk.tensor(f)
            pe_fin[k] = pe.n

            def f(s, k=k):
                pe.wait(s, pe_fin[k])
                for c in range(KC):
                    ss.inc(s.activation(out=xqT[:, c, k * 128:(k + 1) * 128],
                                        in_=tp_ps[:, c // 4, c % 4, 0:128],
                                        func=AF.Identity))
            blk.scalar(f)
            s_fin[k] = ss.n

        wpieces = [v_fin[TT - 1], v_fin[TT - 1]]
        def emit_wpiece_q(src_ap, F, dst_fn):
            i = len(wpieces)
            p = i % 2
            war = wpieces[i - 2]

            def f(sync, p=p, war=war, src_ap=src_ap, F=F):
                sv.wait(sync, war)
                wlp[p].inc(sync.dma_start(out=wbuf[:, p, 0:F], in_=src_ap), 16)
            blk.sync(f)
            dmam = wlp[p].n

            qm = emit_quant(wbuf[:, p, 0:F], F, waits_s=((wlp[p], dmam),))
            wpieces.append(qm)
            nblk = F // 128
            for g0 in range(0, nblk, 8):
                g1 = min(g0 + 8, nblk)
                war_ev = ss.n

                def f(t, g0=g0, g1=g1, qm=qm, war_ev=war_ev):
                    sv.wait(t, qm)
                    ss.wait(t, war_ev)
                    for c in range(g0, g1):
                        cc = c - g0
                        pe.inc(t.transpose(tp_ps[:, cc // 4, cc % 4, 0:128],
                                           qout[:, c * 128:(c + 1) * 128],
                                           identb_sb[:]))
                blk.tensor(f)
                tm = pe.n

                def f(s, g0=g0, g1=g1, tm=tm, dst_fn=dst_fn):
                    pe.wait(s, tm)
                    for c in range(g0, g1):
                        cc = c - g0
                        ss.inc(s.activation(out=dst_fn(c),
                                            in_=tp_ps[:, cc // 4, cc % 4, 0:128],
                                            func=AF.Identity))
                blk.scalar(f)

        state["q_sm"] = 0
        state["q_sm_v"] = wpieces[-1]

        def pair_dma(srcA, FA, srcB, FB):
            def f(sync, srcA=srcA, FA=FA, srcB=srcB, FB=FB,
                  war=state["q_sm"], war_v=state["q_sm_v"]):
                ss.wait(sync, war)
                sv.wait(sync, war_v)
                wlp[0].inc(sync.dma_start(out=wbuf[:, 0, 0:FA], in_=srcA), 16)
                wlp[1].inc(sync.dma_start(out=wbuf[:, 1, 0:FB], in_=srcB), 16)
            blk.sync(f)
            return wlp[0].n, wlp[1].n

        def pair_transposes(qm, qbuf, F, dst_fn):
            nblk = F // 128
            war_ev = ss.n

            def f(t, qm=qm, qbuf=qbuf, nblk=nblk, war_ev=war_ev):
                sv.wait(t, qm)
                ss.wait(t, war_ev)
                for c in range(nblk):
                    pe.inc(t.transpose(tp_ps[:, c // 4, c % 4, 0:128],
                                       qbuf[:, c * 128:(c + 1) * 128],
                                       identb_sb[:]))
            blk.tensor(f)
            tm = pe.n

            def f(s, tm=tm, nblk=nblk, dst_fn=dst_fn):
                pe.wait(s, tm)
                for c in range(nblk):
                    ss.inc(s.activation(out=dst_fn(c),
                                        in_=tp_ps[:, c // 4, c % 4, 0:128],
                                        func=AF.Identity))
            blk.scalar(f)

        for rt in range(NK):
            dmA, dmB = pair_dma(wg[rt * 128:(rt + 1) * 128, :], D,
                                wu[rt * 128:(rt + 1) * 128, :], D)
            qmA, qmB = emit_quant_pair(wbuf[:, 0, 0:D], D, wbuf[:, 1, 0:D], D,
                                       waits_sA=((wlp[0], dmA),),
                                       waits_sB=((wlp[1], dmB),))
            state["q_sm"] = ss.n
            pair_transposes(qmA, qout,
                            D, lambda c, rt=rt: wgq[:, c,
                                                    rt * 128:(rt + 1) * 128])
            pair_transposes(qmB, qoutB,
                            D, lambda c, rt=rt: wuq[:, c,
                                                    rt * 128:(rt + 1) * 128])
            state["q_cons_pe"] = pe.n
        dpieces = [(rt, pc, plen) for rt in range(KC)
                   for pc, plen in ((0, 1024), (1024, 1024), (2048, 768))]
        for pi in range(0, len(dpieces), 2):
            (rtA, pcA, plA) = dpieces[pi]
            (rtB, pcB, plB) = dpieces[pi + 1]
            dmA, dmB = pair_dma(wd[rtA * 128:(rtA + 1) * 128, pcA:pcA + plA],
                                plA,
                                wd[rtB * 128:(rtB + 1) * 128, pcB:pcB + plB],
                                plB)
            qmA, qmB = emit_quant_pair(wbuf[:, 0, 0:plA], plA,
                                       wbuf[:, 1, 0:plB], plB,
                                       waits_sA=((wlp[0], dmA),),
                                       waits_sB=((wlp[1], dmB),))
            state["q_sm"] = ss.n

            def f(sync, qmA=qmA, qmB=qmB, rtA=rtA, pcA=pcA, plA=plA,
                  rtB=rtB, pcB=pcB, plB=plB):
                sv.wait(sync, qmB)
                sdq.inc(sync.dma_start(
                    out=wdqnat[rtA * 128:(rtA + 1) * 128, pcA:pcA + plA],
                    in_=qout[:, 0:plA]), 16)
                sdq.inc(sync.dma_start(
                    out=wdqnat[rtB * 128:(rtB + 1) * 128, pcB:pcB + plB],
                    in_=qoutB[:, 0:plB]), 16)
            blk.sync(f)
            state["qout_store"] = sdq.n
        wq_done_s = ss.n
        wq_done_v = sv.n
        xqT_done = s_fin[TT - 1]

        es2 = contextlib.ExitStack()
        g_ps = es2.enter_context(nc.psum_tensor(pfx + "g_ps", [128, 512], F32))
        u_ps = es2.enter_context(nc.psum_tensor(pfx + "u_ps", [128, 512], F32))
        d_ps = es2.enter_context(nc.psum_tensor(pfx + "d_ps", [128, 2, 512], F32))
        t2_ps = es2.enter_context(nc.psum_tensor(pfx + "t2_ps", [128, 4, 256], BF16))

        chunk_sizes = [512, 512, 512, 512, 512, 256]
        NCH = len(chunk_sizes)
        prev_sg = 0
        o_marks = []
        state_c = {"prev_opmm": 0, "prev_ysz": 0, "prev_bcst": 0, "prev_hq": 0}
        v_out = [0] * TT
        ld_ks = {}
        pend = [None]

        def emit_tail():
            (pk, pci, pcsz, pqm, poff, pn0) = pend[0]
            pend[0] = None
            nblk = pcsz // 128

            def f(t, pqm=pqm, nblk=nblk, poff=poff,
                  war=state_c["prev_hq"]):
                sv.wait(t, pqm)
                ss.wait(t, war)
                for c in range(nblk):
                    pe.inc(t.transpose(t2_ps[:, c, 0:128],
                                       qout[:, poff + c * 128:
                                            poff + (c + 1) * 128],
                                       identb_sb[:]))
            blk.tensor(f)
            tqm = pe.n

            def f(s, tqm=tqm, nblk=nblk):
                pe.wait(s, tqm)
                for c in range(nblk):
                    ss.inc(s.activation(out=hqT_sb[:, c, 0:128],
                                        in_=t2_ps[:, c, 0:128],
                                        func=AF.Identity))
            blk.scalar(f)
            state_c["prev_hq"] = ss.n
            hq_ev = ss.n

            for c in range(nblk):
                nk2 = (pn0 // 128) + c
                wp = state["wdn"] % 2
                war_mm = state["wdn_mm"].get(state["wdn"] - 2, 0)

                def f(sync, wp=wp, nk2=nk2, war_mm=war_mm,
                      sdq_all=sdq.n):
                    sdq.wait(sync, sdq_all)
                    pe.wait(sync, war_mm)
                    wlp[wp].inc(sync.dma_start(
                        out=wdb[:, wp, :],
                        in_=wdqnat[:, nk2 * 128:(nk2 + 1) * 128],
                        transpose=True), 16)
                blk.sync(f)
                wdm = wlp[wp].n

                def f(t, pci=pci, c=c, nblk=nblk, wp=wp, hq_ev=hq_ev,
                      wdm=wdm, wq_s=wq_done_s,
                      war_o=(o_marks[-1] if o_marks else 0)):
                    ss.wait(t, max(hq_ev, wq_s))
                    wlp[wp].wait(t, wdm)
                    if pci == 0 and c == 0:
                        sv.wait(t, war_o)
                    instr = None
                    for sp in range(2):
                        instr = t.matmul(d_ps[:, sp, :],
                                         hqT_sb[:, c, 0:128],
                                         wdb[:, wp, sp * 512:(sp + 1) * 512],
                                         start=(pci == 0 and c == 0),
                                         stop=(pci == NCH - 1 and
                                               c == nblk - 1))
                    pe.inc(instr)
                blk.tensor(f)
                state["wdn_mm"][state["wdn"]] = pe.n
                state["wdn"] += 1

            if pci == NCH - 1:
                dn_mm = pe.n
                slot = pk % 2

                def f(v, pk=pk, slot=slot, dn_mm=dn_mm, ld_k=ld_ks[pk]):
                    pe.wait(v, dn_mm)
                    ldx[pk % 2].wait(v, ld_k)
                    if pk >= 2:
                        stx[pk % 2].wait(v, 16 * ((pk - 2) // 2 + 1))
                    for sp in range(2):
                        sv.inc(v.tensor_tensor(
                            out=obuf[:, slot, sp * 512:(sp + 1) * 512],
                            in0=xrbuf[:, slot, sp * 512:(sp + 1) * 512],
                            in1=d_ps[:, sp, :], op=OP.add))
                blk.vector(f)
                v_out[pk] = sv.n
                o_marks.append(sv.n)

                def f(sync, pk=pk, slot=slot):
                    sv.wait(sync, v_out[pk])
                    stx[pk % 2].inc(sync.dma_start(
                        out=xout[pk * 128:(pk + 1) * 128, :],
                        in_=obuf[:, slot, :]), 16)
                blk.sync(f)

        for k in range(TT):
            slot = k % 2

            def f(sync, k=k, slot=slot):
                if k >= 2:
                    sv.wait(sync, v_out[k - 2])
                ldx[k % 2].inc(sync.dma_start(out=xrbuf[:, slot, :],
                                              in_=xin[k * 128:(k + 1) * 128, :]), 16)
            blk.sync(f)
            ld_ks[k] = ldx[k % 2].n

            n0 = 0
            for ci, csz in enumerate(chunk_sizes):
                ncol = slice(n0, n0 + csz)

                def f(t, k=k, ci=ci, csz=csz, ncol=ncol, war=prev_sg):
                    ss.wait(t, max(xqT_done, wq_done_s))
                    sv.wait(t, max(war, wq_done_v))
                    instr = None
                    for c in range(KC):
                        instr = t.matmul(g_ps[:, 0:csz],
                                         xqT[:, c, k * 128:(k + 1) * 128],
                                         wgq[:, c, ncol],
                                         start=(c == 0), stop=(c == KC - 1))
                    pe.inc(instr)
                    for c in range(KC):
                        instr = t.matmul(u_ps[:, 0:csz],
                                         xqT[:, c, k * 128:(k + 1) * 128],
                                         wuq[:, c, ncol],
                                         start=(c == 0), stop=(c == KC - 1))
                    pe.inc(instr)
                blk.tensor(f)
                gm_mm = pe.n

                def f(s, csz=csz, gm_mm=gm_mm):
                    pe.wait(s, gm_mm)
                    ss.inc(s.activation(out=sg[:, 0:csz], in_=g_ps[:, 0:csz],
                                        func=AF.Silu))
                blk.scalar(f)
                silu_m = ss.n

                def f(v, csz=csz, silu_m=silu_m):
                    ss.wait(v, silu_m)
                    sv.inc(v.tensor_tensor(out=hbuf[:, 0:csz], in0=sg[:, 0:csz],
                                           in1=u_ps[:, 0:csz], op=OP.mult))
                blk.vector(f)
                prev_sg = sv.n

                qoff = (state["qn2"] % 2) * 512
                state["qn2"] += 1
                qm = emit_quant(hbuf[:, 0:csz], csz, out_off=qoff)
                if pend[0] is not None:
                    emit_tail()
                pend[0] = (k, ci, csz, qm, qoff, n0)
                n0 += csz
        emit_tail()
        es2.close()

        if final:
            def f(g):
                stx[0].wait(g, stx[0].n)
                stx[1].wait(g, stx[1].n)
            blk.gpsimd(f)
    return [(s, s.n) for s in (ldx[0], ldx[1], stx[0], stx[1], wlp[0],
                               wlp[1], sdq, pe, sv, ss)]


def run_l0_ffn(x_full, w_gate, w_up, w_down):
    import ml_dtypes
    nc = build_l0_ffn()
    shared = dict(
        wg=np.ascontiguousarray(w_gate, np.float32),
        wu=np.ascontiguousarray(w_up, np.float32),
        wd=np.ascontiguousarray(w_down, np.float32),
        identb=np.eye(128, dtype=ml_dtypes.bfloat16),
    )
    in_maps = []
    for b in range(NCORES):
        m = dict(shared)
        m["xin"] = np.ascontiguousarray(x_full[b], np.float32)
        in_maps.append(m)
    res = run_bass_kernel_spmd(nc, in_maps, core_ids=list(range(NCORES)))
    return np.stack([r["xout"] for r in res.results], axis=0)





def build_l1_mamba():
    DI = 2048
    nc = bass.Bass("TRN2", target_bir_lowering=False)
    xin = nc.dram_tensor("xin", [T, D], F32, kind="ExternalInput").ap()
    ipwT = nc.dram_tensor("ipwT", [D, 2 * DI], F32, kind="ExternalInput").ap()
    lng = nc.dram_tensor("lng", [128, KC], F32, kind="ExternalInput").ap()
    lnb = nc.dram_tensor("lnb", [128, KC], F32, kind="ExternalInput").ap()
    convw = nc.dram_tensor("convw", [128, DI // 128, 4], F32, kind="ExternalInput").ap()
    convb = nc.dram_tensor("convb", [128, DI // 128], F32, kind="ExternalInput").ap()
    xpwT = nc.dram_tensor("xpwT", [128, DI // 128, 96], F32, kind="ExternalInput").ap()
    dtpwT = nc.dram_tensor("dtpwT", [64, DI], F32, kind="ExternalInput").ap()
    dtpb = nc.dram_tensor("dtpb", [128, DI // 128], F32, kind="ExternalInput").ap()
    alog = nc.dram_tensor("alog", [128, DI // 128, 16], F32, kind="ExternalInput").ap()
    dpar = nc.dram_tensor("dpar", [128, DI // 128], F32, kind="ExternalInput").ap()
    opwT = nc.dram_tensor("opwT", [DI, D], F32, kind="ExternalInput").ap()
    identb = nc.dram_tensor("identb", [128, 128], BF16, kind="ExternalInput").ap()
    xout = nc.dram_tensor("xout", [T, D], F32, kind="ExternalOutput").ap()
    es_top = contextlib.ExitStack()
    with es_top:
        blk = es_top.enter_context(nc.Block())
        emit_l1_mamba(nc, blk, es_top,
                      dict(xin=xin, ipwT=ipwT, lng=lng, lnb=lnb, convw=convw,
                           convb=convb, xpwT=xpwT, dtpwT=dtpwT, dtpb=dtpb,
                           alog=alog, dpar=dpar, opwT=opwT, identb=identb,
                           xout=xout))
    return nc


def emit_l1_mamba(nc, blk, es_top, aps, pfx="", bar=None, final=True):
    DI = 2048
    DH = DI // 128
    TC = 256
    NCH = T // TC
    NTT = TC // 128
    xin = aps["xin"]; ipwT = aps["ipwT"]; lng = aps["lng"]; lnb = aps["lnb"]
    convw = aps["convw"]; convb = aps["convb"]; xpwT = aps["xpwT"]
    dtpwT = aps["dtpwT"]; dtpb = aps["dtpb"]; alog = aps["alog"]
    dpar = aps["dpar"]; opwT = aps["opwT"]; identb = aps["identb"]
    xout = aps["xout"]
    ipw_bf = nc.dram_tensor(pfx + "ipw_bf", [D, 2 * DI], BF16, kind="Internal").ap()
    bc_dram = nc.dram_tensor(pfx + "bc_dram", [2, 16, TC], F32, kind="Internal").ap()

    es = contextlib.ExitStack()
    with es:
        sb = lambda name, shape, dt: es.enter_context(
            nc.sbuf_tensor(pfx + name, shape, dt))

        xnT_ch = sb("xnT_ch", [128, KC, TC], BF16)
        xc_ch = sb("xc_ch", [128, DH, TC], F32)
        dt_ch = sb("dt_ch", [128, DH, TC], F32)
        u_ch = sb("u_ch", [128, DH, TC], F32)
        yst = sb("yst", [128, DH, TC], F32)
        sz_ch = sb("sz_ch", [128, DH, TC], BF16)
        ysz_ch = sb("ysz_ch", [128, DH, TC], BF16)
        mr_ch = sb("mr_ch", [128, NTT, 4], F32)
        opw_sb = sb("opw_sb", [128, DH, D], BF16)
        xpw_sb = sb("xpw_sb", [128, DH, 96], F32)
        dtpw_sb = sb("dtpw_sb", [64, DI], F32)
        A_sb = sb("A_sb", [128, DH, 16], F32)
        cw_sb = sb("cw_sb", [128, DH, 4], F32)
        cb_sb = sb("cb_sb", [128, DH], F32)
        db_sb = sb("db_sb", [128, DH], F32)
        dp_sb = sb("dp_sb", [128, DH], F32)
        g_sb = sb("g_sb", [128, KC], F32)
        b_sb = sb("b_sb", [128, KC], F32)
        bias_e = sb("bias_e", [128, 2 * DH], F32)
        bias_tmp = sb("bias_tmp", [128, DH], F32)
        identb_sb = sb("identb_sb", [128, 128], BF16)
        eps_t = sb("eps_t", [128, 1], F32)
        one_t = sb("one_t", [128, 1], F32)
        es_w = contextlib.ExitStack()
        wstg = es_w.enter_context(nc.sbuf_tensor(pfx + "wstg", [128, 2, DI], F32, side="right"))
        wcvt = es_w.enter_context(nc.sbuf_tensor(pfx + "wcvt", [128, DI], BF16, side="right"))
        ipb = sb("ipb", [128, 2, KC, 128], BF16)
        xbuf = sb("xbuf", [128, 2, D], F32)
        junk = sb("junk", [128, 2, D], F32)
        xn1b = sb("xn1b", [128, 2, D], F32)
        xn2b = sb("xn2b", [128, 2, D], BF16)
        small = sb("small", [128, 32], F32)
        xs_scr = sb("xs_scr", [128, 2, TC + 3], F32)
        cacc = sb("cacc", [128, 2, TC], F32)
        carry3 = sb("carry3", [128, DH, 4], F32)
        dtlow = sb("dtlow", [64, TC], F32)
        bsb = sb("bsb", [16, TC], F32)
        csb = sb("csb", [16, TC], F32)
        bcb = sb("bcb", [128, 2, 2, TC], F32)
        hcar = sb("hcar", [128, 256], F32)
        Et = sb("Et", [128, TC], F32)
        spr = sb("spr", [128, TC], F32)
        udb2 = sb("udb2", [128, 2, TC], F32)
        ht2 = sb("ht2", [128, 2, TC], F32)
        hct2 = sb("hct2", [128, 2, TC], F32)
        ldx = [Sem(es_top, nc, pfx + "ldx0"), Sem(es_top, nc, pfx + "ldx1")]
        stx = [Sem(es_top, nc, pfx + "stx0"), Sem(es_top, nc, pfx + "stx1")]
        wlp = [Sem(es_top, nc, pfx + "wl0"), Sem(es_top, nc, pfx + "wl1")]
        bcp = [Sem(es_top, nc, pfx + "bc0"), Sem(es_top, nc, pfx + "bc1")]
        sbc = Sem(es_top, nc, pfx + "sbc")
        ldc = Sem(es_top, nc, pfx + "ldc")
        pe = Sem(es_top, nc, pfx + "pe")
        sv = Sem(es_top, nc, pfx + "sv")
        ss = Sem(es_top, nc, pfx + "ss")
        if bar:
            def fbar(eng):
                for _s, _c in bar:
                    _s.wait(eng, _c)
            blk.sync(fbar)
            blk.scalar(fbar)
            blk.vector(fbar)
            blk.tensor(fbar)

        def f(sync):
            for o, i in [
                (identb_sb[:], identb),
                (A_sb[:], alog), (cw_sb[:], convw), (cb_sb[:], convb),
                (db_sb[:], dtpb), (dp_sb[:], dpar),
                (g_sb[:], lng), (b_sb[:], lnb),
                (xpw_sb[:], xpwT),
                (dtpw_sb[:], dtpwT),
            ]:
                ldc.inc(sync.dma_start(out=o, in_=i), 16)
        blk.sync(f)
        ldc_all = ldc.n

        def f(v):
            v.memset(eps_t[:], 1e-5)
            v.memset(one_t[:], 1.0)
            v.memset(hcar[:], 0.0)
            v.memset(carry3[:], 0.0)
            ldc.wait(v, ldc_all)
            sv.inc(v.tensor_scalar(out=small[:, 0:1], in0=eps_t[:, 0:1],
                                   scalar1=1.0, scalar2=None, op0=OP.mult))
        blk.vector(f)
        cst_v = sv.n

        def f(s):
            ldc.wait(s, ldc_all)
            ss.inc(s.activation(out=A_sb[:], in_=A_sb[:], func=AF.Exp))
        blk.scalar(f)
        aexp = ss.n

        def f(v, aexp=aexp):
            ss.wait(v, aexp)
            sv.inc(v.tensor_scalar(out=A_sb[:], in0=A_sb[:], scalar1=-1.0,
                                   scalar2=None, op0=OP.mult))
        blk.vector(f)
        a_done = sv.n

        es1 = contextlib.ExitStack()
        bias_ps = es1.enter_context(nc.psum_tensor(pfx + "bias_ps", [128, 512], F32))
        wcv = []
        state_b = {}
        for kc in range(KC):
            for hh in range(2):
                i = len(wcv)
                p = i % 2
                war = wcv[i - 2] if i >= 2 else 0

                def f(sync, kc=kc, hh=hh, p=p, war=war):
                    ss.wait(sync, war)
                    wlp[p].inc(sync.dma_start(
                        out=wstg[0:64, p, :],
                        in_=ipwT[kc * 128:kc * 128 + 64,
                                 hh * DI:(hh + 1) * DI]), 16)
                blk.sync(f)

                def f(s, kc=kc, hh=hh, p=p, war=war):
                    ss.wait(s, war)
                    wlp[p].inc(s.dma_start(
                        out=wstg[64:128, p, :],
                        in_=ipwT[kc * 128 + 64:(kc + 1) * 128,
                                 hh * DI:(hh + 1) * DI]), 16)
                blk.scalar(f)
                dm = wlp[p].n

                def f(v, kc=kc, p=p, dm=dm):
                    wlp[p].wait(v, dm)
                    sv.inc(v.tensor_scalar(out=wstg[:, p, :], in0=wstg[:, p, :],
                                           scalar1=g_sb[:, kc:kc + 1],
                                           scalar2=None, op0=OP.mult))
                blk.vector(f)
                sc_m = sv.n

                def f(t, kc=kc, hh=hh, p=p, sc_m=sc_m,
                      war=state_b.get("bias_ev", 0)):
                    sv.wait(t, sc_m)
                    ss.wait(t, war)
                    for et in range(DH):
                        pe.inc(t.matmul(bias_ps[:, et:et + 1],
                                        wstg[:, p, et * 128:(et + 1) * 128],
                                        b_sb[:, kc:kc + 1],
                                        start=True, stop=True))
                blk.tensor(f)
                pe_m = pe.n

                def f(s, hh=hh, pe_m=pe_m):
                    pe.wait(s, pe_m)
                    ss.inc(s.activation(out=bias_tmp[:, 0:DH],
                                        in_=bias_ps[:, 0:DH], func=AF.Identity))
                blk.scalar(f)
                bias_ev = ss.n
                state_b["bias_ev"] = bias_ev

                def f(v, kc=kc, hh=hh, bias_ev=bias_ev):
                    ss.wait(v, bias_ev)
                    if kc == 0:
                        sv.inc(v.tensor_scalar(
                            out=bias_e[:, hh * DH:(hh + 1) * DH],
                            in0=bias_tmp[:, 0:DH], scalar1=1.0, scalar2=None,
                            op0=OP.mult))
                    else:
                        sv.inc(v.tensor_tensor(
                            out=bias_e[:, hh * DH:(hh + 1) * DH],
                            in0=bias_e[:, hh * DH:(hh + 1) * DH],
                            in1=bias_tmp[:, 0:DH], op=OP.add))
                    sv.wait(v, sv.n)
                blk.vector(f)

                def f(s, p=p, sc_m=sc_m, pe_m=pe_m, kc=kc, hh=hh,
                      wst=state_b.get("wcvt_store", 0)):
                    sv.wait(s, sc_m)
                    pe.wait(s, pe_m)
                    sbc.wait(s, wst)
                    ss.inc(s.activation(out=wcvt[:, 0:DI],
                                        in_=wstg[:, p, :], func=AF.Identity))
                blk.scalar(f)
                cv_m = ss.n

                def f(sync, kc=kc, hh=hh, p=p, cv_m=cv_m):
                    ss.wait(sync, cv_m)
                    sbc.inc(sync.dma_start(
                        out=ipw_bf[kc * 128:(kc + 1) * 128, hh * DI:(hh + 1) * DI],
                        in_=wcvt[:, 0:DI]), 16)
                blk.sync(f)
                wcv.append(ss.n)
                state_b["wcvt_store"] = sbc.n
        ipw_stored = sbc.n

        bias_done_v = sv.n
        es1.close()

        ocv = [wcv[-1], wcv[-1]]
        for rt in range(DH):
            p = rt % 2
            war = ocv[rt]

            def f(sync, rt=rt, p=p, war=war):
                ss.wait(sync, war)
                wlp[p].inc(sync.dma_start(
                    out=wstg[0:64, p, 0:D],
                    in_=opwT[rt * 128:rt * 128 + 64, :]), 16)
            blk.sync(f)

            def f(s, rt=rt, p=p, war=war):
                ss.wait(s, war)
                wlp[p].inc(s.dma_start(
                    out=wstg[64:128, p, 0:D],
                    in_=opwT[rt * 128 + 64:(rt + 1) * 128, :]), 16)
            blk.scalar(f)
            dm = wlp[p].n

            def f(s, rt=rt, p=p, dm=dm):
                wlp[p].wait(s, dm)
                ss.inc(s.activation(out=opw_sb[:, rt, :], in_=wstg[:, p, 0:D],
                                    func=AF.Identity))
            blk.scalar(f)
            ocv.append(ss.n)
        opw_done = ss.n
        es_w.close()
        obuf = es.enter_context(nc.sbuf_tensor(pfx + "obuf", [128, 2, D], F32, side="right"))
        xrbuf = es.enter_context(nc.sbuf_tensor(pfx + "xrbuf", [128, 2, D], F32, side="right"))

        es2 = contextlib.ExitStack()
        tp_ps = es2.enter_context(nc.psum_tensor(pfx + "tp_ps", [128, 2, 4, 256], BF16))
        ip_ps = es2.enter_context(nc.psum_tensor(pfx + "ip_ps", [128, 512], F32))
        xd_ps = es2.enter_context(nc.psum_tensor(pfx + "xd_ps", [128, 512], F32))
        xd2_ps = es2.enter_context(nc.psum_tensor(pfx + "xd2_ps", [128, 512], F32))
        dt_ps = es2.enter_context(nc.psum_tensor(pfx + "dt_ps", [128, 512], F32))
        op_ps = es2.enter_context(nc.psum_tensor(pfx + "op_ps", [128, 2, 512], F32))

        ipb_marks = {}
        state = {"ipn": 0}
        prev_ip_ev = 0
        prev_xn_done = [0] * 2
        ln_v = [0] * (T // 128)
        ln_pe = [0] * (T // 128)
        ln_s = [0] * (T // 128)
        prev_scan_v = 0
        prev_dtev = 0
        prev_bc_store = 0
        repl_hist = [0, 0]
        state_c = {"prev_opmm": 0, "prev_ysz": 0, "prev_bcst": 0}
        v_out = {}

        for ch in range(NCH):
            t0 = ch * TC
            for kt in range(NTT):
                k = ch * NTT + kt
                slot = k % 2
                SB = 8 * slot

                def f(sync, k=k, slot=slot,
                      war=(ln_v[k - 2] if k >= 2 else 0)):
                    sv.wait(sync, war)
                    ldx[slot].inc(sync.dma_start(
                        out=xbuf[:, slot, :],
                        in_=xin[k * 128:(k + 1) * 128, :]), 16)
                blk.sync(f)
                ld_k = ldx[slot].n

                def f(v, k=k, slot=slot, ld_k=ld_k, SB=SB):
                    ldx[slot].wait(v, ld_k)
                    sv.inc(v.tensor_reduce(out=small[:, SB:SB + 1],
                                           in_=xbuf[:, slot, :], axis=AX,
                                           op=OP.add))
                blk.vector(f)
                m1 = sv.n

                def f(s, m1=m1, SB=SB):
                    sv.wait(s, m1)
                    ss.inc(s.activation(out=small[:, SB + 4:SB + 5],
                                        in_=small[:, SB:SB + 1],
                                        func=AF.Identity, scale=1.0 / D))
                blk.scalar(f)
                m2 = ss.n

                def f(v, k=k, slot=slot, m2=m2, SB=SB):
                    ss.wait(v, m2)
                    sv.inc(v.tensor_scalar(out=junk[:, slot, :],
                                           in0=xbuf[:, slot, :],
                                           scalar1=small[:, SB + 4:SB + 5],
                                           scalar2=None, op0=OP.subtract))
                blk.vector(f)
                m3 = sv.n

                def f(s, slot=slot, m3=m3, SB=SB):
                    sv.wait(s, m3)
                    ss.inc(s.activation(out=xn1b[:, slot, :],
                                        in_=junk[:, slot, :], func=AF.Square,
                                        accum_out=small[:, SB + 5:SB + 6]))
                    ss.wait(s, ss.n)
                    ss.inc(s.activation(out=small[:, SB + 7:SB + 8],
                                        in_=small[:, SB + 5:SB + 6],
                                        func=AF.Ln, scale=1.0 / D,
                                        bias=eps_t[:, 0:1]))
                    ss.wait(s, ss.n)
                    ss.inc(s.activation(out=small[:, SB + 6:SB + 7],
                                        in_=small[:, SB + 7:SB + 8],
                                        func=AF.Exp, scale=-0.5))
                blk.scalar(f)
                m4 = ss.n

                def f(v, k=k, kt=kt, slot=slot, m4=m4, SB=SB):
                    ss.wait(v, m4)
                    v.tensor_scalar(out=mr_ch[:, kt, 0:1],
                                    in0=small[:, SB + 4:SB + 5], scalar1=1.0,
                                    scalar2=None, op0=OP.mult)
                    v.tensor_scalar(out=mr_ch[:, kt, 1:2],
                                    in0=small[:, SB + 6:SB + 7], scalar1=1.0,
                                    scalar2=None, op0=OP.mult)
                    sv.inc(v.tensor_scalar(out=xn1b[:, slot, :],
                                           in0=junk[:, slot, :],
                                           scalar1=small[:, SB + 6:SB + 7],
                                           scalar2=None, op0=OP.mult))
                blk.vector(f)
                ln1_done = sv.n

                def f(v, slot=slot, ln1_done=ln1_done, SB=SB):
                    sv.wait(v, ln1_done)
                    sv.inc(v.tensor_reduce(out=small[:, SB + 2:SB + 3],
                                           in_=xn1b[:, slot, :], axis=AX,
                                           op=OP.add))
                blk.vector(f)
                n1 = sv.n

                def f(s, n1=n1, SB=SB):
                    sv.wait(s, n1)
                    ss.inc(s.activation(out=small[:, SB + 7:SB + 8],
                                        in_=small[:, SB + 2:SB + 3],
                                        func=AF.Identity, scale=1.0 / D))
                blk.scalar(f)
                n2 = ss.n

                def f(v, slot=slot, n2=n2, SB=SB):
                    ss.wait(v, n2)
                    sv.inc(v.tensor_scalar(out=junk[:, slot, :],
                                           in0=xn1b[:, slot, :],
                                           scalar1=small[:, SB + 7:SB + 8],
                                           scalar2=None, op0=OP.subtract))
                blk.vector(f)
                n3 = sv.n

                def f(s, slot=slot, n3=n3, SB=SB):
                    sv.wait(s, n3)
                    ss.inc(s.activation(out=xn2b[:, slot, :],
                                        in_=junk[:, slot, :], func=AF.Square,
                                        accum_out=small[:, SB + 5:SB + 6]))
                    ss.wait(s, ss.n)
                    ss.inc(s.activation(out=small[:, SB + 7:SB + 8],
                                        in_=small[:, SB + 5:SB + 6],
                                        func=AF.Ln, scale=1.0 / D,
                                        bias=eps_t[:, 0:1]))
                    ss.wait(s, ss.n)
                    ss.inc(s.activation(out=small[:, SB + 6:SB + 7],
                                        in_=small[:, SB + 7:SB + 8],
                                        func=AF.Exp, scale=-0.5))
                blk.scalar(f)
                n4 = ss.n

                def f(v, k=k, slot=slot, n4=n4, SB=SB,
                      war=(ln_pe[k - 2] if k >= 2 else 0)):
                    ss.wait(v, n4)
                    pe.wait(v, war)
                    sv.inc(v.tensor_scalar(out=xn2b[:, slot, :],
                                           in0=junk[:, slot, :],
                                           scalar1=small[:, SB + 6:SB + 7],
                                           scalar2=None, op0=OP.mult))
                blk.vector(f)
                ln_v[k] = sv.n

                def f(t, k=k, slot=slot, war=(ln_s[k - 1] if k >= 1 else 0)):
                    sv.wait(t, ln_v[k])
                    ss.wait(t, war)
                    for c in range(KC):
                        pe.inc(t.transpose(tp_ps[:, c // 4, c % 4, 0:128],
                                           xn2b[:, slot, c * 128:(c + 1) * 128],
                                           identb_sb[:]))
                blk.tensor(f)
                ln_pe[k] = pe.n

                def f(s, k=k, kt=kt):
                    pe.wait(s, ln_pe[k])
                    for c in range(KC):
                        ss.inc(s.activation(
                            out=xnT_ch[:, c, kt * 128:(kt + 1) * 128],
                            in_=tp_ps[:, c // 4, c % 4, 0:128],
                            func=AF.Identity))
                blk.scalar(f)
                ln_s[k] = ss.n
            xnT_ready = ss.n

            conv_evs = []
            conv_vm = []
            conv_sil = []
            evac_v = []
            sil_pend = []
            for et in range(2 * DH):
                ipn = state["ipn"]
                p = ipn % 2
                war_mm = ipb_marks.get(ipn - 2, 0)

                def f(sync, et=et, p=p, war_mm=war_mm, ipst=ipw_stored):
                    sbc.wait(sync, ipst)
                    pe.wait(sync, war_mm)
                    wlp[p].inc(sync.dma_start(
                        out=ipb[:, p, :, :],
                        in_=ipw_bf.rearrange("(kc p) e -> p kc e", p=128)[
                            :, :, et * 128:(et + 1) * 128]), 16)
                blk.sync(f)
                dm = wlp[p].n

                def f(t, et=et, p=p, dm=dm, xr=xnT_ready,
                      war=(evac_v[-2] if len(evac_v) >= 2 else 0),
                      bd=bias_done_v, wv=state_c["prev_ysz"]):
                    wlp[p].wait(t, dm)
                    ss.wait(t, xr)
                    sv.wait(t, max(bd, wv, war))
                    ips = ip_ps if et % 2 == 0 else xd_ps
                    instr = None
                    for c in range(KC):
                        instr = t.matmul(ips[:, 0:TC], ipb[:, p, c, :],
                                         xnT_ch[:, c, :],
                                         start=(c == 0), stop=(c == KC - 1))
                    pe.inc(instr)
                blk.tensor(f)
                ipb_marks[ipn] = pe.n
                state["ipn"] += 1
                mm_m = pe.n

                if et < DH:
                    cs = et % 2

                    def f(v, ch=ch, et=et, cs=cs, mm_m=mm_m,
                          war=(conv_sil[-2] if len(conv_sil) >= 2 else 0)):
                        pe.wait(v, mm_m)
                        ss.wait(v, war)
                        ips = ip_ps if et % 2 == 0 else xd_ps
                        sv.inc(v.tensor_tensor(
                            out=xs_scr[:, cs, 3:3 + TC], in0=ips[:, 0:TC],
                            in1=bias_e[:, et:et + 1].to_broadcast([128, TC]),
                            op=OP.add))
                        evac_v.append(sv.n)
                        sv.wait(v, sv.n)
                        sv.inc(v.tensor_scalar(out=xs_scr[:, cs, 0:3],
                                               in0=carry3[:, et, 0:3],
                                               scalar1=1.0, scalar2=None,
                                               op0=OP.mult))
                        sv.wait(v, sv.n)
                        sv.inc(v.tensor_scalar(out=carry3[:, et, 0:3],
                                               in0=xs_scr[:, cs, TC:TC + 3],
                                               scalar1=1.0, scalar2=None,
                                               op0=OP.mult))
                        sv.inc(v.scalar_tensor_tensor(
                            out=cacc[:, cs, :], in0=xs_scr[:, cs, 0:TC],
                            scalar=cw_sb[:, et, 0:1],
                            in1=cb_sb[:, et:et + 1].to_broadcast([128, TC]),
                            op0=OP.mult, op1=OP.add))
                        sv.wait(v, sv.n)
                        for j in range(1, 4):
                            sv.inc(v.scalar_tensor_tensor(
                                out=cacc[:, cs, :], in0=xs_scr[:, cs, j:j + TC],
                                scalar=cw_sb[:, et, j:j + 1], in1=cacc[:, cs, :],
                                op0=OP.mult, op1=OP.add))
                            sv.wait(v, sv.n)
                    blk.vector(f)
                    cv_m = sv.n
                    conv_vm.append(cv_m)
                    sil_pend.append((et, cs, cv_m, True))
                    if len(sil_pend) == 2:
                        def f(s, pend=tuple(sil_pend)):
                            for pet, pcs, pcv, isx in pend:
                                sv.wait(s, pcv)
                                if isx:
                                    ss.inc(s.activation(out=xc_ch[:, pet, :],
                                                        in_=cacc[:, pcs, :],
                                                        func=AF.Silu))
                                else:
                                    ss.inc(s.activation(
                                        out=sz_ch[:, pet - DH, :],
                                        in_=cacc[:, pcs, :], func=AF.Silu))
                                conv_sil.append(ss.n)
                                conv_evs.append(ss.n)
                        blk.scalar(f)
                        sil_pend.clear()
                else:
                    cs = et % 2

                    def f(v, et=et, cs=cs, mm_m=mm_m,
                          war=(conv_sil[-2] if len(conv_sil) >= 2 else 0)):
                        pe.wait(v, mm_m)
                        ss.wait(v, war)
                        ips = ip_ps if et % 2 == 0 else xd_ps
                        sv.inc(v.tensor_tensor(
                            out=cacc[:, cs, :], in0=ips[:, 0:TC],
                            in1=bias_e[:, et:et + 1].to_broadcast([128, TC]),
                            op=OP.add))
                    blk.vector(f)
                    zv_m = sv.n
                    evac_v.append(zv_m)
                    sil_pend.append((et, cs, zv_m, False))
                    if len(sil_pend) == 2:
                        def f(s, pend=tuple(sil_pend)):
                            for pet, pcs, pcv, isx in pend:
                                sv.wait(s, pcv)
                                if isx:
                                    ss.inc(s.activation(out=xc_ch[:, pet, :],
                                                        in_=cacc[:, pcs, :],
                                                        func=AF.Silu))
                                else:
                                    ss.inc(s.activation(
                                        out=sz_ch[:, pet - DH, :],
                                        in_=cacc[:, pcs, :], func=AF.Silu))
                                conv_sil.append(ss.n)
                                conv_evs.append(ss.n)
                        blk.scalar(f)
                        sil_pend.clear()
            xc_done = ss.n

            def f(t, xc_done=xc_done, war=prev_dtev):
                ss.wait(t, max(xc_done, war))
                instr = None
                for c in range(DH):
                    instr = t.matmul(xd_ps[0:80, 0:TC], xpw_sb[:, c, 0:80],
                                     xc_ch[:, c, :],
                                     start=(c == 0), stop=(c == DH - 1))
                pe.inc(instr)
                instr = None
                for c in range(DH):
                    instr = t.matmul(xd2_ps[0:16, 0:TC], xpw_sb[:, c, 80:96],
                                     xc_ch[:, c, :],
                                     start=(c == 0), stop=(c == DH - 1))
                pe.inc(instr)
            blk.tensor(f)
            xd_mm = pe.n

            def f(s, xd_mm=xd_mm, wb=state_c["prev_bcst"]):
                pe.wait(s, xd_mm)
                sbc.wait(s, wb)
                ss.inc(s.activation(out=dtlow[:, :], in_=xd_ps[0:64, 0:TC],
                                    func=AF.Identity))
                ss.inc(s.activation(out=bsb[0:16, :], in_=xd_ps[64:80, 0:TC],
                                    func=AF.Identity))
                ss.inc(s.activation(out=csb[0:16, :], in_=xd2_ps[0:16, 0:TC],
                                    func=AF.Identity))
            blk.scalar(f)
            xd_ev = ss.n

            def f(sync, ch=ch, xd_ev=xd_ev):
                ss.wait(sync, xd_ev)
                sbc.inc(sync.dma_start(out=bc_dram[0], in_=bsb[0:16, :]), 16)
                sbc.inc(sync.dma_start(out=bc_dram[1], in_=csb[0:16, :]), 16)
            blk.sync(f)
            bc_store = sbc.n
            state_c["prev_bcst"] = bc_store

            dt_evs = []
            dtv_marks = [0]
            for dhi in range(DH):
                def f(t, dhi=dhi, xd_ev=xd_ev,
                      war=(dt_evs[-1] if dt_evs else prev_dtev)):
                    ss.wait(t, max(xd_ev, war))
                    pe.inc(t.matmul(dt_ps[:, 0:TC],
                                    dtpw_sb[:, dhi * 128:(dhi + 1) * 128],
                                    dtlow[:, :], start=True, stop=True))
                blk.tensor(f)
                dtm = pe.n

                def f(s, dhi=dhi, dtm=dtm, wv=dtv_marks[-1]):
                    pe.wait(s, dtm)
                    sv.wait(s, wv)
                    ss.inc(s.activation(out=Et[:, :], in_=dt_ps[:, 0:TC],
                                        func=AF.Abs, bias=db_sb[:, dhi:dhi + 1]))
                    ss.wait(s, ss.n)
                    ss.inc(s.activation(out=spr[:, :], in_=Et[:, :],
                                        func=AF.Exp, scale=-1.0))
                    ss.wait(s, ss.n)
                    ss.inc(s.activation(out=Et[:, :], in_=spr[:, :],
                                        func=AF.Ln, bias=one_t[:, 0:1]))
                    ss.wait(s, ss.n)
                    ss.inc(s.activation(out=spr[:, :], in_=dt_ps[:, 0:TC],
                                        func=AF.Relu, bias=db_sb[:, dhi:dhi + 1]))
                blk.scalar(f)
                sp_m = ss.n
                dt_evs.append(ss.n)

                def f(v, dhi=dhi, sp_m=sp_m):
                    ss.wait(v, sp_m)
                    sv.inc(v.tensor_tensor(out=dt_ch[:, dhi, :], in0=Et[:, :],
                                           in1=spr[:, :], op=OP.add))
                    sv.wait(v, sv.n)
                    sv.inc(v.tensor_tensor(out=u_ch[:, dhi, :],
                                           in0=dt_ch[:, dhi, :],
                                           in1=xc_ch[:, dhi, :], op=OP.mult))
                blk.vector(f)
                dtv_marks.append(sv.n)
            dt_done = sv.n
            prev_dtev = ss.n

            eta_t = [xn1b, junk]
            eta_sm = {}
            pq_scan = [0, 0]
            eta_war = [dt_done, dt_done]

            def emit_eta(n):
                def f(s, n=n, war=(eta_war[0], eta_war[1])):
                    for hh in range(2):
                        sv.wait(s, war[hh])
                        for di in range(8):
                            dhi = hh * 8 + di
                            ev = eta_t[hh][:, di // 4,
                                           (di % 4) * TC:(di % 4 + 1) * TC]
                            ss.inc(s.activation(out=ev, in_=dt_ch[:, dhi, :],
                                                func=AF.Exp,
                                                scale=A_sb[:, dhi, n:n + 1]))
                        eta_sm[(n, hh)] = ss.n
                blk.scalar(f)

            emit_eta(0)
            for n in range(16):
                p = n % 2
                war_r = repl_hist[-2]

                def f(sync, ch=ch, n=n, p=p, bc_store=bc_store, war_r=war_r):
                    sbc.wait(sync, bc_store)
                    sv.wait(sync, war_r)
                    bcp[p].inc(sync.dma_start(
                        out=bcb[:, p, 0, :],
                        in_=bc_dram[0, n, :].unsqueeze(0).to_broadcast([128, TC])),
                        16)
                    bcp[p].inc(sync.dma_start(
                        out=bcb[:, p, 1, :],
                        in_=bc_dram[1, n, :].unsqueeze(0).to_broadcast([128, TC])),
                        16)
                blk.sync(f)
                repl_m = bcp[p].n

                for hh in range(2):
                    for d0 in range(0, 8, 2):
                        def f(v, n=n, hh=hh, d0=d0, p=p, repl_m=repl_m):
                            if d0 == 0:
                                ss.wait(v, eta_sm[(n, hh)])
                                bcp[p].wait(v, repl_m)
                            pair = []
                            for di in (d0, d0 + 1):
                                dhi = hh * 8 + di
                                q = dhi % 2
                                idx = dhi * 16 + n
                                ev = eta_t[hh][:, di // 4,
                                               (di % 4) * TC:(di % 4 + 1) * TC]
                                pair.append((dhi, q, idx, ev))
                            mu = []
                            for dhi, q, idx, ev in pair:
                                sv.wait(v, pq_scan[q])
                                sv.inc(v.tensor_tensor(out=udb2[:, q, :],
                                                       in0=u_ch[:, dhi, :],
                                                       in1=bcb[:, p, 0, :],
                                                       op=OP.mult))
                                mu.append(sv.n)
                            msc = []
                            for (dhi, q, idx, ev), m in zip(pair, mu):
                                sv.wait(v, m)
                                sv.inc(v.tensor_tensor_scan(
                                    out=ht2[:, q, :], data0=ev,
                                    data1=udb2[:, q, :],
                                    initial=hcar[:, idx:idx + 1],
                                    op0=OP.mult, op1=OP.add))
                                msc.append(sv.n)
                                pq_scan[q] = sv.n
                                eta_war[hh] = sv.n
                            mh = []
                            for (dhi, q, idx, ev), m in zip(pair, msc):
                                sv.wait(v, m)
                                sv.inc(v.tensor_scalar(out=hcar[:, idx:idx + 1],
                                                       in0=ht2[:, q, TC - 1:TC],
                                                       scalar1=1.0, scalar2=None,
                                                       op0=OP.mult))
                                sv.inc(v.tensor_tensor(out=hct2[:, q, :],
                                                       in0=ht2[:, q, :],
                                                       in1=bcb[:, p, 1, :],
                                                       op=OP.mult))
                                mh.append(sv.n)
                            for (dhi, q, idx, ev), m in zip(pair, mh):
                                sv.wait(v, m)
                                if n == 0:
                                    sv.inc(v.tensor_scalar(out=yst[:, dhi, :],
                                                           in0=hct2[:, q, :],
                                                           scalar1=1.0,
                                                           scalar2=None,
                                                           op0=OP.mult))
                                else:
                                    sv.inc(v.tensor_tensor(out=yst[:, dhi, :],
                                                           in0=yst[:, dhi, :],
                                                           in1=hct2[:, q, :],
                                                           op=OP.add))
                        blk.vector(f)
                if n + 1 < 16:
                    emit_eta(n + 1)
                repl_hist.append(sv.n)
            prev_scan_v = sv.n

            fin_v = []
            for dhi in range(DH):
                def f(v, dhi=dhi, war=prev_scan_v, wpe=state_c["prev_opmm"]):
                    sv.wait(v, war)
                    pe.wait(v, wpe)
                    sv.inc(v.scalar_tensor_tensor(out=yst[:, dhi, :],
                                                  in0=xc_ch[:, dhi, :],
                                                  scalar=dp_sb[:, dhi:dhi + 1],
                                                  in1=yst[:, dhi, :],
                                                  op0=OP.mult, op1=OP.add))
                    sv.wait(v, sv.n)
                    sv.inc(v.tensor_tensor(out=ysz_ch[:, dhi, :],
                                           in0=yst[:, dhi, :],
                                           in1=sz_ch[:, dhi, :], op=OP.mult))
                blk.vector(f)
                fin_v.append(sv.n)
            ysz_done = sv.n

            for kt in range(NTT):
                k = ch * NTT + kt
                slot = k % 2

                def f(sync, k=k, slot=slot,
                      war=v_out.get(k - 2, 0)):
                    sv.wait(sync, war)
                    ldx[slot].inc(sync.dma_start(
                        out=xrbuf[:, slot, :],
                        in_=xin[k * 128:(k + 1) * 128, :]), 16)
                blk.sync(f)
                ld_k = ldx[slot].n

                for sp in range(2):
                    def f(t, k=k, sp=sp, kt=kt, ysz_done=ysz_done, opd=opw_done,
                          war=v_out.get(k - 1, 0)):
                        sv.wait(t, max(ysz_done, war))
                        ss.wait(t, opd)
                        instr = None
                        for c in range(DH):
                            instr = t.matmul(op_ps[:, sp, :],
                                             ysz_ch[:, c, kt * 128:(kt + 1) * 128],
                                             opw_sb[:, c, sp * 512:(sp + 1) * 512],
                                             start=(c == 0), stop=(c == DH - 1))
                        pe.inc(instr)
                    blk.tensor(f)
                op_mm = pe.n

                def f(v, k=k, kt=kt, slot=slot, op_mm=op_mm, ld_k=ld_k):
                    pe.wait(v, op_mm)
                    ldx[slot].wait(v, ld_k)
                    if k >= 2:
                        stx[slot].wait(v, 16 * ((k - 2) // 2 + 1))
                    sv.inc(v.tensor_scalar(out=junk[:, slot, :],
                                           in0=xrbuf[:, slot, :],
                                           scalar1=mr_ch[:, kt, 0:1],
                                           scalar2=mr_ch[:, kt, 1:2],
                                           op0=OP.subtract, op1=OP.mult))
                    sv.wait(v, sv.n)
                    sv.inc(v.tensor_tensor(out=junk[:, slot, :],
                                           in0=junk[:, slot, :],
                                           in1=xrbuf[:, slot, :], op=OP.add))
                    sv.wait(v, sv.n)
                    for sp in range(2):
                        sv.inc(v.tensor_tensor(
                            out=obuf[:, slot, sp * 512:(sp + 1) * 512],
                            in0=junk[:, slot, sp * 512:(sp + 1) * 512],
                            in1=op_ps[:, sp, :], op=OP.add))
                blk.vector(f)
                v_out[k] = sv.n

                def f(sync, k=k, slot=slot):
                    sv.wait(sync, v_out[k])
                    stx[slot].inc(sync.dma_start(
                        out=xout[k * 128:(k + 1) * 128, :],
                        in_=obuf[:, slot, :]), 16)
                blk.sync(f)
            state_c["prev_opmm"] = pe.n
            state_c["prev_ysz"] = ysz_done
        es2.close()

        if final:
            def f(g):
                stx[0].wait(g, stx[0].n)
                stx[1].wait(g, stx[1].n)
            blk.gpsimd(f)
    return [(s, s.n) for s in (ldx[0], ldx[1], stx[0], stx[1], wlp[0],
                               wlp[1], bcp[0], bcp[1], sbc, ldc, pe, sv, ss)]


def run_l1_mamba(x_full, ln_gamma, ln_beta, in_proj_w, conv_w, conv_b, x_proj_w,
                 dt_proj_w, dt_proj_b, A_log, D_param, out_proj_w):
    import ml_dtypes
    nc = build_l1_mamba()
    r1 = lambda a: np.ascontiguousarray(
        np.asarray(a, np.float32).reshape(-1, 128).T)
    r2 = lambda a, w: np.ascontiguousarray(
        np.asarray(a, np.float32).reshape(-1, 128, w).transpose(1, 0, 2))
    shared = dict(
        ipwT=np.ascontiguousarray(in_proj_w.T, np.float32),
        lng=r1(ln_gamma), lnb=r1(ln_beta),
        convw=r2(conv_w, 4), convb=r1(conv_b),
        xpwT=r2(x_proj_w.T, 96),
        dtpwT=np.ascontiguousarray(dt_proj_w.T, np.float32),
        dtpb=r1(dt_proj_b),
        alog=r2(A_log, 16), dpar=r1(D_param),
        opwT=np.ascontiguousarray(out_proj_w.T, np.float32),
        identb=np.eye(128, dtype=ml_dtypes.bfloat16),
    )
    in_maps = []
    for b in range(NCORES):
        m = dict(shared)
        m["xin"] = np.ascontiguousarray(x_full[b], np.float32)
        in_maps.append(m)
    res = run_bass_kernel_spmd(nc, in_maps, core_ids=list(range(NCORES)))
    return np.stack([r["xout"] for r in res.results], axis=0)





def build_l1_stats():
    FQ = 2048
    nc = bass.Bass("TRN2", target_bir_lowering=False)
    xin = nc.dram_tensor("xin", [T, D], F32, kind="ExternalInput").ap()
    wq = nc.dram_tensor("wq", [D, FQ], F32, kind="ExternalInput").ap()
    bq = nc.dram_tensor("bq", [128, 16], F32, kind="ExternalInput").ap()
    identb = nc.dram_tensor("identb", [128, 128], BF16, kind="ExternalInput").ap()
    statout = nc.dram_tensor("statout", [128, 4], F32, kind="ExternalOutput").ap()
    es_top = contextlib.ExitStack()
    with es_top:
        blk = es_top.enter_context(nc.Block())
        emit_l1_stats(nc, blk, es_top,
                      dict(xin=xin, wq=wq, bq=bq, identb=identb,
                           statout=statout))
    return nc


def emit_l1_stats(nc, blk, es_top, aps, pfx="", bar=None, final=True):
    FQ = 2048
    xin = aps["xin"]; wq = aps["wq"]; bq = aps["bq"]
    identb = aps["identb"]; statout = aps["statout"]

    es = contextlib.ExitStack()
    with es:
        sb = lambda name, shape, dt: es.enter_context(
            nc.sbuf_tensor(pfx + name, shape, dt))
        wq_sb = sb("wq_sb", [128, KC, FQ], BF16)
        xbuf = sb("xbuf", [128, 2, D], F32)
        junk = sb("junk", [128, 2, D], F32)
        hsb = sb("hsb", [128, 2, D], BF16)
        hsT = sb("hsT", [128, KC, 128], BF16)
        small = sb("small", [128, 32], F32)
        eps_t = sb("eps_t", [128, 1], F32)
        identb_sb = sb("identb_sb", [128, 128], BF16)
        bq_sb = sb("bq_sb", [128, 16], F32)
        qjunk = sb("qjunk", [128, 128], F32)
        acc = sb("acc", [128, 8], F32)
        wstg = sb("wstg", [128, 2, FQ], F32)

        tp_ps = es.enter_context(nc.psum_tensor(pfx + "tp_ps", [128, 2, 4, 256], BF16))
        q_ps = es.enter_context(nc.psum_tensor(pfx + "q_ps", [128, 2, 512], F32))

        ldx = [Sem(es_top, nc, pfx + "ldx0"), Sem(es_top, nc, pfx + "ldx1")]
        wlp = [Sem(es_top, nc, pfx + "wl0"), Sem(es_top, nc, pfx + "wl1")]
        ldc = Sem(es_top, nc, pfx + "ldc")
        stx = Sem(es_top, nc, pfx + "stx")
        pe = Sem(es_top, nc, pfx + "pe")
        sv = Sem(es_top, nc, pfx + "sv")
        ss = Sem(es_top, nc, pfx + "ss")
        if bar:
            def fbar(eng):
                for _s, _c in bar:
                    _s.wait(eng, _c)
            blk.sync(fbar)
            blk.scalar(fbar)
            blk.vector(fbar)
            blk.tensor(fbar)

        def f(sync):
            ldc.inc(sync.dma_start(out=identb_sb[:], in_=identb), 16)
            ldc.inc(sync.dma_start(out=bq_sb[:], in_=bq), 16)
        blk.sync(f)
        ldc_n = ldc.n

        def f(v):
            v.memset(eps_t[:], 1e-5)
            v.memset(acc[:], 0.0)
        blk.vector(f)

        wmarks = [0, 0]
        for kc in range(KC):
            p = kc % 2
            war = wmarks[kc]

            def f(sync, kc=kc, p=p, war=war):
                ss.wait(sync, war)
                wlp[p].inc(sync.dma_start(out=wstg[:, p, :],
                                          in_=wq[kc * 128:(kc + 1) * 128, :]), 16)
            blk.sync(f)
            dm = wlp[p].n

            def f(s, kc=kc, p=p, dm=dm):
                wlp[p].wait(s, dm)
                ss.inc(s.activation(out=wq_sb[:, kc, :], in_=wstg[:, p, :],
                                    func=AF.Identity))
            blk.scalar(f)
            wmarks.append(ss.n)
        wq_done = ss.n

        stat_marks = [0, 0]
        pe_fin = [0] * TT
        s_fin = [0] * TT
        v_fin = [0] * TT
        for k in range(TT):
            slot = k % 2
            SB = 8 * slot

            def f(sync, k=k, slot=slot):
                if k >= 2:
                    sv.wait(sync, v_fin[k - 2])
                ldx[slot].inc(sync.dma_start(out=xbuf[:, slot, :],
                                             in_=xin[k * 128:(k + 1) * 128, :]), 16)
            blk.sync(f)
            ld_k = ldx[slot].n

            def f(v, k=k, slot=slot, ld_k=ld_k, SB=SB):
                ldx[slot].wait(v, ld_k)
                sv.inc(v.tensor_reduce(out=small[:, SB:SB + 1],
                                       in_=xbuf[:, slot, :], axis=AX, op=OP.add))
            blk.vector(f)
            m1 = sv.n

            def f(s, m1=m1, SB=SB):
                sv.wait(s, m1)
                ss.inc(s.activation(out=small[:, SB + 4:SB + 5],
                                    in_=small[:, SB:SB + 1],
                                    func=AF.Identity, scale=1.0 / D))
            blk.scalar(f)
            m2 = ss.n

            def f(v, k=k, slot=slot, m2=m2, SB=SB):
                ss.wait(v, m2)
                sv.inc(v.tensor_scalar(out=junk[:, slot, :], in0=xbuf[:, slot, :],
                                       scalar1=small[:, SB + 4:SB + 5],
                                       scalar2=None, op0=OP.subtract))
            blk.vector(f)
            m3 = sv.n

            def f(s, slot=slot, m3=m3, SB=SB):
                sv.wait(s, m3)
                ss.inc(s.activation(out=hsb[:, slot, :], in_=junk[:, slot, :],
                                    func=AF.Square, accum_out=small[:, SB + 5:SB + 6]))
                ss.wait(s, ss.n)
                ss.inc(s.activation(out=small[:, SB + 7:SB + 8],
                                    in_=small[:, SB + 5:SB + 6],
                                    func=AF.Ln, scale=1.0 / D,
                                    bias=eps_t[:, 0:1]))
                ss.wait(s, ss.n)
                ss.inc(s.activation(out=small[:, SB + 6:SB + 7],
                                    in_=small[:, SB + 7:SB + 8],
                                    func=AF.Exp, scale=-0.5))
            blk.scalar(f)
            m4 = ss.n

            def f(v, k=k, slot=slot, m4=m4, SB=SB,
                  war=(pe_fin[k - 2] if k >= 2 else 0)):
                ss.wait(v, m4)
                pe.wait(v, war)
                sv.inc(v.tensor_scalar(out=hsb[:, slot, :], in0=junk[:, slot, :],
                                       scalar1=small[:, SB + 6:SB + 7],
                                       scalar2=None, op0=OP.mult))
            blk.vector(f)
            v_fin[k] = sv.n

            def f(t, k=k, slot=slot, war=(s_fin[k - 1] if k >= 1 else 0)):
                sv.wait(t, v_fin[k])
                ss.wait(t, war)
                for c in range(KC):
                    pe.inc(t.transpose(tp_ps[:, c // 4, c % 4, 0:128],
                                       hsb[:, slot, c * 128:(c + 1) * 128],
                                       identb_sb[:]))
            blk.tensor(f)
            pe_tr = pe.n

            def f(s, k=k, pe_tr=pe_tr):
                pe.wait(s, pe_tr)
                for c in range(KC):
                    ss.inc(s.activation(out=hsT[:, c, :],
                                        in_=tp_ps[:, c // 4, c % 4, 0:128],
                                        func=AF.Identity))
            blk.scalar(f)
            hsT_m = ss.n
            s_fin[k] = ss.n

            for f0 in range(0, 16, 2):
                qms = []
                for ft in (f0, f0 + 1):
                    bank = ft % 2

                    def f(t, ft=ft, bank=bank, hsT_m=hsT_m, wq_done=wq_done,
                          war=stat_marks[-1 if ft == f0 + 1 else -2]
                          if len(stat_marks) >= 2 else 0):
                        ss.wait(t, max(hsT_m, wq_done))
                        sv.wait(t, war)
                        instr = None
                        for c in range(KC):
                            instr = t.matmul(q_ps[:, bank, 0:128], wq_sb[:, c,
                                             ft * 128:(ft + 1) * 128],
                                             hsT[:, c, :],
                                             start=(c == 0), stop=(c == KC - 1))
                        pe.inc(instr)
                    blk.tensor(f)
                    qms.append(pe.n)

                def f(s, f0=f0, qms=tuple(qms)):
                    for j, (ft, qm) in enumerate(zip((f0, f0 + 1), qms)):
                        bank = ft % 2
                        pe.wait(s, qm)
                        ss.inc(s.activation(out=qjunk[:],
                                            in_=q_ps[:, bank, 0:128],
                                            func=AF.Identity,
                                            bias=bq_sb[:, ft:ft + 1],
                                            accum_out=acc[:, 4 + 2 * j:
                                                          5 + 2 * j]))
                        ss.wait(s, ss.n)
                        ss.inc(s.activation(out=qjunk[:],
                                            in_=q_ps[:, bank, 0:128],
                                            func=AF.Square,
                                            bias=bq_sb[:, ft:ft + 1],
                                            accum_out=acc[:, 5 + 2 * j:
                                                          6 + 2 * j]))
                        ss.wait(s, ss.n)
                blk.scalar(f)
                sacc = ss.n

                def f(v, f0=f0, sacc=sacc):
                    ss.wait(v, sacc)
                    for j, ft in enumerate((f0, f0 + 1)):
                        half = ft % 2
                        sv.inc(v.tensor_tensor(out=acc[:, half:half + 1],
                                               in0=acc[:, half:half + 1],
                                               in1=acc[:, 4 + 2 * j:5 + 2 * j],
                                               op=OP.add))
                        sv.inc(v.tensor_tensor(out=acc[:, 2 + half:3 + half],
                                               in0=acc[:, 2 + half:3 + half],
                                               in1=acc[:, 5 + 2 * j:6 + 2 * j],
                                               op=OP.add))
                    sv.wait(v, sv.n)
                blk.vector(f)
                stat_marks.append(sv.n)
                stat_marks.append(sv.n)

        def f(sync):
            sv.wait(sync, sv.n)
            stx.inc(sync.dma_start(out=statout, in_=acc[:, 0:4]), 16)
        blk.sync(f)

        if final:
            def f(g):
                stx.wait(g, 16)
            blk.gpsimd(f)
    return [(s, s.n) for s in (ldx[0], ldx[1], wlp[0], wlp[1], ldc, stx,
                               pe, sv, ss)]


def run_l1_stats(x_full, Wq, bq_):
    import ml_dtypes
    nc = build_l1_stats()
    bqr = np.ascontiguousarray(np.asarray(bq_, np.float32).reshape(16, 128).T)
    shared = dict(wq=np.ascontiguousarray(Wq, np.float32), bq=bqr,
                  identb=np.eye(128, dtype=ml_dtypes.bfloat16))
    in_maps = []
    for b in range(NCORES):
        m = dict(shared)
        m["xin"] = np.ascontiguousarray(x_full[b], np.float32)
        in_maps.append(m)
    res = run_bass_kernel_spmd(nc, in_maps, core_ids=list(range(NCORES)))
    return np.stack([r["statout"] for r in res.results], axis=0)





def build_l1_pkm():
    FQ = 2048
    E = 1024
    nc = bass.Bass("TRN2", target_bir_lowering=False)
    xin = nc.dram_tensor("xin", [T, D], F32, kind="ExternalInput").ap()
    stats = nc.dram_tensor("stats", [128, NCORES, 4], F32, kind="ExternalInput").ap()
    wq = nc.dram_tensor("wq", [D, FQ], F32, kind="ExternalInput").ap()
    bq = nc.dram_tensor("bq", [128, 16], F32, kind="ExternalInput").ap()
    bng = nc.dram_tensor("bng", [128, 2], F32, kind="ExternalInput").ap()
    bnb = nc.dram_tensor("bnb", [128, 2], F32, kind="ExternalInput").ap()
    sk0 = nc.dram_tensor("sk0", [32, 128], F32, kind="ExternalInput").ap()
    sk1 = nc.dram_tensor("sk1", [32, 128], F32, kind="ExternalInput").ap()
    edT = nc.dram_tensor("edT", [D, E], F32, kind="ExternalInput").ap()
    eu = nc.dram_tensor("eu", [E, D], F32, kind="ExternalInput").ap()
    iota32 = nc.dram_tensor("iota32", [128, 32], F32, kind="ExternalInput").ap()
    identb = nc.dram_tensor("identb", [128, 128], BF16, kind="ExternalInput").ap()
    xout = nc.dram_tensor("xout", [T, D], F32, kind="ExternalOutput").ap()

    es = contextlib.ExitStack()
    with es:
        sb = lambda name, shape, dt: es.enter_context(nc.sbuf_tensor(name, shape, dt))
        wq_sb = sb("wq_sb", [128, KC, FQ], BF16)
        ed_sb = sb("ed_sb", [128, KC, E], BF16)
        eu_sb = sb("eu_sb", [128, KC, D], BF16)
        kb = sb("kb", [128, 2, 33], BF16)
        xbuf = sb("xbuf", [128, 2, D], F32)
        junk = sb("junk", [128, 2, D], F32)
        hsb = sb("hsb", [128, 2, D], BF16)
        hsT = sb("hsT", [128, KC, 128], BF16)
        qhat = sb("qhat", [128, 16, 128], BF16)
        sqh = sb("sqh", [128, 16, 128], BF16)
        s0n = sb("s0n", [128, 16, 32], F32)
        tsv = sb("tsv", [128, 16, 8], F32)
        tif = sb("tif", [128, 16, 8], F32)
        tiu = sb("tiu", [128, 16, 8], U32)
        eg = sb("eg", [128, 2, 32], F32)
        egs = sb("egs", [128, 8, 2, 32], F32)
        nmx = sb("nmx", [128, 16], F32)
        m0t = sb("m0t", [128, 32], F32)
        m0t2 = sb("m0t2", [128, E], F32)
        gsum = sb("gsum", [128, E], F32)
        gf = sb("gf", [128, E], F32)
        ggf = sb("ggf", [128, E], BF16)
        ggfT = sb("ggfT", [128, KC, 128], BF16)
        small = sb("small", [128, 32], F32)
        bstat = sb("bstat", [128, 16], F32)
        coefs = sb("coefs", [128, 2, 16], F32)
        eps_t = sb("eps_t", [128, 1], F32)
        e18_t = sb("e18_t", [128, 1], F32)
        identb_sb = sb("identb_sb", [128, 128], BF16)
        bq_sb = sb("bq_sb", [128, 16], F32)
        bng_sb = sb("bng_sb", [128, 2], F32)
        bnb_sb = sb("bnb_sb", [128, 2], F32)
        iota_sb = sb("iota_sb", [128, 32], F32)
        stats_sb = sb("stats_sb", [128, NCORES, 4], F32)
        keyst = sb("keyst", [32, 128], F32)
        keyn = sb("keyn", [32, 4], F32)
        wstg = sb("wstg", [128, 2, FQ], F32)
        obuf = sb("obuf", [128, 2, D], F32)
        xrbuf = sb("xrbuf", [128, 2, D], F32)

        tp_ps = es.enter_context(nc.psum_tensor("tp_ps", [128, 2, 4, 256], BF16))
        q_ps = es.enter_context(nc.psum_tensor("q_ps", [128, 2, 512], F32))
        sc_ps = es.enter_context(nc.psum_tensor("sc_ps", [128, 2, 512], F32))
        fd_ps = es.enter_context(nc.psum_tensor("fd_ps", [128, 2, 512], F32))
        ou_ps = fd_ps

        ldx = [Sem(es, nc, "ldx0"), Sem(es, nc, "ldx1")]
        stx = [Sem(es, nc, "stx0"), Sem(es, nc, "stx1")]
        wlp = [Sem(es, nc, "wl0"), Sem(es, nc, "wl1")]
        ldc = Sem(es, nc, "ldc")
        pe = Sem(es, nc, "pe")
        sv = Sem(es, nc, "sv")
        ss = Sem(es, nc, "ss")
        blk = es.enter_context(nc.Block())

        def f(sync):
            for o, i in [(identb_sb[:], identb), (bq_sb[:], bq),
                         (bng_sb[:], bng), (bnb_sb[:], bnb),
                         (iota_sb[:], iota32), (stats_sb[:], stats),
                         (keyst[:], sk0)]:
                ldc.inc(sync.dma_start(out=o, in_=i), 16)
        blk.sync(f)
        ldc_n = ldc.n

        def f(v):
            v.memset(eps_t[:], 1e-5)
            v.memset(e18_t[:], 1e-18)
        blk.vector(f)

        def f(v):
            ldc.wait(v, ldc_n)
            sv.inc(v.tensor_reduce(out=bstat[:, 0:4],
                                   in_=stats_sb[:].rearrange("p c s -> p s c"),
                                   axis=AX, op=OP.add))
            sv.wait(v, sv.n)
            CNT = 1.0 / (NCORES * T * 8)
            sv.inc(v.tensor_scalar(out=bstat[:, 4:8], in0=bstat[:, 0:4],
                                   scalar1=CNT, scalar2=None, op0=OP.mult))
            sv.wait(v, sv.n)
            sv.inc(v.tensor_tensor(out=bstat[:, 8:10], in0=bstat[:, 4:6],
                                   in1=bstat[:, 4:6], op=OP.mult))
            sv.wait(v, sv.n)
            sv.inc(v.tensor_tensor(out=bstat[:, 10:12], in0=bstat[:, 6:8],
                                   in1=bstat[:, 8:10], op=OP.subtract))
            sv.wait(v, sv.n)
        blk.vector(f)
        bn1 = sv.n

        def f(s, bn1=bn1):
            sv.wait(s, bn1)
            ss.inc(s.activation(out=bstat[:, 12:14], in_=bstat[:, 10:12],
                                func=AF.Ln, bias=eps_t[:, 0:1]))
            ss.wait(s, ss.n)
            ss.inc(s.activation(out=bstat[:, 0:2], in_=bstat[:, 12:14],
                                func=AF.Exp, scale=-0.5))
        blk.scalar(f)
        bn2 = ss.n

        def f(v, bn2=bn2):
            ss.wait(v, bn2)
            sv.inc(v.tensor_tensor(out=coefs[:, 0, 0:2], in0=bstat[:, 0:2],
                                   in1=bng_sb[:], op=OP.mult))
            sv.wait(v, sv.n)
            sv.inc(v.tensor_tensor(out=bstat[:, 2:4], in0=bstat[:, 4:6],
                                   in1=coefs[:, 0, 0:2], op=OP.mult))
            sv.wait(v, sv.n)
            sv.inc(v.tensor_tensor(out=bstat[:, 2:4], in0=bnb_sb[:],
                                   in1=bstat[:, 2:4], op=OP.subtract))
            sv.wait(v, sv.n)
            for ft in range(16):
                hf = ft % 2
                sv.inc(v.scalar_tensor_tensor(
                    out=coefs[:, 1, ft:ft + 1], in0=bq_sb[:, ft:ft + 1],
                    scalar=1.0, in1=coefs[:, 0, hf:hf + 1],
                    op0=OP.mult, op1=OP.mult))
            sv.wait(v, sv.n)
            for ft in range(16):
                hf = ft % 2
                sv.inc(v.tensor_tensor(out=coefs[:, 1, ft:ft + 1],
                                       in0=coefs[:, 1, ft:ft + 1],
                                       in1=bstat[:, 2 + hf:3 + hf], op=OP.add))
            sv.wait(v, sv.n)
        blk.vector(f)
        bn_done = sv.n

        key_done = []
        for hf in range(2):
            if hf == 1:
                def f(sync, m=key_done[-1]):
                    ss.wait(sync, m)
                    ldc.inc(sync.dma_start(out=keyst[:], in_=sk1), 16)
                blk.sync(f)
            ldck = ldc.n

            def f(s, hf=hf, ldck=ldck):
                ldc.wait(s, ldck)
                ss.inc(s.activation(out=junk[0:32, 0, 0:128],
                                    in_=keyst[0:32, :], func=AF.Square,
                                    accum_out=keyn[0:32, 1:2]))
            blk.scalar(f)
            k1 = ss.n

            def f(s2, k1=k1):
                ss.wait(s2, k1)
                ss.inc(s2.activation(out=keyn[0:32, 2:3], in_=keyn[0:32, 1:2],
                                     func=AF.Ln, bias=e18_t[0:32, 0:1]))
                ss.wait(s2, ss.n)
                ss.inc(s2.activation(out=keyn[0:32, 3:4], in_=keyn[0:32, 2:3],
                                     func=AF.Exp, scale=-0.5))
            blk.scalar(f)
            k2 = ss.n

            def f(v, hf=hf, k2=k2):
                ss.wait(v, k2)
                sv.inc(v.tensor_scalar(out=keyst[0:32, :], in0=keyst[0:32, :],
                                       scalar1=keyn[0:32, 3:4], scalar2=None,
                                       op0=OP.mult))
            blk.vector(f)
            k3 = sv.n

            def f(v, k3=k3):
                sv.wait(v, k3)
                sv.inc(v.tensor_copy(hsb[0:32, 0, 0:128], keyst[0:32, :]))
            blk.vector(f)
            k4 = sv.n

            def f(t, k4=k4):
                sv.wait(t, k4)
                pe.inc(t.transpose(tp_ps[:, 0, 0, 0:32], hsb[0:32, 0, 0:128],
                                   identb_sb[0:32, 0:32]))
            blk.tensor(f)
            k5 = pe.n

            def f(s, hf=hf, k5=k5):
                pe.wait(s, k5)
                ss.inc(s.activation(out=kb[:, hf, 0:32], in_=tp_ps[:, 0, 0, 0:32],
                                    func=AF.Identity))
            blk.scalar(f)
            key_done.append(ss.n)

        def f(v, m=key_done[-1]):
            ss.wait(v, m)
            v.memset(kb[:, 0, 32:33], 1.0)
            sv.inc(v.memset(kb[:, 1, 32:33], 1.0))
        blk.vector(f)
        keys_done_v = sv.n
        keys_done_s = key_done[-1]

        wmarks = [0, 0]
        wsrcs = ([(wq[kc * 128:(kc + 1) * 128, :], wq_sb[:, kc, 0:FQ], FQ)
                  for kc in range(KC)] +
                 [(edT[kc * 128:(kc + 1) * 128, :], ed_sb[:, kc, 0:E], E)
                  for kc in range(KC)] +
                 [(eu[kc * 128:(kc + 1) * 128, :], eu_sb[:, kc, 0:D], D)
                  for kc in range(KC)])
        for i, (src_ap, dst_ap, F) in enumerate(wsrcs):
            p = i % 2
            war = wmarks[i]

            def f(sync, src_ap=src_ap, p=p, war=war, F=F):
                ss.wait(sync, war)
                wlp[p].inc(sync.dma_start(out=wstg[:, p, 0:F], in_=src_ap), 16)
            blk.sync(f)
            dm = wlp[p].n

            def f(s, dst_ap=dst_ap, p=p, dm=dm, F=F):
                wlp[p].wait(s, dm)
                ss.inc(s.activation(out=dst_ap, in_=wstg[:, p, 0:F],
                                    func=AF.Identity))
            blk.scalar(f)
            wmarks.append(ss.n)
        w_done = ss.n

        v_fin = [0] * TT
        pe_fin = [0] * TT
        s_fin = [0] * TT
        v_out = [0] * TT
        prev_q_ev = 0
        prev_sc_v = 0
        prev_fd_ev = 0
        prev_eg_war = 0
        prev_ou_v = 0
        for k in range(TT):
            slot = k % 2
            SB = 8 * slot

            def f(sync, k=k, slot=slot):
                if k >= 2:
                    sv.wait(sync, v_fin[k - 2])
                ldx[slot].inc(sync.dma_start(out=xbuf[:, slot, :],
                                             in_=xin[k * 128:(k + 1) * 128, :]), 16)
            blk.sync(f)
            ld_k = ldx[slot].n

            def f(v, k=k, slot=slot, ld_k=ld_k, SB=SB):
                ldx[slot].wait(v, ld_k)
                sv.inc(v.tensor_reduce(out=small[:, SB:SB + 1],
                                       in_=xbuf[:, slot, :], axis=AX, op=OP.add))
            blk.vector(f)
            m1 = sv.n

            def f(s, m1=m1, SB=SB):
                sv.wait(s, m1)
                ss.inc(s.activation(out=small[:, SB + 4:SB + 5],
                                    in_=small[:, SB:SB + 1],
                                    func=AF.Identity, scale=1.0 / D))
            blk.scalar(f)
            m2 = ss.n

            def f(v, k=k, slot=slot, m2=m2, SB=SB):
                ss.wait(v, m2)
                sv.inc(v.tensor_scalar(out=junk[:, slot, :], in0=xbuf[:, slot, :],
                                       scalar1=small[:, SB + 4:SB + 5],
                                       scalar2=None, op0=OP.subtract))
            blk.vector(f)
            m3 = sv.n

            def f(s, slot=slot, m3=m3, SB=SB):
                sv.wait(s, m3)
                ss.inc(s.activation(out=hsb[:, slot, :], in_=junk[:, slot, :],
                                    func=AF.Square, accum_out=small[:, SB + 5:SB + 6]))
                ss.wait(s, ss.n)
                ss.inc(s.activation(out=small[:, SB + 7:SB + 8],
                                    in_=small[:, SB + 5:SB + 6],
                                    func=AF.Ln, scale=1.0 / D,
                                    bias=eps_t[:, 0:1]))
                ss.wait(s, ss.n)
                ss.inc(s.activation(out=small[:, SB + 6:SB + 7],
                                    in_=small[:, SB + 7:SB + 8],
                                    func=AF.Exp, scale=-0.5))
            blk.scalar(f)
            m4 = ss.n

            def f(v, k=k, slot=slot, m4=m4, SB=SB,
                  war=(pe_fin[k - 2] if k >= 2 else 0), kdv=keys_done_v):
                ss.wait(v, m4)
                sv.wait(v, kdv)
                pe.wait(v, war)
                sv.inc(v.tensor_scalar(out=hsb[:, slot, :], in0=junk[:, slot, :],
                                       scalar1=small[:, SB + 6:SB + 7],
                                       scalar2=None, op0=OP.mult))
            blk.vector(f)
            v_fin[k] = sv.n

            def f(t, k=k, slot=slot, war=(s_fin[k - 1] if k >= 1 else 0)):
                sv.wait(t, v_fin[k])
                ss.wait(t, war)
                for c in range(KC):
                    pe.inc(t.transpose(tp_ps[:, c // 4, c % 4, 0:128],
                                       hsb[:, slot, c * 128:(c + 1) * 128],
                                       identb_sb[:]))
            blk.tensor(f)
            tr_m = pe.n
            pe_fin[k] = pe.n

            def f(s, k=k, tr_m=tr_m):
                pe.wait(s, tr_m)
                for c in range(KC):
                    ss.inc(s.activation(out=hsT[:, c, :],
                                        in_=tp_ps[:, c // 4, c % 4, 0:128],
                                        func=AF.Identity))
            blk.scalar(f)
            hsT_m = ss.n
            s_fin[k] = ss.n

            q_evs = []
            for f0 in range(0, 16, 2):
                qms = []
                for ft in (f0, f0 + 1):
                    bank = ft % 2

                    def f(t, ft=ft, bank=bank, hsT_m=hsT_m, wd=w_done,
                          war=(q_evs[ft - 2] if ft >= 2 else prev_q_ev)):
                        ss.wait(t, max(hsT_m, wd, war))
                        instr = None
                        for c in range(KC):
                            instr = t.matmul(q_ps[:, bank, 0:128],
                                             wq_sb[:, c, ft * 128:(ft + 1) * 128],
                                             hsT[:, c, :],
                                             start=(c == 0), stop=(c == KC - 1))
                        pe.inc(instr)
                    blk.tensor(f)
                    qms.append(pe.n)

                def f(s, f0=f0, qms=tuple(qms), bnd=bn_done):
                    sv.wait(s, bnd)
                    for ft, qm in zip((f0, f0 + 1), qms):
                        bank = ft % 2
                        hf = ft % 2
                        pe.wait(s, qm)
                        ss.inc(s.activation(out=qhat[:, ft, :],
                                            in_=q_ps[:, bank, 0:128],
                                            func=AF.Identity,
                                            scale=coefs[:, 0, hf:hf + 1],
                                            bias=coefs[:, 1, ft:ft + 1]))
                        ss.wait(s, ss.n)
                        ss.inc(s.activation(out=sqh[:, ft, :],
                                            in_=qhat[:, ft, :],
                                            func=AF.Square))
                        q_evs.append(ss.n)
                blk.scalar(f)
            prev_q_ev = ss.n
            qhat_m = ss.n

            sc_vs = []
            for f0 in range(0, 16, 2):
                scms = []
                for ft in (f0, f0 + 1):
                    def f(t, ft=ft, qhat_m=qhat_m, ksd=keys_done_s,
                          war=(sc_vs[ft - 2] if ft >= 2 else prev_sc_v)):
                        ss.wait(t, max(qhat_m, ksd))
                        sv.wait(t, war)
                        hf = ft % 2
                        bk = ft % 2
                        r2 = ((ft // 2) % 2) * 256
                        pe.inc(t.matmul(sc_ps[:, bk, r2:r2 + 33],
                                        qhat[:, ft, :],
                                        kb[:, hf, 0:33], start=True, stop=True))
                        pe.inc(t.matmul(sc_ps[:, bk, r2 + 64:r2 + 65],
                                        sqh[:, ft, :],
                                        kb[:, hf, 32:33], start=True, stop=True))
                    blk.tensor(f)
                    scms.append(pe.n)

                def f(s, f0=f0, scms=tuple(scms)):
                    for ft, scm in zip((f0, f0 + 1), scms):
                        pe.wait(s, scm)
                        bk = ft % 2
                        r2 = ((ft // 2) % 2) * 256
                        c0 = 16 + (ft % 2) * 2
                        ss.inc(s.activation(out=small[:, c0:c0 + 1],
                                            in_=sc_ps[:, bk, r2 + 64:r2 + 65],
                                            func=AF.Ln, bias=e18_t[:, 0:1]))
                        ss.wait(s, ss.n)
                        ss.inc(s.activation(out=small[:, c0 + 1:c0 + 2],
                                            in_=small[:, c0:c0 + 1],
                                            func=AF.Exp, scale=-0.5))
                blk.scalar(f)
                nsq = ss.n

                def f(v, f0=f0, nsq=nsq):
                    ss.wait(v, nsq)
                    for ft in (f0, f0 + 1):
                        bk = ft % 2
                        r2 = ((ft // 2) % 2) * 256
                        c0 = 16 + (ft % 2) * 2
                        sv.inc(v.tensor_scalar(out=s0n[:, ft, :],
                                               in0=sc_ps[:, bk, r2:r2 + 32],
                                               scalar1=small[:, c0 + 1:c0 + 2],
                                               scalar2=None, op0=OP.mult))
                        sv.wait(v, sv.n)
                        sv.inc(v.max(out=tsv[:, ft, :], in_=s0n[:, ft, :]))
                        sv.wait(v, sv.n)
                        sv.inc(v.max_index(out=tiu[:, ft, :],
                                           in_max=tsv[:, ft, :],
                                           in_values=s0n[:, ft, :]))
                        sv.wait(v, sv.n)
                        sv.inc(v.tensor_scalar(out=tif[:, ft, 0:4],
                                               in0=tiu[:, ft, 0:4], scalar1=1.0,
                                               scalar2=None, op0=OP.mult))
                        sv.wait(v, sv.n)
                        sc_vs.append(sv.n)
                blk.vector(f)
            prev_sc_v = sv.n
            top_m = sv.n

            def f(t, k=k, hsT_m=hsT_m, wd=w_done, war=prev_fd_ev,
                  war_v=prev_ou_v):
                ss.wait(t, max(hsT_m, wd, war))
                sv.wait(t, war_v)
                for sp in range(2):
                    instr = None
                    for c in range(KC):
                        instr = t.matmul(fd_ps[:, sp, :], hsT[:, c, :],
                                         ed_sb[:, c, sp * 512:(sp + 1) * 512],
                                         start=(c == 0), stop=(c == KC - 1))
                    pe.inc(instr)
            blk.tensor(f)
            fdm = pe.n

            def f(s, fdm=fdm):
                pe.wait(s, fdm)
                for sp in range(2):
                    ss.inc(s.activation(out=gf[:, sp * 512:(sp + 1) * 512],
                                        in_=fd_ps[:, sp, :], func=AF.Gelu))
            blk.scalar(f)
            prev_fd_ev = ss.n
            gf_m = ss.n

            def f(v, top_m=top_m):
                sv.wait(v, top_m)
                sv.inc(v.tensor_scalar(out=nmx[:].unsqueeze(2),
                                       in0=tsv[:, :, 0:1], scalar1=-1.0,
                                       scalar2=None, op0=OP.mult))
                sv.wait(v, sv.n)
            blk.vector(f)
            nm_all = sv.n

            e_ms = []
            for h0 in range(0, 8, 4):
                def f(s, h0=h0, nm_all=nm_all, war=prev_eg_war):
                    sv.wait(s, nm_all)
                    sv.wait(s, war)
                    for h in range(h0, h0 + 4):
                        for hf in range(2):
                            ft = 2 * h + hf
                            ss.inc(s.activation(out=egs[:, h, hf, :],
                                                in_=s0n[:, ft, :],
                                                func=AF.Exp,
                                                bias=nmx[:, ft:ft + 1]))
                blk.scalar(f)
                e_ms.append(ss.n)

            gacc = 0
            for h in range(8):
                ft0, ft1 = 2 * h, 2 * h + 1
                e_m = e_ms[h // 4]

                def f(v, h=h, ft0=ft0, ft1=ft1, e_m=e_m, gacc=gacc):
                    ss.wait(v, e_m)
                    for hf, ft in ((0, ft0), (1, ft1)):
                        sv.inc(v.tensor_scalar(out=m0t[:, 0:32],
                                               in0=iota_sb[:, :],
                                               scalar1=tif[:, ft, 0:1],
                                               scalar2=None, op0=OP.is_equal))
                        sv.wait(v, sv.n)
                        for kk in range(1, 4):
                            sv.inc(v.scalar_tensor_tensor(
                                out=m0t[:, 0:32], in0=iota_sb[:, :],
                                scalar=tif[:, ft, kk:kk + 1], in1=m0t[:, 0:32],
                                op0=OP.is_equal, op1=OP.add))
                            sv.wait(v, sv.n)
                        sv.inc(v.tensor_tensor(out=eg[:, hf, :],
                                               in0=egs[:, h, hf, :],
                                               in1=m0t[:, 0:32], op=OP.mult))
                        sv.wait(v, sv.n)
                        sv.inc(v.tensor_reduce(out=small[:, 22 + hf:23 + hf],
                                               in_=eg[:, hf, :], axis=AX,
                                               op=OP.add))
                        sv.wait(v, sv.n)
                    sv.inc(v.tensor_tensor(out=small[:, 24:25],
                                           in0=small[:, 22:23],
                                           in1=small[:, 23:24], op=OP.mult))
                    sv.wait(v, sv.n)
                    sv.inc(v.reciprocal(out=small[:, 25:26], in_=small[:, 24:25]))
                    sv.wait(v, sv.n)
                    sv.inc(v.tensor_scalar(out=eg[:, 0, :], in0=eg[:, 0, :],
                                           scalar1=small[:, 25:26], scalar2=None,
                                           op0=OP.mult))
                    sv.wait(v, sv.n)
                    g0v = eg[:, 0, :].unsqueeze(2).to_broadcast([128, 32, 32])
                    g1v = eg[:, 1, :].unsqueeze(1).to_broadcast([128, 32, 32])
                    gv = gsum[:].rearrange("p (a b) -> p a b", b=32)
                    if h == 0:
                        sv.inc(v.tensor_tensor(out=gv, in0=g0v, in1=g1v,
                                               op=OP.mult))
                    else:
                        sv.inc(v.tensor_tensor(out=m0t2[:].rearrange(
                            "p (a b) -> p a b", b=32), in0=g0v, in1=g1v,
                            op=OP.mult))
                        sv.wait(v, sv.n)
                        sv.inc(v.tensor_tensor(out=gsum[:], in0=gsum[:],
                                               in1=m0t2[:], op=OP.add))
                    sv.wait(v, sv.n)
                blk.vector(f)
                gacc = sv.n
            gsum_m = sv.n
            prev_eg_war = sv.n

            def f(v, gsum_m=gsum_m, gf_m=gf_m):
                sv.wait(v, gsum_m)
                ss.wait(v, gf_m)
                sv.inc(v.tensor_tensor(out=ggf[:], in0=gsum[:], in1=gf[:],
                                       op=OP.mult))
            blk.vector(f)
            ggf_m = sv.n

            def f(t, k=k, ggf_m=ggf_m):
                sv.wait(t, ggf_m)
                ss.wait(t, hsT_m)
                for c in range(KC):
                    pe.inc(t.transpose(tp_ps[:, c // 4, c % 4, 0:128],
                                       ggf[:, c * 128:(c + 1) * 128],
                                       identb_sb[:]))
            blk.tensor(f)
            gt_m = pe.n

            def f(s, gt_m=gt_m):
                pe.wait(s, gt_m)
                for c in range(KC):
                    ss.inc(s.activation(out=ggfT[:, c, :],
                                        in_=tp_ps[:, c // 4, c % 4, 0:128],
                                        func=AF.Identity))
            blk.scalar(f)
            ggfT_m = ss.n

            def f(sync, k=k, slot=slot, war=(v_out[k - 2] if k >= 2 else 0)):
                sv.wait(sync, war)
                ldx[slot].inc(sync.dma_start(out=xrbuf[:, slot, :],
                                             in_=xin[k * 128:(k + 1) * 128, :]), 16)
            blk.sync(f)
            ldr = ldx[slot].n

            def f(t, k=k, ggfT_m=ggfT_m, wd=w_done, war=prev_ou_v):
                ss.wait(t, max(ggfT_m, wd))
                sv.wait(t, war)
                for sp in range(2):
                    instr = None
                    for c in range(KC):
                        instr = t.matmul(ou_ps[:, sp, :], ggfT[:, c, :],
                                         eu_sb[:, c, sp * 512:(sp + 1) * 512],
                                         start=(c == 0), stop=(c == KC - 1))
                    pe.inc(instr)
            blk.tensor(f)
            oum = pe.n

            def f(v, k=k, slot=slot, oum=oum, ldr=ldr):
                pe.wait(v, oum)
                ldx[slot].wait(v, ldr)
                if k >= 2:
                    stx[slot].wait(v, 16 * ((k - 2) // 2 + 1))
                for sp in range(2):
                    sv.inc(v.tensor_tensor(
                        out=obuf[:, slot, sp * 512:(sp + 1) * 512],
                        in0=xrbuf[:, slot, sp * 512:(sp + 1) * 512],
                        in1=ou_ps[:, sp, :], op=OP.add))
            blk.vector(f)
            v_out[k] = sv.n
            prev_ou_v = sv.n

            def f(sync, k=k, slot=slot):
                sv.wait(sync, v_out[k])
                stx[slot].inc(sync.dma_start(out=xout[k * 128:(k + 1) * 128, :],
                                             in_=obuf[:, slot, :]), 16)
            blk.sync(f)

        def f(g):
            stx[0].wait(g, stx[0].n)
            stx[1].wait(g, stx[1].n)
        blk.gpsimd(f)
    return nc


def run_l1_pkm(x_full, stats_all, Wq, bq_, bng_, bnb_, sk0_, sk1_, ed_, eu_):
    import ml_dtypes
    nc = build_l1_pkm()
    shared = dict(
        wq=np.ascontiguousarray(Wq, np.float32),
        bq=np.ascontiguousarray(np.asarray(bq_, np.float32).reshape(16, 128).T),
        bng=np.ascontiguousarray(np.asarray(bng_, np.float32).reshape(2, 128).T),
        bnb=np.ascontiguousarray(np.asarray(bnb_, np.float32).reshape(2, 128).T),
        sk0=np.ascontiguousarray(sk0_, np.float32),
        sk1=np.ascontiguousarray(sk1_, np.float32),
        edT=np.ascontiguousarray(np.asarray(ed_, np.float32).T),
        eu=np.ascontiguousarray(eu_, np.float32),
        iota32=np.ascontiguousarray(
            np.broadcast_to(np.arange(32, dtype=np.float32), (128, 32))),
        identb=np.eye(128, dtype=ml_dtypes.bfloat16),
        stats=np.ascontiguousarray(stats_all.transpose(1, 0, 2), np.float32),
    )
    in_maps = []
    for b in range(NCORES):
        m = dict(shared)
        m["xin"] = np.ascontiguousarray(x_full[b], np.float32)
        in_maps.append(m)
    res = run_bass_kernel_spmd(nc, in_maps, core_ids=list(range(NCORES)))
    return np.stack([r["xout"] for r in res.results], axis=0)





def build_mega():
    NI = 2816
    DI = 2048
    FQ = 2048
    nc = bass.Bass("TRN2", target_bir_lowering=False)
    xin = nc.dram_tensor("xin", [T, D], F32, kind="ExternalInput").ap()
    identb = nc.dram_tensor("identb", [128, 128], BF16, kind="ExternalInput").ap()
    a_wbig = nc.dram_tensor("a_wbig", [D, 3200], F32, kind="ExternalInput").ap()
    a_bbig = nc.dram_tensor("a_bbig", [128, 24], F32, kind="ExternalInput").ap()
    a_wout = nc.dram_tensor("a_wout", [D, D], F32, kind="ExternalInput").ap()
    a_bout = nc.dram_tensor("a_bout", [D], F32, kind="ExternalInput").ap()
    a_pscale = nc.dram_tensor("a_pscale", [16], F32, kind="ExternalInput").ap()
    a_sign48 = nc.dram_tensor("a_sign48", [48], F32, kind="ExternalInput").ap()
    a_maskd = nc.dram_tensor("a_maskd", [128, 128], BF16, kind="ExternalInput").ap()
    b_wg = nc.dram_tensor("b_wg", [NI, D], F32, kind="ExternalInput").ap()
    b_wu = nc.dram_tensor("b_wu", [NI, D], F32, kind="ExternalInput").ap()
    b_wd = nc.dram_tensor("b_wd", [D, NI], F32, kind="ExternalInput").ap()
    c_ipwT = nc.dram_tensor("c_ipwT", [D, 2 * DI], F32, kind="ExternalInput").ap()
    c_lng = nc.dram_tensor("c_lng", [128, KC], F32, kind="ExternalInput").ap()
    c_lnb = nc.dram_tensor("c_lnb", [128, KC], F32, kind="ExternalInput").ap()
    c_convw = nc.dram_tensor("c_convw", [128, 16, 4], F32, kind="ExternalInput").ap()
    c_convb = nc.dram_tensor("c_convb", [128, 16], F32, kind="ExternalInput").ap()
    c_xpwT = nc.dram_tensor("c_xpwT", [128, 16, 96], F32, kind="ExternalInput").ap()
    c_dtpwT = nc.dram_tensor("c_dtpwT", [64, DI], F32, kind="ExternalInput").ap()
    c_dtpb = nc.dram_tensor("c_dtpb", [128, 16], F32, kind="ExternalInput").ap()
    c_alog = nc.dram_tensor("c_alog", [128, 16, 16], F32, kind="ExternalInput").ap()
    c_dpar = nc.dram_tensor("c_dpar", [128, 16], F32, kind="ExternalInput").ap()
    c_opwT = nc.dram_tensor("c_opwT", [DI, D], F32, kind="ExternalInput").ap()
    d_wq = nc.dram_tensor("d_wq", [D, FQ], F32, kind="ExternalInput").ap()
    d_bq = nc.dram_tensor("d_bq", [128, 16], F32, kind="ExternalInput").ap()
    xout = nc.dram_tensor("xout", [T, D], F32, kind="ExternalOutput").ap()
    statout = nc.dram_tensor("statout", [128, 4], F32, kind="ExternalOutput").ap()
    x1d = nc.dram_tensor("x1d", [T, D], F32, kind="Internal").ap()
    x2d = nc.dram_tensor("x2d", [T, D], F32, kind="Internal").ap()

    es_top = contextlib.ExitStack()
    with es_top:
        blk = es_top.enter_context(nc.Block())
        b1 = emit_l0_attn(nc, blk, es_top,
                          dict(xin=xin, wbig=a_wbig, bbig=a_bbig, wout=a_wout,
                               bout=a_bout, pscale=a_pscale, sign48=a_sign48,
                               identb=identb, maskd=a_maskd, xout=x1d),
                          pfx="a_", bar=None, final=False)
        b2 = emit_l0_ffn(nc, blk, es_top,
                         dict(xin=x1d, wg=b_wg, wu=b_wu, wd=b_wd,
                              identb=identb, xout=x2d),
                         pfx="b_", bar=b1, final=False)
        b3 = emit_l1_mamba(nc, blk, es_top,
                           dict(xin=x2d, ipwT=c_ipwT, lng=c_lng, lnb=c_lnb,
                                convw=c_convw, convb=c_convb, xpwT=c_xpwT,
                                dtpwT=c_dtpwT, dtpb=c_dtpb, alog=c_alog,
                                dpar=c_dpar, opwT=c_opwT, identb=identb,
                                xout=xout),
                           pfx="c_", bar=b2, final=False)
        emit_l1_stats(nc, blk, es_top,
                      dict(xin=xout, wq=d_wq, bq=d_bq, identb=identb,
                           statout=statout),
                      pfx="d_", bar=b3, final=True)
    return nc


def run_mega(x_full, inputs):
    import ml_dtypes
    nc = build_mega()
    pa = prep_l0_attn_inputs(
        inputs["L0_attn_Wqkv"], inputs["L0_attn_bqkv"], inputs["L0_attn_W1p"],
        inputs["L0_attn_W2p"], inputs["L0_attn_plucker_scale"],
        inputs["L0_attn_Wout"], inputs["L0_attn_bout"])
    r1 = lambda a: np.ascontiguousarray(
        np.asarray(a, np.float32).reshape(-1, 128).T)
    r2 = lambda a, w: np.ascontiguousarray(
        np.asarray(a, np.float32).reshape(-1, 128, w).transpose(1, 0, 2))
    shared = dict(
        identb=pa["identb"],
        a_wbig=pa["wbig"], a_bbig=pa["bbig"], a_wout=pa["wout"],
        a_bout=pa["bout"], a_pscale=pa["pscale"], a_sign48=pa["sign48"],
        a_maskd=pa["maskd"],
        b_wg=np.ascontiguousarray(inputs["L0_ffn_w_gate"], np.float32),
        b_wu=np.ascontiguousarray(inputs["L0_ffn_w_up"], np.float32),
        b_wd=np.ascontiguousarray(inputs["L0_ffn_w_down"], np.float32),
        c_ipwT=np.ascontiguousarray(inputs["L1_attn_in_proj_w"].T, np.float32),
        c_lng=r1(inputs["L1_attn_ln_gamma"]),
        c_lnb=r1(inputs["L1_attn_ln_beta"]),
        c_convw=r2(inputs["L1_attn_conv_w"], 4),
        c_convb=r1(inputs["L1_attn_conv_b"]),
        c_xpwT=r2(inputs["L1_attn_x_proj_w"].T, 96),
        c_dtpwT=np.ascontiguousarray(inputs["L1_attn_dt_proj_w"].T, np.float32),
        c_dtpb=r1(inputs["L1_attn_dt_proj_b"]),
        c_alog=r2(inputs["L1_attn_A_log"], 16),
        c_dpar=r1(inputs["L1_attn_D_param"]),
        c_opwT=np.ascontiguousarray(inputs["L1_attn_out_proj_w"].T, np.float32),
        d_wq=np.ascontiguousarray(inputs["L1_ffn_Wq"], np.float32),
        d_bq=np.ascontiguousarray(
            np.asarray(inputs["L1_ffn_bq"], np.float32).reshape(16, 128).T),
    )
    in_maps = []
    for b in range(NCORES):
        m = dict(shared)
        m["xin"] = np.ascontiguousarray(x_full[b], np.float32)
        in_maps.append(m)
    res = run_bass_kernel_spmd(nc, in_maps, core_ids=list(range(NCORES)))
    x3 = np.stack([r["xout"] for r in res.results], axis=0)
    stats = np.stack([r["statout"] for r in res.results], axis=0)
    return x3, stats


def kernel(**inputs):
    x = np.ascontiguousarray(inputs["L0_attn_x"], np.float32)
    x3, stats = run_mega(x, inputs)
    x4 = run_l1_pkm(x3, stats, inputs["L1_ffn_Wq"], inputs["L1_ffn_bq"],
                    inputs["L1_ffn_bn_gamma"], inputs["L1_ffn_bn_beta"],
                    inputs["L1_ffn_sub_key0"], inputs["L1_ffn_sub_key1"],
                    inputs["L1_ffn_expert_down"], inputs["L1_ffn_expert_up"])
    return np.ascontiguousarray(x4, np.float32)
